# version 19 (final)
# baseline (speedup 1.0000x reference)
import numpy as np
import contextlib
import ml_dtypes
import concourse.bass as bass
import concourse.mybir as mybir
from concourse.bass_utils import run_bass_kernel_spmd

F32 = mybir.dt.float32
BF16 = mybir.dt.bfloat16
AF = mybir.ActivationFunctionType
ALU = mybir.AluOpType
AX = mybir.AxisListType

T = 2048
D = 1024
DI = 2048
DS = 16
DFF0 = 512
FDFF0 = 1024
LN_EPS = 1e-5

ENGINES = ("tensor", "vector", "scalar", "gpsimd", "sync")


class Prog:

    def __init__(self, nc, sems, active=None, eng=None):
        self.nc = nc
        self.sems = sems
        self.active = active
        self.eng = eng
        self.n = {k: 0 for k in sems}
        self.hw = {k: 0 for k in sems}

    def _waits(self, wait):
        for w in wait:
            if len(w) == 3:
                s, c, force = w
            else:
                (s, c), force = w, False
            if c > self.hw[s] or (force and c > 0):
                self.eng.wait_ge(self.sems[s], c)
                self.hw[s] = max(self.hw[s], c)

    def op(self, engine, fn, wait=(), inc=None, amt=1):
        if self.active == engine:
            self._waits(wait)
            instr = fn(self.eng)
            if inc is not None:
                instr.then_inc(self.sems[inc], amt)
        if inc is not None:
            self.n[inc] += amt
            if self.active == engine and amt == 1:
                self.hw[inc] = max(self.hw[inc], self.n[inc])
        return self.n[inc] if inc is not None else None

    def t(self, fn, wait=(), inc="t"):
        return self.op("tensor", fn, wait, inc)

    def v(self, fn, wait=(), inc="v"):
        return self.op("vector", fn, wait, inc)

    def s(self, fn, wait=(), inc="s"):
        return self.op("scalar", fn, wait, inc)

    def g(self, fn, wait=(), inc="g"):
        return self.op("gpsimd", fn, wait, inc)

    DMA_SEM = {"sync": "qs", "scalar": "qa", "gpsimd": "qg"}

    def dma(self, queue, out, in_, wait=()):
        return self.op(queue, lambda e: e.dma_start(out=out, in_=in_), wait,
                       inc=self.DMA_SEM[queue], amt=16)

    def barrier(self):
        return dict(self.n)

    def wait_snapshot(self, snap, engines=ENGINES, sems=("t", "v", "s", "g", "y", "qg", "qs", "qa")):
        w = [(sm, snap[sm]) for sm in sems]
        for en in engines:
            if self.active == en:
                self._waits(w)


def ptrans(p, out_psum, in_ap, ident, wait=(), inc="t"):
    npart = in_ap.shape[0]
    return p.t(lambda e: e.matmul(out_psum, in_ap, ident[0:npart, 0:npart],
                                  start=True, stop=True), wait=wait, inc=inc)


def tiled_mm(p, psum, lhsT_tiles, rhs_tiles, wait=(), inc="t"):
    n = len(lhsT_tiles)
    last = None
    for i, (lt, rt) in enumerate(zip(lhsT_tiles, rhs_tiles)):
        last = p.t(lambda e, lt=lt, rt=rt, i=i: e.matmul(
            psum, lt, rt, start=(i == 0), stop=(i == n - 1)),
            wait=(wait if i == 0 else ()), inc=(inc if i == n - 1 else None))
    return last


def ln_stats(p, m, TC, tiles_ap, ntiles, onescol, sq, bf16_ones=False):
    r0, r1, t2 = m["row0"], m["row1"], m["row2"]
    c_sq = 0
    for i in range(ntiles):
        c_sq = p.s(lambda e, i=i: e.activation(sq[:, i * TC:(i + 1) * TC], tiles_ap[i], AF.Square))
    c_mm1 = tiled_mm(p, m["ps_r0"][0:1, 0:TC], [onescol] * ntiles, tiles_ap)
    c_mm2 = tiled_mm(p, m["ps_r1"][0:1, 0:TC], [m["onescol"][:, 0:1]] * ntiles,
                     [sq[:, i * TC:(i + 1) * TC] for i in range(ntiles)],
                     wait=[("s", c_sq)])
    nfeat = ntiles * 128
    p.v(lambda e: e.tensor_scalar(r0[0:1, 0:TC], m["ps_r0"][0:1, 0:TC], -1.0 / nfeat, None, ALU.mult),
        wait=[("t", c_mm2)])
    p.v(lambda e: e.tensor_scalar(r1[0:1, 0:TC], m["ps_r1"][0:1, 0:TC], 1.0 / nfeat, None, ALU.mult),
        wait=[("t", c_mm2)])
    p.v(lambda e: e.tensor_tensor(t2[0:1, 0:TC], r0[0:1, 0:TC], r0[0:1, 0:TC], ALU.mult))
    p.v(lambda e: e.tensor_scalar(t2[0:1, 0:TC], t2[0:1, 0:TC], -1.0, LN_EPS, ALU.mult, ALU.add))
    c_var = p.v(lambda e: e.tensor_tensor(t2[0:1, 0:TC], r1[0:1, 0:TC], t2[0:1, 0:TC], ALU.add))
    p.s(lambda e: e.activation(t2[0:1, 0:TC], t2[0:1, 0:TC], AF.Ln), wait=[("v", c_var)])
    c_rstd = p.s(lambda e: e.activation(t2[0:1, 0:TC], t2[0:1, 0:TC], AF.Exp, scale=-0.5))
    c_rep0 = p.t(lambda e: e.matmul(m["ps_rep"][:, 0:TC], m["onesrow"][0:1, :], r0[0:1, 0:TC],
                                    start=True, stop=True), wait=[("v", c_var)])
    c_rep1 = p.t(lambda e: e.matmul(m["ps_rep2"][:, 0:TC], m["onesrow"][0:1, :], t2[0:1, 0:TC],
                                    start=True, stop=True), wait=[("s", c_rstd)])
    p.v(lambda e: e.tensor_copy(m["mu_rep"][:, 0:TC], m["ps_rep"][:, 0:TC]), wait=[("t", c_rep1)])
    c_reps = p.v(lambda e: e.tensor_copy(m["mu_rep"][:, TC:2 * TC], m["ps_rep2"][:, 0:TC]),
                 wait=[("t", c_rep1)])
    return c_reps



S1_NCH = T // 256
S1_UPTO = "all"


def s1_body(p, m):
    TC = m.get("TC1", 256)
    NCH = T // TC
    nd = D // 128
    ndi = DI // 128
    snap = p.barrier()
    p.wait_snapshot(snap)
    prev_braw = None
    H = dict(x_done={}, xn_done={}, sq_read={}, xln_read={}, zpg={}, xcpg={}, xnpg={})
    for c in range(NCH):
        par = c % 2
        x_t = m["x_ch2"][par]
        xn_t = m["xn_ch2"][par]
        w = H["x_done"].get(c - 2, None)
        c_x = p.dma("sync", x_t[:, 0:nd * TC].rearrange("p (i tc) -> p i tc", i=nd),
                    m["x"][:, c * TC:(c + 1) * TC].rearrange("(i p) tc -> p i tc", p=128),
                    wait=(w if w else ()))
        x_tiles = [x_t[:, i * TC:(i + 1) * TC] for i in range(nd)]
        p.op("scalar", lambda e: e.nop(), wait=[("qs", c_x)] + H["sq_read"].get(c - 1, []))
        p.op("tensor", lambda e: e.nop(), wait=[("qs", c_x)])
        c_reps = ln_stats(p, m, TC, x_tiles, nd, m["onescol"][:, 0:1], m["sq"])
        c_xn_v = 0
        wxn = H["xn_done"].get(c - 2, [])
        for i in range(nd):
            p.v(lambda e, i=i: e.tensor_tensor(m["sq"][:, i * TC:(i + 1) * TC],
                                               x_t[:, i * TC:(i + 1) * TC], m["mu_rep"][:, 0:TC], ALU.add),
                wait=[("v", c_reps)] + (wxn if i == 0 else []))
            c_xn_v = p.v(lambda e, i=i: e.tensor_tensor(xn_t[:, i * TC:(i + 1) * TC],
                                                        m["sq"][:, i * TC:(i + 1) * TC],
                                                        m["mu_rep"][:, TC:2 * TC], ALU.mult))
        c_xnpg = p.dma("scalar", m["xn_s"][:, c * TC:(c + 1) * TC].rearrange("(i p) tc -> p i tc", p=128),
                       xn_t[:, 0:nd * TC].rearrange("p (i tc) -> p i tc", i=nd), wait=[("v", c_xn_v)])

        xn_tiles = [xn_t[:, i * TC:(i + 1) * TC] for i in range(nd)]
        p.op("scalar", lambda e: e.nop(), wait=[("v", c_xn_v)])
        p.op("tensor", lambda e: e.nop(), wait=[("v", c_xn_v)])
        c_sq2_start = p.n["s"]
        c_reps2 = ln_stats(p, m, TC, xn_tiles, nd, m["onescol_b"][:, 0:1], m["sq"])
        c_m1_t = p.n["t"]
        xln_t = m["xln_ch"]
        c_xln = 0
        wxl = H["xln_read"].get(c - 1, [])
        for i in range(nd):
            p.v(lambda e, i=i: e.tensor_tensor(m["sq"][:, i * TC:(i + 1) * TC],
                                               xn_t[:, i * TC:(i + 1) * TC], m["mu_rep"][:, 0:TC], ALU.add),
                wait=[("v", c_reps2)])
            c_vm = p.v(lambda e, i=i: e.tensor_tensor(m["sq"][:, i * TC:(i + 1) * TC],
                                                      m["sq"][:, i * TC:(i + 1) * TC],
                                                      m["mu_rep"][:, TC:2 * TC], ALU.mult))
            c_xln_v = p.v(lambda e, i=i: e.tensor_scalar(xln_t[:, i * TC:(i + 1) * TC],
                                                         m["sq"][:, i * TC:(i + 1) * TC],
                                                         m["lng"][:, i:i + 1], m["lnb"][:, i:i + 1],
                                                         ALU.mult, ALU.add),
                          wait=(wxl if i == 0 else ()))
            c_xln = c_xln_v
        H["sq_read"][c] = [("v", c_xln)]
        H["xn_done"][c] = [("t", c_m1_t), ("s", p.n["s"]), ("qa", c_xnpg)]

        braw = m["braw_ch"][c % 2]
        z_t = m["z_ch"]
        if prev_braw is None:
            c_carry = p.v(lambda e: e.memset(
                braw[:, 0:ndi * (TC + 3)].rearrange("p (i tc) -> p i tc", i=ndi)[:, :, 0:3], 0.0))
        else:
            pb_ = prev_braw
            c_carry = 0
            for i in range(ndi):
                c_carry = p.v(lambda e, i=i: e.tensor_copy(
                    braw[:, i * (TC + 3):i * (TC + 3) + 3],
                    pb_[:, i * (TC + 3) + TC:i * (TC + 3) + TC + 3]))
        ps_free = m["s1_ps_free"]
        c_zwait = H["zpg"].get(c - 1, 0)
        last_mm = 0
        for po in range(32):
            ps = m["ps_mm"][po % 2]
            c_mm = tiled_mm(p, ps[:, 0:TC],
                            [m["w_in"][:, pi * 4096 + po * 128:pi * 4096 + (po + 1) * 128] for pi in range(nd)],
                            [xln_t[:, pi * TC:(pi + 1) * TC] for pi in range(nd)],
                            wait=[("v", c_xln), ("v", ps_free[po % 2])])
            last_mm = c_mm
            if po < 16:
                dst = braw[:, po * (TC + 3) + 3:po * (TC + 3) + 3 + TC]
            else:
                dst = z_t[:, (po - 16) * TC:(po - 15) * TC]
            ps_free[po % 2] = p.v(lambda e, dst=dst, ps=ps: e.tensor_copy(dst, ps[:, 0:TC]),
                                  wait=[("t", c_mm)] + ([("qa", c_zwait)] if po == 16 else []))
        H["xln_read"][c] = [("t", last_mm)]
        c_z = p.n["v"]
        H["zpg"][c] = p.dma("scalar", m["z_s"][:, c * TC:(c + 1) * TC].rearrange("(i p) tc -> p i tc", p=128),
                            z_t[:, 0:ndi * TC].rearrange("p (i tc) -> p i tc", i=ndi), wait=[("v", c_z)])

        xc_t = m["xc_ch"]
        c_silu = {}
        c_xcw = H["xcpg"].get(c - 1, 0)
        for i in range(ndi):
            base = i * (TC + 3)
            cv = m["cv"][:, (i % 2) * TC:(i % 2 + 1) * TC]
            w2 = [("v", c_z), ("v", c_carry)]
            if i >= 2:
                w2.append(("s", c_silu[i - 2]))
            p.v(lambda e, i=i, base=base, cv=cv: e.tensor_scalar(
                cv, braw[:, base + 0:base + 0 + TC], m["convw"][:, 4 * i + 0:4 * i + 1], None, ALU.mult),
                wait=w2)
            for j in range(1, 4):
                p.v(lambda e, i=i, base=base, j=j, cv=cv: e.scalar_tensor_tensor(
                    cv, braw[:, base + j:base + j + TC], m["convw"][:, 4 * i + j:4 * i + j + 1],
                    cv, ALU.mult, ALU.add))
            c_last = p.n["v"]
            c_silu[i] = p.s(lambda e, i=i, cv=cv: e.activation(
                xc_t[:, i * TC:(i + 1) * TC], cv, AF.Silu,
                bias=m["convb"][:, i:i + 1]), wait=[("v", c_last)] + ([("qa", c_xcw)] if i == 0 else []))
        c_conv = c_silu[ndi - 1]
        H["xcpg"][c] = p.dma("scalar", m["xc_s"][:, c * TC:(c + 1) * TC].rearrange("(i p) tc -> p i tc", p=128),
                             xc_t[:, 0:ndi * TC].rearrange("p (i tc) -> p i tc", i=ndi),
                             wait=[("s", c_conv)])
        prev_braw = braw
        H["x_done"][c] = [("v", c_xn_v), ("s", c_sq2_start), ("t", c_m1_t)]
    p.op("sync", lambda e: e.nop(), wait=[("qa", p.n["qa"]), ("qs", p.n["qs"]), ("qg", p.n["qg"])])


def s1_preamble(p, m):
    nd = D // 128
    for pi in range(nd):
        p.dma("gpsimd", m["w_in"][:, pi * 4096:(pi + 1) * 4096],
              m["w_in_d"][pi * 128:(pi + 1) * 128, :])
    p.dma("sync", m["lng"][:], m["lng_d"])
    p.dma("sync", m["lnb"][:], m["lnb_d"])
    p.dma("sync", m["convw"][:], m["convw_d"])
    p.dma("sync", m["convb"][:], m["convb_d"])
    p.v(lambda e: e.memset(m["onescol"][:], 1.0))
    p.v(lambda e: e.memset(m["onescol_b"][:], 1.0))
    p.v(lambda e: e.memset(m["onesrow"][:], 1.0))
    m.setdefault("s1_ps_free", [0, 0])
    m["s1_ps_free"][0] = 0
    m["s1_ps_free"][1] = 0


def _new_nc():
    nc = bass.Bass("TRN2", target_bir_lowering=False)
    nc.detect_race_conditions = False
    return nc


def build_s1():
    nc = _new_nc()
    TC = 256
    nd = D // 128
    ndi = DI // 128
    m = {}
    m["x"] = nc.dram_tensor("x", [D, T], F32, kind="ExternalInput").ap()
    m["w_in_d"] = nc.dram_tensor("w_in", [D, 4096], F32, kind="ExternalInput").ap()
    m["lng_d"] = nc.dram_tensor("lng", [128, nd], F32, kind="ExternalInput").ap()
    m["lnb_d"] = nc.dram_tensor("lnb", [128, nd], F32, kind="ExternalInput").ap()
    m["convw_d"] = nc.dram_tensor("convw", [128, 4 * ndi], F32, kind="ExternalInput").ap()
    m["convb_d"] = nc.dram_tensor("convb", [128, ndi], F32, kind="ExternalInput").ap()
    m["xn_s"] = nc.dram_tensor("xn_s", [D, T], BF16, kind="ExternalOutput").ap()
    m["xc_s"] = nc.dram_tensor("xc_s", [DI, T], BF16, kind="ExternalOutput").ap()
    m["z_s"] = nc.dram_tensor("z_s", [DI, T], BF16, kind="ExternalOutput").ap()

    es = contextlib.ExitStack()
    with es:
        sb = lambda name, shape, dt: es.enter_context(nc.sbuf_tensor(name, shape, dt))
        m["w_in"] = sb("w_in_sb", [128, nd * 4096], BF16)
        m["x_ch2"] = [sb(f"x_ch{i}", [128, nd * TC], F32) for i in range(2)]
        m["xn_ch2"] = [sb(f"xn_ch{i}", [128, nd * TC], BF16) for i in range(2)]
        m["xln_ch"] = sb("xln_ch", [128, nd * TC], BF16)
        m["sq"] = sb("sq", [128, nd * TC], F32)
        m["braw_ch"] = [sb(f"braw{i}", [128, ndi * (TC + 3)], BF16) for i in range(2)]
        m["z_ch"] = sb("z_ch", [128, ndi * TC], BF16)
        m["xc_ch"] = sb("xc_ch", [128, ndi * TC], BF16)
        m["cv"] = sb("cv", [128, 2 * TC], F32)
        m["row0"] = sb("row0", [1, TC], F32)
        m["row1"] = sb("row1", [1, TC], F32)
        m["row2"] = sb("row2", [1, TC], F32)
        m["mu_rep"] = sb("mu_rep", [128, 2 * TC], F32)
        m["onescol"] = sb("onescol", [128, 1], F32)
        m["onescol_b"] = sb("onescol_b", [128, 1], BF16)
        m["onesrow"] = sb("onesrow", [1, 128], F32)
        m["lng"] = sb("lng_sb", [128, nd], F32)
        m["lnb"] = sb("lnb_sb", [128, nd], F32)
        m["convw"] = sb("convw_sb", [128, 4 * ndi], F32)
        m["convb"] = sb("convb_sb", [128, ndi], F32)
        m["ps_r0"] = es.enter_context(nc.psum_tensor("ps_r0", [128, 512], F32))
        m["ps_r1"] = es.enter_context(nc.psum_tensor("ps_r1", [128, 512], F32))
        m["ps_rep"] = es.enter_context(nc.psum_tensor("ps_rep", [128, 512], F32))
        m["ps_rep2"] = es.enter_context(nc.psum_tensor("ps_rep2", [128, 512], F32))
        m["ps_mm"] = [es.enter_context(nc.psum_tensor(f"ps_mm{i}", [128, 512], F32)) for i in range(2)]

        sem_names = ["t", "v", "s", "g", "qg", "qs", "qa", "y"]
        sems = {k: es.enter_context(nc.semaphore(k)) for k in sem_names}

        def describe(p):
            s1_preamble(p, m)
            s1_body(p, m)

        with nc.Block() as block:
            for en in ENGINES:
                def mk(en):
                    def section(eng):
                        describe(Prog(nc, sems, active=en, eng=eng))
                    return section
                getattr(block, en)(mk(en))
    return nc



def _col_tiles(v, p=128):
    return np.ascontiguousarray(np.asarray(v, np.float32).reshape(-1, p).T)


def s1_inmap(inputs, b):
    x = np.asarray(inputs["L0_attn_x"], dtype=np.float32)
    return dict(
        x=np.ascontiguousarray(x[b].T),
        w_in=np.ascontiguousarray(np.asarray(inputs["L0_attn_w_in"], np.float32)),
        lng=_col_tiles(inputs["L0_attn_ln_g"]),
        lnb=_col_tiles(inputs["L0_attn_ln_b"]),
        convw=np.ascontiguousarray(
            np.asarray(inputs["L0_attn_conv_w"], np.float32).reshape(DI, 4)
            .reshape(-1, 128, 4).transpose(1, 0, 2).reshape(128, -1)),
        convb=_col_tiles(inputs["L0_attn_conv_b"]),
    )


def kernel(**inputs):
    n_cores = 8
    nc = build_s1()
    in_maps = [s1_inmap(inputs, b) for b in range(n_cores)]
    res = run_bass_kernel_spmd(nc, in_maps, core_ids=list(range(n_cores)))
    return res


def s2_body(p, m):
    TC = 256
    NCH = T // TC
    ndi = DI // 128
    snap = p.barrier()
    p.wait_snapshot(snap)
    ch_v_end = {-2: 0, -1: 0}
    ch_t_end = {-2: 0, -1: 0}
    scan_end = {-1: 0}
    xc_load = {}
    yg_pg = {}
    for c in range(NCH):
        par = c % 2
        xc_t = m["xc_ch2"][par]
        dt_t = m["dt_ch2"][par]
        xc_load[c] = p.dma("sync", xc_t[:, 0:ndi * TC].rearrange("p (i tc) -> p i tc", i=ndi),
                           m["xc_s"][:, c * TC:(c + 1) * TC].rearrange("(i p) tc -> p i tc", p=128),
                           wait=[("v", ch_v_end[c - 2]), ("t", ch_t_end[c - 2])])
        ps_free = m["s2_ps_free"]
        for g in range(ndi):
            ps = m["pb"][g % 2]
            c_mm = tiled_mm(p, ps[:, 0:TC],
                            [m["w_dt"][:, pi * DI + g * 128:pi * DI + (g + 1) * 128] for pi in range(ndi)],
                            [xc_t[:, pi * TC:(pi + 1) * TC] for pi in range(ndi)],
                            wait=[("v", ps_free[g % 2][0]), ("s", ps_free[g % 2][1]),
                                  ("qs", xc_load[c])])
            bcol = m["bdt"][:, g:g + 1]
            pre = m["scr"][:, 0:TC]
            ab = m["scr"][:, TC:2 * TC]
            rr = m["scr"][:, 2 * TC:3 * TC]
            c_pre = p.v(lambda e, ps=ps, pre=pre, bcol=bcol: e.tensor_scalar(
                pre, ps[:, 0:TC], bcol, None, ALU.add),
                wait=[("t", c_mm), ("s", m["sp_done"].get((c, g - 1), 0))])
            p.s(lambda e, pre=pre, ab=ab: e.activation(ab, pre, AF.Abs), wait=[("v", c_pre)])
            p.s(lambda e, ab=ab: e.activation(ab, ab, AF.Exp, scale=-1.0))
            c_l = p.s(lambda e, ab=ab: e.activation(ab, ab, AF.Ln, bias=1.0))
            c_r = p.s(lambda e, pre=pre, rr=rr: e.activation(rr, pre, AF.Relu))
            m["sp_done"][(c, g)] = c_r
            c_dtv = p.v(lambda e, g=g, ab=ab, rr=rr, dt_t=dt_t: e.tensor_tensor(
                dt_t[:, g * TC:(g + 1) * TC], rr, ab, ALU.add), wait=[("s", c_r)])
            ps_free[g % 2] = (c_dtv, c_r)
        c_dt = p.n["v"]

        c_xd = tiled_mm(p, m["pb"][2][0:32, 0:TC],
                        [m["w_xp"][:, pi * 32:(pi + 1) * 32] for pi in range(ndi)],
                        [xc_t[:, pi * TC:(pi + 1) * TC] for pi in range(ndi)],
                        wait=[("v", m["xd_free"])])
        m["xd_free"] = p.v(lambda e: e.tensor_copy(m["xdbl"][0:32, 0:TC], m["pb"][2][0:32, 0:TC]),
                           wait=[("t", c_xd)])
        c_xdv = m["xd_free"]
        rep_free = m["s2_rep_free"]
        for n in range(32):
            ps = m["pb"][3 + n % 2]
            dst = (m["B_rep"] if n < 16 else m["C_rep"])[:, (n % 16) * TC:(n % 16 + 1) * TC]
            c_mm = p.t(lambda e, ps=ps, n=n: e.matmul(ps[:, 0:TC], m["sel32"][:, n * 128:(n + 1) * 128],
                                                      m["xdbl"][0:32, 0:TC], start=True, stop=True),
                       wait=[("v", c_xdv), ("s", rep_free[n % 2]), ("v", scan_end[c - 1])])
            rep_free[n % 2] = p.s(lambda e, dst=dst, ps=ps: e.activation(dst, ps[:, 0:TC], AF.Identity),
                                  wait=[("t", c_mm)])
        c_rep = p.n["s"]

        yg_t = m["yg_ch"]
        for g in range(ndi):
            dt_g = dt_t[:, g * TC:(g + 1) * TC]
            xc_g = xc_t[:, g * TC:(g + 1) * TC]
            a_g = m["a_ch2"][0]
            c_a_n = []
            for n in range(DS):
                c_a_n.append(p.s(lambda e, n=n, g=g: e.activation(
                    a_g[:, n * TC:(n + 1) * TC], dt_g, AF.Exp,
                    scale=m["Aneg"][:, g * DS + n:g * DS + n + 1]),
                    wait=([("v", c_dt), ("v", m["a_free"][0]), ("s", c_rep)] if n == 0 else ())))
            c_u = p.v(lambda e: e.tensor_tensor(m["u_g"][:, 0:TC], dt_g, xc_g, ALU.mult),
                      wait=[("s", c_rep)])
            y_g = m["y_g"][:, 0:TC]
            tmp = m["y_g"][:, TC:2 * TC]
            b_g = m["y_g"][:, 2 * TC:3 * TC]
            hs = m["h_state"]
            for n in range(DS):
                p.v(lambda e, n=n: e.tensor_tensor(b_g, m["u_g"][:, 0:TC],
                                                   m["B_rep"][:, n * TC:(n + 1) * TC], ALU.mult))
                p.v(lambda e, n=n, g=g: e.tensor_tensor_scan(
                    m["h_g"][:, n * TC:(n + 1) * TC], a_g[:, n * TC:(n + 1) * TC], b_g,
                    hs[:, g * DS + n:g * DS + n + 1], ALU.mult, ALU.add),
                    wait=[("s", c_a_n[n])])
                if n == 0:
                    p.v(lambda e, n=n: e.tensor_tensor(y_g, m["h_g"][:, n * TC:(n + 1) * TC],
                                                       m["C_rep"][:, n * TC:(n + 1) * TC], ALU.mult))
                else:
                    p.v(lambda e, n=n: e.tensor_tensor(tmp, m["h_g"][:, n * TC:(n + 1) * TC],
                                                       m["C_rep"][:, n * TC:(n + 1) * TC], ALU.mult))
                    p.v(lambda e: e.tensor_tensor(y_g, y_g, tmp, ALU.add))
            m["a_free"][g % 2] = p.n["v"]
            hv = m["h_g"][:, 0:DS * TC].rearrange("p (n tc) -> p n tc", tc=TC)
            p.v(lambda e, g=g: e.tensor_copy(
                hs[:, g * DS:(g + 1) * DS].rearrange("p (n o) -> p n o", o=1), hv[:, :, TC - 1:TC]))
            slot = yg_t[:, (g % 2) * TC:(g % 2 + 1) * TC]
            c_y2 = p.v(lambda e, g=g, slot=slot: e.scalar_tensor_tensor(slot,
                                                             xc_g, m["Dcol"][:, g:g + 1],
                                                             y_g, ALU.mult, ALU.add),
                       wait=[("qa", yg_pg.get(g % 2, 0))])
            yg_pg[g % 2] = p.dma("scalar",
                             m["yg_s"][g * 128:(g + 1) * 128, c * TC:(c + 1) * TC],
                             slot, wait=[("v", c_y2)])
        scan_end[c] = p.n["v"]
        ch_v_end[c] = p.n["v"]
        ch_t_end[c] = p.n["t"]
    p.op("sync", lambda e: e.nop(), wait=[("qa", p.n["qa"]), ("qs", p.n["qs"]), ("qg", p.n["qg"])])


def s2_preamble(p, m):
    ndi = DI // 128
    for pi in range(ndi):
        p.dma("gpsimd", m["w_dt"][:, pi * DI:(pi + 1) * DI], m["w_dt_d"][pi * 128:(pi + 1) * 128, :])
    for pi in range(ndi):
        p.dma("gpsimd", m["w_xp"][:, pi * 32:(pi + 1) * 32], m["w_xp_d"][pi * 128:(pi + 1) * 128, :])
    p.dma("sync", m["bdt"][:], m["bdt_d"])
    p.dma("sync", m["Aneg"][:], m["Alog_d"])
    p.dma("sync", m["Dcol"][:], m["D_d"])
    c_sel = p.dma("sync", m["sel32"][0:32, :], m["sel32_d"])
    c_e = p.s(lambda e: e.activation(m["Aneg"][:], m["Aneg"][:], AF.Exp), wait=[("qs", p.n["qs"])])
    p.v(lambda e: e.tensor_scalar(m["Aneg"][:], m["Aneg"][:], -1.0, None, ALU.mult), wait=[("s", c_e)])
    p.v(lambda e: e.memset(m["h_state"][:], 0.0))
    m["sp_done"] = {}
    m["xd_free"] = 0
    m["a_free"] = [0, 0]
    m["s2_ps_free"] = [(0, 0), (0, 0)]
    m["s2_rep_free"] = [0, 0]


def build_s2():
    nc = _new_nc()
    TC = 256
    ndi = DI // 128
    m = {}
    m["xc_s"] = nc.dram_tensor("xc_s", [DI, T], BF16, kind="ExternalInput").ap()
    m["w_dt_d"] = nc.dram_tensor("w_dt", [DI, DI], F32, kind="ExternalInput").ap()
    m["w_xp_d"] = nc.dram_tensor("w_xp", [DI, 32], F32, kind="ExternalInput").ap()
    m["bdt_d"] = nc.dram_tensor("bdt", [128, ndi], F32, kind="ExternalInput").ap()
    m["Alog_d"] = nc.dram_tensor("Alog", [128, ndi * DS], F32, kind="ExternalInput").ap()
    m["D_d"] = nc.dram_tensor("Dv", [128, ndi], F32, kind="ExternalInput").ap()
    m["sel32_d"] = nc.dram_tensor("sel32", [32, 32 * 128], F32, kind="ExternalInput").ap()
    m["yg_s"] = nc.dram_tensor("yg_s", [DI, T], BF16, kind="ExternalOutput").ap()

    es = contextlib.ExitStack()
    with es:
        sb = lambda name, shape, dt: es.enter_context(nc.sbuf_tensor(name, shape, dt))
        m["w_dt"] = sb("w_dt_sb", [128, ndi * DI], BF16)
        m["w_xp"] = sb("w_xp_sb", [128, ndi * 32], BF16)
        m["bdt"] = sb("bdt_sb", [128, ndi], F32)
        m["Aneg"] = sb("Aneg", [128, ndi * DS], F32)
        m["Dcol"] = sb("Dcol", [128, ndi], F32)
        m["sel32"] = sb("sel32_sb", [32, 32 * 128], F32)
        m["xc_ch2"] = [sb(f"xc_ch{i}", [128, ndi * TC], BF16) for i in range(2)]
        m["dt_ch2"] = [sb(f"dt_ch{i}", [128, ndi * TC], F32) for i in range(2)]
        a1 = sb("a_ch", [128, DS * TC], F32)
        m["a_ch2"] = [a1, a1]
        m["h_g"] = sb("h_g", [128, DS * TC], F32)
        m["u_g"] = sb("u_g", [128, TC], F32)
        m["y_g"] = sb("y_g", [128, 3 * TC], F32)

        m["scr"] = sb("scr", [128, 3 * TC], F32)
        m["B_rep"] = sb("B_rep", [128, DS * TC], F32)
        m["C_rep"] = sb("C_rep", [128, DS * TC], BF16)
        m["xdbl"] = sb("xdbl", [32, TC], F32)
        m["h_state"] = sb("h_state", [128, ndi * DS], F32)
        m["yg_ch"] = sb("yg_ch", [128, 2 * TC], BF16)
        m["pb"] = [es.enter_context(nc.psum_tensor(f"pb{i}", [128, 512], F32)) for i in range(5)]

        sem_names = ["t", "v", "s", "g", "qg", "qs", "qa", "y"]
        sems = {k: es.enter_context(nc.semaphore(k)) for k in sem_names}

        def describe(p):
            s2_preamble(p, m)
            s2_body(p, m)

        with nc.Block() as block:
            for en in ENGINES:
                def mk(en):
                    def section(eng):
                        describe(Prog(nc, sems, active=en, eng=eng))
                    return section
                getattr(block, en)(mk(en))
    return nc


def s2_inmap(inputs, s1_out):
    ndi = DI // 128
    return dict(
        xc_s=s1_out["xc_s"],
        w_dt=np.ascontiguousarray(np.asarray(inputs["L0_attn_w_dt"], np.float32)),
        w_xp=np.ascontiguousarray(np.asarray(inputs["L0_attn_w_xproj"], np.float32)),
        bdt=_col_tiles(inputs["L0_attn_b_dt"]),
        Alog=np.ascontiguousarray(
            np.asarray(inputs["L0_attn_A_log"], np.float32).reshape(ndi, 128, DS)
            .transpose(1, 0, 2).reshape(128, ndi * DS)),
        Dv=_col_tiles(inputs["L0_attn_D"]),
        sel32=_sel32(),
    )


def _sel32():
    sel = np.zeros((32, 32 * 128), np.float32)
    for n in range(32):
        sel[n, n * 128:(n + 1) * 128] = 1.0
    return sel

S3_DEBUG = False


def s3_body(p, m):
    nd = D // 128
    ndi = DI // 128
    NE = 16
    TCA = 256
    TCE = 512
    NTT = T // 128

    snap = p.barrier()
    for c in range(T // TCA):
        p.wait_snapshot(snap)
        x_t, xn_t, yg_t, x1_t = m["x_ch"], m["xn_ch"], m["yg_ch"], m["x1_ch"]
        c_x = p.dma("sync", x_t[:, 0:nd * TCA].rearrange("p (i tc) -> p i tc", i=nd),
                    m["x"][:, c * TCA:(c + 1) * TCA].rearrange("(i p) tc -> p i tc", p=128))
        c_xn = p.dma("sync", xn_t[:, 0:nd * TCA].rearrange("p (i tc) -> p i tc", i=nd),
                     m["xn_s"][:, c * TCA:(c + 1) * TCA].rearrange("(i p) tc -> p i tc", p=128))
        c_yg = p.dma("sync", yg_t[:, 0:ndi * TCA].rearrange("p (i tc) -> p i tc", i=ndi),
                     m["yg_s"][:, c * TCA:(c + 1) * TCA].rearrange("(i p) tc -> p i tc", p=128))
        c_z = p.dma("sync", m["z_ch"][:, 0:ndi * TCA].rearrange("p (i tc) -> p i tc", i=ndi),
                    m["z_s"][:, c * TCA:(c + 1) * TCA].rearrange("(i p) tc -> p i tc", p=128))
        c_gate = 0
        for i in range(ndi):
            c_sz = p.s(lambda e, i=i: e.activation(m["sz_g"][:, (i % 2) * TCA:(i % 2 + 1) * TCA],
                                                   m["z_ch"][:, i * TCA:(i + 1) * TCA], AF.Silu),
                       wait=[("qs", c_z), ("v", m["sz_free"][i % 2])])
            m["sz_free"][i % 2] = c_gate = p.v(
                lambda e, i=i: e.tensor_tensor(yg_t[:, i * TCA:(i + 1) * TCA],
                                               yg_t[:, i * TCA:(i + 1) * TCA],
                                               m["sz_g"][:, (i % 2) * TCA:(i % 2 + 1) * TCA], ALU.mult),
                wait=[("s", c_sz)])
        p.op("tensor", lambda e: e.nop(), wait=[("v", c_gate)])
        ps_free = [0, 0]
        for po in range(nd):
            ps = m["pb"][po % 2]
            c_mm = tiled_mm(p, ps[:, 0:TCA],
                            [m["w_out"][:, pi * D + po * 128:pi * D + (po + 1) * 128] for pi in range(ndi)],
                            [yg_t[:, pi * TCA:(pi + 1) * TCA] for pi in range(ndi)],
                            wait=[("v", ps_free[po % 2])])
            p.v(lambda e, po=po, ps=ps: e.tensor_tensor(
                m["scr"][:, 0:TCA], ps[:, 0:TCA], x_t[:, po * TCA:(po + 1) * TCA], ALU.add),
                wait=[("t", c_mm), ("qa", m["x1_paged"])])
            ps_free[po % 2] = p.v(lambda e, po=po: e.tensor_tensor(
                x1_t[:, po * TCA:(po + 1) * TCA], m["scr"][:, 0:TCA],
                xn_t[:, po * TCA:(po + 1) * TCA], ALU.add))
        c_x1 = p.n["v"]
        m["x1_paged"] = p.dma("scalar",
                              m["x1p_s"][:, c * TCA:(c + 1) * TCA].rearrange("(i p) tc -> p i tc", p=128),
                              x1_t[:, 0:nd * TCA].rearrange("p (i tc) -> p i tc", i=nd),
                              wait=[("v", c_x1)])
        c_lg = tiled_mm(p, m["pb"][2][0:16, 0:TCA],
                        [m["gate_wT"][:, pi * NE:(pi + 1) * NE] for pi in range(nd)],
                        [xn_t[:, pi * TCA:(pi + 1) * TCA] for pi in range(nd)],
                        wait=[("s", m["lg_free"])])
        m["lg_free"] = p.s(lambda e, c=c: e.activation(
            m["logitsT"][0:16, c * TCA:(c + 1) * TCA], m["pb"][2][0:16, 0:TCA], AF.Identity),
            wait=[("t", c_lg)])
        snap = p.barrier()

    NTE = NTT * NE
    LG = m["gb"][:, 0 * NTE:1 * NTE]
    BI = m["gb"][:, 1 * NTE:2 * NTE]
    SC = m["gb"][:, 2 * NTE:3 * NTE]
    M1 = m["gb"][:, 3 * NTE:4 * NTE]
    M2 = m["gb"][:, 4 * NTE:5 * NTE]
    TM = m["gb"][:, 5 * NTE:6 * NTE]
    G1 = m["gb"][:, 6 * NTE:7 * NTE]
    V1 = m["gn"][:, 0:16]
    V2 = m["gn"][:, 16:32]
    S1 = m["gn"][:, 32:48]
    S2 = m["gn"][:, 48:64]
    RC = m["gn"][:, 64:80]
    W1 = m["gn"][:, 80:96]
    W2 = m["gn"][:, 96:112]
    DEN = m["gn"][:, 112:128]
    DEN2 = m["gn"][:, 128:144]
    RC2 = m["gn"][:, 144:160]
    d3 = lambda ap: ap.rearrange("p (o e) -> p o e", e=NE)
    bc = lambda ap: ap.rearrange("p (o e) -> p o e", o=1).broadcast_to((128, NTT, NE))
    bcn = lambda ap: ap.rearrange("p (o e) -> p o e", e=1).broadcast_to((128, NTT, NE))
    for tt in range(NTT):
        c_tr = ptrans(p, m["pb"][3][:, 0:16], m["logitsT"][0:16, tt * 128:(tt + 1) * 128],
                      m["ident"], wait=[("s", m["lg_free"]), ("v", m["lg_copy"])])
        m["lg_copy"] = p.v(lambda e, tt=tt: e.tensor_copy(LG[:, tt * NE:(tt + 1) * NE],
                                                          m["pb"][3][:, 0:16]), wait=[("t", c_tr)])
    c_lgv = p.v(lambda e: e.tensor_tensor(d3(BI), d3(LG), bc(m["ebias_rep"][:, 0:16]), ALU.add),
                wait=[("v", m["lg_copy"], True)])
    c_sig = p.s(lambda e: e.activation(SC, LG, AF.Sigmoid), wait=[("v", c_lgv)])
    c_v1 = p.v(lambda e: e.tensor_reduce(V1, d3(BI), AX.X, ALU.max))
    p.v(lambda e: e.tensor_tensor(d3(M1), d3(BI), bcn(V1), ALU.is_ge), wait=[("v", c_v1, True)])
    p.v(lambda e: e.scalar_tensor_tensor(TM, M1, -1e30, BI, ALU.mult, ALU.add))
    c_v2 = p.v(lambda e: e.tensor_reduce(V2, d3(TM), AX.X, ALU.max))
    p.v(lambda e: e.tensor_tensor(d3(M2), d3(TM), bcn(V2), ALU.is_ge), wait=[("v", c_v2, True)])
    p.v(lambda e: e.tensor_tensor(TM, SC, M1, ALU.mult), wait=[("s", c_sig)])
    c_s1 = p.v(lambda e: e.tensor_reduce(S1, d3(TM), AX.X, ALU.add))
    p.v(lambda e: e.tensor_tensor(TM, SC, M2, ALU.mult))
    c_s2 = p.v(lambda e: e.tensor_reduce(S2, d3(TM), AX.X, ALU.add))
    p.v(lambda e: e.tensor_tensor(DEN, S1, S2, ALU.add), wait=[("v", c_s2, True)])
    c_den = p.v(lambda e: e.tensor_scalar(DEN2, DEN, 1e-10, None, ALU.add), wait=[("v", p.n["v"], True)])
    p.s(lambda e: e.activation(RC, DEN2, AF.Ln), wait=[("v", c_den)])
    c_rec = p.s(lambda e: e.activation(RC2, RC, AF.Exp, scale=-1.0))
    p.v(lambda e: e.tensor_tensor(W1, S1, RC2, ALU.mult), wait=[("s", c_rec)])
    c_w2 = p.v(lambda e: e.tensor_tensor(W2, S2, RC2, ALU.mult))
    p.v(lambda e: e.tensor_tensor(d3(G1), d3(M1), bcn(W1), ALU.mult), wait=[("v", c_w2, True)])
    p.v(lambda e: e.tensor_tensor(d3(TM), d3(M2), bcn(W2), ALU.mult))
    c_gw = p.v(lambda e: e.tensor_tensor(m["gatew"][:, 0:NTE], G1, TM, ALU.add))
    for tt in range(NTT):
        c_tr2 = ptrans(p, m["pb"][4][0:16, 0:128], m["gatew"][:, tt * NE:(tt + 1) * NE],
                       m["ident"], wait=[("v", c_gw), ("s", m["gt_free"])])
        m["gt_free"] = p.s(lambda e, tt=tt: e.activation(
            m["gatewT"][0:16, tt * 128:(tt + 1) * 128], m["pb"][4][0:16, 0:128], AF.Identity),
            wait=[("t", c_tr2)])
    if S3_DEBUG:
        p.v(lambda e: e.tensor_copy(m["dbgm_sb"][:, 0:NTE], M1), wait=[("v", c_gw)])
        p.v(lambda e: e.tensor_copy(m["dbgm_sb"][:, NTE:2 * NTE], M2))
        p.v(lambda e: e.tensor_copy(m["dbgm_sb"][:, 2 * NTE:3 * NTE], BI))
        p.v(lambda e: e.tensor_copy(m["dbgm_sb"][:, 3 * NTE:4 * NTE], SC))
        p.v(lambda e: e.tensor_copy(m["dbgc_sb"][:, 0:112], m["gn"][:, 0:112]))
    c_gating = p.barrier()

    c_ones = m["c_ones"]
    for tt in range(NTT):
        for h in range(2):
            c_mm = p.t(lambda e, tt=tt, h=h: e.matmul(
                m["pb"][tt % 2][:, 0:512],
                m["gatewT"][0:17, tt * 128:(tt + 1) * 128],
                m["b2aug"][0:17, h * 512:(h + 1) * 512], start=True, stop=True),
                wait=[("v", max(c_ones, m["t2_free"][tt % 2])), ("s", m["gt_free"])])
            m["t2_free"][tt % 2] = p.v(lambda e, tt=tt, h=h: e.tensor_copy(
                m["Y_acc"][:, tt * D + h * 512:tt * D + (h + 1) * 512], m["pb"][tt % 2][:, 0:512]),
                wait=[("t", c_mm)])
    snap = p.barrier()

    for e_i in range(NE + 1):
        shared = (e_i == 0)
        nf = FDFF0 // 128 if shared else DFF0 // 128
        p.wait_snapshot(snap)
        if shared:
            c_w = 0
            c_w = p.dma("gpsimd", m["wp1"][:, 0:nd * FDFF0].rearrange("p (i f) -> p i f", i=nd),
                        m["sw1T_d"].rearrange("(i p) f -> p i f", p=128))
            c_w = p.dma("gpsimd", m["wp3"][:, 0:nd * FDFF0].rearrange("p (i f) -> p i f", i=nd),
                        m["sw3T_d"].rearrange("(i p) f -> p i f", p=128))
            c_w = p.dma("gpsimd", m["wp2"][:, 0:nf * D].rearrange("p (i f) -> p i f", i=nf),
                        m["sw2T_d"].rearrange("(i p) f -> p i f", p=128))
        else:
            ee = e_i - 1
            c_w = p.dma("gpsimd", m["wp1"][:, 0:nd * DFF0].rearrange("p (i f) -> p i f", i=nd),
                        m["w1T_d"][ee].rearrange("(i p) f -> p i f", p=128))
            c_w = p.dma("gpsimd", m["wp3"][:, 0:nd * DFF0].rearrange("p (i f) -> p i f", i=nd),
                        m["w3T_d"][ee].rearrange("(i p) f -> p i f", p=128))
            c_w = p.dma("gpsimd", m["wp2"][:, 0:nf * D].rearrange("p (i f) -> p i f", i=nf),
                        m["w2T_d"][ee].rearrange("(i p) f -> p i f", p=128))
        FD = nf * 128
        for c in range(T // TCE):
            c_xn = p.dma("sync", m["xn_e"][:, 0:nd * TCE].rearrange("p (i tc) -> p i tc", i=nd),
                         m["xn_s"][:, c * TCE:(c + 1) * TCE].rearrange("(i p) tc -> p i tc", p=128),
                         wait=[("t", m["xn_used"])])
            p.op("tensor", lambda e: e.nop(), wait=[("qs", c_xn), ("qg", c_w)])
            for po in range(nf):
                ps = m["pb"][po % 2]
                c_mm = tiled_mm(p, ps[:, 0:TCE],
                                [m["wp1"][:, pi * FD + po * 128:pi * FD + (po + 1) * 128] for pi in range(nd)],
                                [m["xn_e"][:, pi * TCE:(pi + 1) * TCE] for pi in range(nd)],
                                wait=[("v", m["h_free"][po % 2]), ("s", m["h_free_s"][po % 2])])
                if shared:
                    bcol = m["sb1"][:, po:po + 1]
                else:
                    bcol = m["b1_all"][:, (e_i - 1) * 4 + po:(e_i - 1) * 4 + po + 1]
                m["h_free_s"][po % 2] = p.s(lambda e, po=po, ps=ps, bcol=bcol: e.activation(
                    m["H1"][:, po * TCE:(po + 1) * TCE], ps[:, 0:TCE], AF.Silu, bias=bcol),
                    wait=[("t", c_mm)])
                ps2 = m["pb"][2 + po % 2]
                c_mm3 = tiled_mm(p, ps2[:, 0:TCE],
                                 [m["wp3"][:, pi * FD + po * 128:pi * FD + (po + 1) * 128] for pi in range(nd)],
                                 [m["xn_e"][:, pi * TCE:(pi + 1) * TCE] for pi in range(nd)],
                                 wait=[("v", m["h3_free"][po % 2])])
                if shared:
                    bcol3 = m["sb3"][:, po:po + 1]
                else:
                    bcol3 = m["b3_all"][:, (e_i - 1) * 4 + po:(e_i - 1) * 4 + po + 1]
                p.v(lambda e, po=po, ps2=ps2, bcol3=bcol3: e.tensor_scalar(
                    m["H3"][:, po * TCE:(po + 1) * TCE], ps2[:, 0:TCE], bcol3, None, ALU.add),
                    wait=[("t", c_mm3), ("s", m["h_free_s"][po % 2])])
                m["h3_free"][po % 2] = m["h_free"][po % 2] = p.v(
                    lambda e, po=po: e.tensor_tensor(
                        m["Hg"][:, po * TCE:(po + 1) * TCE], m["H1"][:, po * TCE:(po + 1) * TCE],
                        m["H3"][:, po * TCE:(po + 1) * TCE], ALU.mult))
            m["xn_used"] = p.n["t"]
            c_hg = p.n["v"]
            for j in range(TCE // 128):
                gtile = c * (TCE // 128) + j
                for h in range(2):
                    ps = m["pb"][4 + (j * 2 + h) % 2]
                    c_mm = tiled_mm(p, ps[:, 0:512],
                                    [m["Hg"][:, pi * TCE + j * 128:pi * TCE + (j + 1) * 128] for pi in range(nf)],
                                    [m["wp2"][:, pi * D + h * 512:pi * D + (h + 1) * 512] for pi in range(nf)],
                                    wait=[("v", max(c_hg, m["y_free"][(j * 2 + h) % 2]))])
                    dst = m["Y_acc"][:, gtile * D + h * 512:gtile * D + (h + 1) * 512]
                    if shared:
                        m["y_free"][(j * 2 + h) % 2] = p.v(
                            lambda e, dst=dst, ps=ps: e.scalar_tensor_tensor(
                                dst, ps[:, 0:512], 1.0, dst, ALU.mult, ALU.add),
                            wait=[("t", c_mm)])
                    else:
                        gcol = m["gatew"][:, gtile * NE + (e_i - 1):gtile * NE + e_i]
                        m["y_free"][(j * 2 + h) % 2] = p.v(
                            lambda e, dst=dst, ps=ps, gcol=gcol: e.scalar_tensor_tensor(
                                dst, ps[:, 0:512], gcol, dst, ALU.mult, ALU.add),
                            wait=[("t", c_mm)])
        snap = p.barrier()

    for tt in range(NTT):
        p.wait_snapshot(snap)
        c_l = p.dma("sync", m["x1_ch"][:, 0:nd * 128].rearrange("p (i tc) -> p i tc", i=nd),
                    m["x1p_s"][:, tt * 128:(tt + 1) * 128].rearrange("(i p) tc -> p i tc", p=128),
                    wait=[("qa", m["x1o_paged"]), ("v", m["x1o_free"])])
        for dd in range(nd):
            c_tr = ptrans(p, m["pb"][5 + dd % 2][:, 0:128],
                          m["Y_acc"][:, tt * D + dd * 128:tt * D + (dd + 1) * 128],
                          m["ident"], wait=[("v", m["tr_free"][dd % 2])])
            p.op("vector", lambda e: e.nop(), wait=[("qs", c_l)])
            m["tr_free"][dd % 2] = p.v(lambda e, tt=tt, dd=dd: e.tensor_tensor(
                m["x1_ch"][:, dd * 128:(dd + 1) * 128], m["x1_ch"][:, dd * 128:(dd + 1) * 128],
                m["pb"][5 + dd % 2][:, 0:128], ALU.add),
                wait=[("t", c_tr)])
        m["x1o_free"] = p.n["v"]
        m["x1o_paged"] = p.dma("scalar",
                               m["x1_s"][:, tt * 128:(tt + 1) * 128].rearrange("(i p) tc -> p i tc", p=128),
                               m["x1_ch"][:, 0:nd * 128].rearrange("p (i tc) -> p i tc", i=nd),
                               wait=[("v", m["x1o_free"])])
    if S3_DEBUG:
        p.dma("scalar", m["dbg_gatew"], m["gatew"][:])
        p.dma("scalar", m["dbg_logitsT"], m["logitsT"][0:16, :])
        p.dma("scalar", m["dbg_yacc"], m["Y_acc"][:])
        p.dma("scalar", m["dbg_m"], m["dbgm_sb"][:])
        p.dma("scalar", m["dbg_c"], m["dbgc_sb"][:])
    p.op("sync", lambda e: e.nop(), wait=[("qa", p.n["qa"]), ("qs", p.n["qs"]), ("qg", p.n["qg"])])


def s3_preamble(p, m):
    nd = D // 128
    ndi = DI // 128
    for pi in range(ndi):
        p.dma("gpsimd", m["w_out"][:, pi * D:(pi + 1) * D], m["w_out_d"][pi * 128:(pi + 1) * 128, :])
    p.dma("gpsimd", m["gate_wT"][:, 0:nd * 16].rearrange("p (i f) -> p i f", i=nd),
          m["gate_wT_d"].rearrange("(i p) f -> p i f", p=128))
    p.dma("sync", m["ident"][:], m["ident_d"])
    p.dma("sync", m["ebias_rep"][:, 0:16], m["ebias_d"])
    p.dma("sync", m["b2aug"][0:17, :], m["b2aug_d"])
    p.dma("sync", m["b1_all"][:], m["b1_all_d"])
    p.dma("sync", m["b3_all"][:], m["b3_all_d"])
    p.dma("sync", m["sb1"][:], m["sb1_d"])
    p.dma("sync", m["sb3"][:], m["sb3_d"])
    p.v(lambda e: e.memset(m["gatewT"][:], 1.0))
    m["c_ones"] = p.n["v"]
    for k in ["x1_paged", "lg_free", "gt_free", "x1o_paged", "lg_copy", "xn_used"]:
        m[k] = 0
    m["t2_free"] = [0, 0]
    m["sz_free"] = [0, 0]
    m["h_free"] = [0, 0]
    m["h_free_s"] = [0, 0]
    m["h3_free"] = [0, 0]
    m["y_free"] = [0, 0]
    m["tr_free"] = [0, 0]
    m["x1o_free"] = 0


def build_s3():
    nc = _new_nc()
    nd = D // 128
    ndi = DI // 128
    NE = 16
    m = {}
    m["x"] = nc.dram_tensor("x", [D, T], F32, kind="ExternalInput").ap()
    m["xn_s"] = nc.dram_tensor("xn_s", [D, T], BF16, kind="ExternalInput").ap()
    m["yg_s"] = nc.dram_tensor("yg_s", [DI, T], BF16, kind="ExternalInput").ap()
    m["z_s"] = nc.dram_tensor("z_s", [DI, T], BF16, kind="ExternalInput").ap()
    m["w_out_d"] = nc.dram_tensor("w_out", [DI, D], F32, kind="ExternalInput").ap()
    m["gate_wT_d"] = nc.dram_tensor("gate_wT", [D, NE], F32, kind="ExternalInput").ap()
    m["ebias_d"] = nc.dram_tensor("ebias", [128, NE], F32, kind="ExternalInput").ap()
    m["ident_d"] = nc.dram_tensor("ident", [128, 128], F32, kind="ExternalInput").ap()
    m["b2aug_d"] = nc.dram_tensor("b2aug", [17, D], F32, kind="ExternalInput").ap()
    m["b1_all_d"] = nc.dram_tensor("b1_all", [128, NE * 4], F32, kind="ExternalInput").ap()
    m["b3_all_d"] = nc.dram_tensor("b3_all", [128, NE * 4], F32, kind="ExternalInput").ap()
    m["sb1_d"] = nc.dram_tensor("sb1", [128, 8], F32, kind="ExternalInput").ap()
    m["sb3_d"] = nc.dram_tensor("sb3", [128, 8], F32, kind="ExternalInput").ap()
    m["w1T_d"] = [nc.dram_tensor(f"w1T_{e}", [D, DFF0], F32, kind="ExternalInput").ap() for e in range(NE)]
    m["w3T_d"] = [nc.dram_tensor(f"w3T_{e}", [D, DFF0], F32, kind="ExternalInput").ap() for e in range(NE)]
    m["w2T_d"] = [nc.dram_tensor(f"w2T_{e}", [DFF0, D], F32, kind="ExternalInput").ap() for e in range(NE)]
    m["sw1T_d"] = nc.dram_tensor("sw1T", [D, FDFF0], F32, kind="ExternalInput").ap()
    m["sw3T_d"] = nc.dram_tensor("sw3T", [D, FDFF0], F32, kind="ExternalInput").ap()
    m["sw2T_d"] = nc.dram_tensor("sw2T", [FDFF0, D], F32, kind="ExternalInput").ap()
    m["x1p_s"] = nc.dram_tensor("x1p_s", [D, T], F32, kind="ExternalOutput").ap()
    m["x1_s"] = nc.dram_tensor("x1_s", [D, T], F32, kind="ExternalOutput").ap()
    if S3_DEBUG:
        m["dbg_gatew"] = nc.dram_tensor("dbg_gatew", [128, (T // 128) * 16], F32, kind="ExternalOutput").ap()
        m["dbg_logitsT"] = nc.dram_tensor("dbg_logitsT", [16, T], F32, kind="ExternalOutput").ap()
        m["dbg_yacc"] = nc.dram_tensor("dbg_yacc", [128, (T // 128) * D], F32, kind="ExternalOutput").ap()
        m["dbg_m"] = nc.dram_tensor("dbg_m", [128, (T // 128) * 96], F32, kind="ExternalOutput").ap()
        m["dbg_c"] = nc.dram_tensor("dbg_c", [128, 128], F32, kind="ExternalOutput").ap()

    es = contextlib.ExitStack()
    with es:
        sb = lambda name, shape, dt: es.enter_context(nc.sbuf_tensor(name, shape, dt))
        pool_a = sb("pool_a", [128, ndi * D], BF16)
        m["w_out"] = pool_a[:, 0:ndi * D]
        m["gate_wT"] = sb("gate_wT_sb", [128, nd * NE], BF16)
        m["ident"] = sb("ident_sb", [128, 128], F32)
        m["ebias_rep"] = sb("ebias_rep", [128, NE], F32)
        m["b2aug"] = sb("b2aug_sb", [17, D], F32)
        m["b1_all"] = sb("b1_all_sb", [128, NE * 4], F32)
        m["b3_all"] = sb("b3_all_sb", [128, NE * 4], F32)
        m["sb1"] = sb("sb1_sb", [128, 8], F32)
        m["sb3"] = sb("sb3_sb", [128, 8], F32)
        m["x_ch"] = sb("x_ch", [128, nd * 256], F32)
        m["xn_ch"] = sb("xn_ch", [128, nd * 256], BF16)
        pool_b = sb("pool_b", [128, ndi * 256], BF16)
        m["yg_ch"] = pool_b[:, 0:ndi * 256]
        m["xn_e"] = pool_b[:, 0:nd * 512]
        m["z_ch"] = sb("z_ch3", [128, ndi * 256], BF16)
        m["sz_g"] = sb("sz_g3", [128, 512], F32)
        m["x1_ch"] = sb("x1_ch", [128, nd * 256], F32)
        m["scr"] = sb("scr", [128, 512], F32)
        m["logitsT"] = sb("logitsT", [16, T], F32)
        m["gatew"] = sb("gatew", [128, (T // 128) * NE], F32)
        m["gatewT"] = sb("gatewT", [17, T], F32)
        m["gb"] = sb("gb", [128, 7 * (T // 128) * NE], F32)
        m["gn"] = sb("gn", [128, 160], F32)
        if S3_DEBUG:
            m["dbgm_sb"] = sb("dbgm_sb", [128, (T // 128) * 96], F32)
            m["dbgc_sb"] = sb("dbgc_sb", [128, 128], F32)
        m["Y_acc"] = sb("Y_acc", [128, (T // 128) * D], F32)
        m["wp1"] = pool_a[:, 0:nd * FDFF0]
        m["wp3"] = pool_a[:, nd * FDFF0:2 * nd * FDFF0]
        m["wp2"] = sb("wp2", [128, (FDFF0 // 128) * D], BF16)
        m["H1"] = sb("H1", [128, 8 * 512], BF16)
        m["H3"] = sb("H3", [128, 8 * 512], BF16)
        m["Hg"] = sb("Hg", [128, 8 * 512], BF16)
        m["pb"] = [es.enter_context(nc.psum_tensor(f"pb{i}", [128, 512], F32)) for i in range(7)]

        sem_names = ["t", "v", "s", "g", "qg", "qs", "qa", "y"]
        sems = {k: es.enter_context(nc.semaphore(k)) for k in sem_names}

        def describe(p):
            s3_preamble(p, m)
            s3_body(p, m)

        with nc.Block() as block:
            for en in ENGINES:
                def mk(en):
                    def section(eng):
                        describe(Prog(nc, sems, active=en, eng=eng))
                    return section
                getattr(block, en)(mk(en))
    return nc


def s3_inmap(inputs, xT, s1_out, s2_out):
    NE = 16
    w1 = np.asarray(inputs["L0_ffn_w1"], np.float32)
    w3 = np.asarray(inputs["L0_ffn_w3"], np.float32)
    w2 = np.asarray(inputs["L0_ffn_w2"], np.float32)
    b1 = np.asarray(inputs["L0_ffn_b1"], np.float32)
    b3 = np.asarray(inputs["L0_ffn_b3"], np.float32)
    b2 = np.asarray(inputs["L0_ffn_b2"], np.float32)
    sb2 = np.asarray(inputs["L0_ffn_sb2"], np.float32)
    im = dict(
        x=xT,
        xn_s=s1_out["xn_s"],
        yg_s=s2_out["yg_s"],
        z_s=s1_out["z_s"],
        w_out=np.ascontiguousarray(np.asarray(inputs["L0_attn_w_out"], np.float32)),
        gate_wT=np.ascontiguousarray(np.asarray(inputs["L0_ffn_gate_w"], np.float32).T),
        ebias=np.ascontiguousarray(np.broadcast_to(
            np.asarray(inputs["L0_ffn_expert_bias"], np.float32).reshape(1, NE), (128, NE))),
        ident=np.eye(128, dtype=np.float32),
        b2aug=np.ascontiguousarray(np.concatenate([b2, sb2[None, :]], axis=0)),
        b1_all=np.ascontiguousarray(b1.reshape(NE, 4, 128).transpose(2, 0, 1).reshape(128, NE * 4)),
        b3_all=np.ascontiguousarray(b3.reshape(NE, 4, 128).transpose(2, 0, 1).reshape(128, NE * 4)),
        sb1=_col_tiles(inputs["L0_ffn_sb1"]),
        sb3=_col_tiles(inputs["L0_ffn_sb3"]),
        sw1T=np.ascontiguousarray(np.asarray(inputs["L0_ffn_sw1"], np.float32).T),
        sw3T=np.ascontiguousarray(np.asarray(inputs["L0_ffn_sw3"], np.float32).T),
        sw2T=np.ascontiguousarray(np.asarray(inputs["L0_ffn_sw2"], np.float32).T),
    )
    for e in range(NE):
        im[f"w1T_{e}"] = np.ascontiguousarray(w1[e].T)
        im[f"w3T_{e}"] = np.ascontiguousarray(w3[e].T)
        im[f"w2T_{e}"] = np.ascontiguousarray(w2[e].T)
    return im


NH = 16
HD = 64
NLM = 64


def s4_softmax_rows(p, m, dst_bf, src_f32, P_, W, rmx, rsum, rsinv, bias_c, tag):
    c_r = p.v(lambda e: e.reduce_max(rmx, src_f32, AX.X))
    c_b = p.v(lambda e: e.tensor_scalar(bias_c, rmx, -0.125, None, ALU.mult),
              wait=[("v", c_r, True)])
    c_e = p.s(lambda e: e.activation(src_f32, src_f32, AF.Exp, scale=0.125,
                                     bias=bias_c, accum_out=rsum), wait=[("v", c_b)])
    p.s(lambda e: e.activation(rsinv, rsum, AF.Ln))
    c_i = p.s(lambda e: e.activation(rsinv, rsinv, AF.Exp, scale=-1.0))
    return p.v(lambda e: e.tensor_scalar(dst_bf, src_f32, rsinv, None, ALU.mult),
               wait=[("s", c_i)])


def s4_body(p, m):
    nd = D // 128
    TC = 256
    NTT = T // 128
    snap = p.barrier()

    for c in range(T // TC):
        p.wait_snapshot(snap)
        x_t = m["x_ch"]
        c_x = p.dma("sync", x_t[:, 0:nd * TC].rearrange("p (i tc) -> p i tc", i=nd),
                    m["x1_s"][:, c * TC:(c + 1) * TC].rearrange("(i p) tc -> p i tc", p=128))
        x_tiles = [x_t[:, i * TC:(i + 1) * TC] for i in range(nd)]
        p.op("scalar", lambda e: e.nop(), wait=[("qs", c_x)])
        p.op("tensor", lambda e: e.nop(), wait=[("qs", c_x)])
        c_reps = ln_stats(p, m, TC, x_tiles, nd, m["onescol"][:, 0:1], m["sq"])
        c_xn_v = 0
        for i in range(nd):
            p.v(lambda e, i=i: e.tensor_tensor(m["sq"][:, i * TC:(i + 1) * TC],
                                               x_t[:, i * TC:(i + 1) * TC], m["mu_rep"][:, 0:TC], ALU.add),
                wait=[("v", c_reps)])
            c_xn_v = p.v(lambda e, i=i, c=c: e.tensor_tensor(
                m["xnl_ch"][:, i * TC:(i + 1) * TC],
                m["sq"][:, i * TC:(i + 1) * TC], m["mu_rep"][:, TC:2 * TC], ALU.mult),
                wait=[("qa", m.get("xnl_paged", 0))])
        m["xnl_paged"] = p.dma("scalar",
              m["xn1T_s"][:, c * TC:(c + 1) * TC].rearrange("(i p) tc -> p i tc", p=128),
              m["xnl_ch"][:, 0:nd * TC].rearrange("p (i tc) -> p i tc", i=nd),
              wait=[("v", c_xn_v)])
        snap = p.barrier()

    xn_tiles = None
    lanes = [dict(qk_free=[0, 0], v_free=[0, 0], k3_free=[0, 0], k3t_free=[0, 0],
                  k1_free=[0, 0], sv_free=[0, 0], k1s_free=0) for _ in range(2)]

    def pair_qkv(Ls, hpair):
        for lane in range(2):
            Ls[lane]["c_w"] = p.dma("gpsimd",
                m["lane"][lane]["wqkv_h"][:, 0:nd * 3 * HD].rearrange("p (i f) -> p i f", i=nd),
                m["wqkv_hd"][hpair * 2 + lane].rearrange("(i p) f -> p i f", p=128),
                wait=[("t", Ls[lane].get("wqkv_used", 0))])
        for ch in range(4):
            c_xn = p.dma("sync", m["xn_st"][ch % 2][:, 0:nd * 512].rearrange("p (i tc) -> p i tc", i=nd),
                         m["xn1T_s"][:, ch * 512:(ch + 1) * 512].rearrange("(i p) tc -> p i tc", p=128),
                         wait=[("t", m["xn_used"][ch % 2])])
            xs = [m["xn_st"][ch % 2][:, pi * 512:(pi + 1) * 512] for pi in range(nd)]
            for lane in range(2):
                L = Ls[lane]
                B = m["lane"][lane]
                p.op("tensor", lambda e: e.nop(), wait=[("qs", c_xn), ("qg", L["c_w"])])
                for which, dst in ((0, B["QT"]), (1, B["KT"])):
                    ps = m["pb"][lane * 2 + (which % 2)]
                    c_mm = tiled_mm(p, ps[0:HD, 0:512],
                                    [B["wqkv_h"][:, pi * 3 * HD + which * HD:pi * 3 * HD + (which + 1) * HD]
                                     for pi in range(nd)],
                                    xs, wait=[("s", L["qk_free"][which % 2])])
                    L["qk_free"][which % 2] = p.s(lambda e, dst=dst, ch=ch, ps=ps: e.activation(
                        dst[0:HD, ch * 512:(ch + 1) * 512], ps[0:HD, 0:512], AF.Identity),
                        wait=[("t", c_mm)])
                for j in range(4):
                    tt = ch * 4 + j
                    ps = m["pb"][lane * 2 + (j % 2)]
                    c_mm = tiled_mm(p, ps[:, 0:HD],
                                    [m["xn_st"][ch % 2][:, pi * 512 + j * 128:pi * 512 + (j + 1) * 128]
                                     for pi in range(nd)],
                                    [B["wqkv_h"][:, pi * 3 * HD + 2 * HD:pi * 3 * HD + 3 * HD]
                                     for pi in range(nd)],
                                    wait=[("s", L["v_free"][j % 2]), ("s", L["qk_free"][0]),
                                          ("s", L["qk_free"][1])])
                    L["v_free"][j % 2] = p.s(lambda e, tt=tt, ps=ps: e.activation(
                        B["Vn"][:, tt * HD:(tt + 1) * HD], ps[:, 0:HD], AF.Identity),
                        wait=[("t", c_mm)])
            m["xn_used"][ch % 2] = p.n["t"]
        for lane in range(2):
            Ls[lane]["c_qkv_s"] = p.n["s"]

    def head_steps(L, h, lane):
        B = m["lane"][lane]
        yield "landmarks"
        c_klm = p.v(lambda e: e.tensor_reduce(B["lmf"][0:HD, 0:NLM],
                                              B["KT"][0:HD, 0:T].rearrange("p (l g) -> p l g", g=32),
                                              AX.X, ALU.add), wait=[("s", L["c_qkv_s"])])
        p.v(lambda e: e.tensor_scalar(B["klm"][0:HD, 0:NLM], B["lmf"][0:HD, 0:NLM],
                                      1.0 / 32, None, ALU.mult), wait=[("v", c_klm, True)])
        c_qlm = p.v(lambda e: e.tensor_reduce(B["lmf"][0:HD, NLM:2 * NLM],
                                              B["QT"][0:HD, 0:T].rearrange("p (l g) -> p l g", g=32),
                                              AX.X, ALU.add))
        L["c_lm"] = p.v(lambda e: e.tensor_scalar(B["qlm"][0:HD, 0:NLM], B["lmf"][0:HD, NLM:2 * NLM],
                                                  1.0 / 32, None, ALU.mult), wait=[("v", c_qlm, True)])
        yield "k2"
        c_k2 = p.t(lambda e: e.matmul(m["pb"][4 + lane][0:NLM, 0:NLM], B["qlm"][0:HD, 0:NLM],
                                      B["klm"][0:HD, 0:NLM], start=True, stop=True),
                   wait=[("v", L["c_lm"]), ("v", L.get("k2ps_free", 0))])
        c_k2e = p.v(lambda e: e.tensor_copy(B["K2"][0:NLM, 0:NLM], m["pb"][4 + lane][0:NLM, 0:NLM]),
                    wait=[("t", c_k2)])
        nsc = B["nsc"]
        c_sm2 = s4_softmax_rows(p, m, B["K2n"][0:NLM, 0:NLM], B["K2"][0:NLM, 0:NLM],
                                NLM, NLM, nsc[0:NLM, 0:1], nsc[0:NLM, 1:2], nsc[0:NLM, 2:3],
                                nsc[0:NLM, 3:4], "k2")
        c_k2t = ptrans(p, m["pb"][4 + lane][0:NLM, 0:NLM], B["K2n"][0:NLM, 0:NLM], m["ident"],
                       wait=[("v", c_sm2)])
        c_k2tc = p.v(lambda e: e.tensor_copy(B["K2T"][0:NLM, 0:NLM], m["pb"][4 + lane][0:NLM, 0:NLM]),
                     wait=[("t", c_k2t)])
        c_cs = p.t(lambda e: e.matmul(m["pb"][4 + lane][0:1, 0:NLM], m["onescol"][0:NLM, 0:1],
                                      B["K2n"][0:NLM, 0:NLM], start=True, stop=True),
                   wait=[("v", c_sm2), ("v", c_k2tc)])
        c_csv = p.v(lambda e: e.tensor_copy(B["nrow"][0:1, 0:NLM], m["pb"][4 + lane][0:1, 0:NLM]),
                    wait=[("t", c_cs)])
        c_mx = p.v(lambda e: e.reduce_max(B["nrow"][0:1, NLM:NLM + 1], B["nrow"][0:1, 0:NLM], AX.X),
                   wait=[("v", c_csv, True)])
        p.s(lambda e: e.activation(B["nrow"][0:1, NLM + 1:NLM + 2], B["nrow"][0:1, NLM:NLM + 1], AF.Ln),
            wait=[("v", c_mx)])
        c_dd = p.s(lambda e: e.activation(B["nrow"][0:1, NLM + 2:NLM + 3],
                                          B["nrow"][0:1, NLM + 1:NLM + 2], AF.Exp, scale=-1.0))
        c_ddr = p.t(lambda e: e.matmul(m["pb"][4 + lane][0:NLM, 0:1], m["onesrow"][0:1, 0:NLM],
                                       B["nrow"][0:1, NLM + 2:NLM + 3], start=True, stop=True),
                    wait=[("s", c_dd)])
        c_ddc = p.v(lambda e: e.tensor_copy(B["nsc"][0:NLM, 4:5], m["pb"][4 + lane][0:NLM, 0:1]),
                    wait=[("t", c_ddr)])
        cur = B["NV0"]
        L["c_prev"] = p.v(lambda e, cur=cur: e.tensor_scalar(cur[0:NLM, 0:NLM], B["K2T"][0:NLM, 0:NLM],
                                                             B["nsc"][0:NLM, 4:5], None, ALU.mult),
                          wait=[("v", c_ddc, True)])
        L["cur"] = cur
        L["k2ps_free"] = L["c_prev"]
        for it in range(6):
            yield f"newton{it}"
            cur = L["cur"]
            nxt = B["NV1"] if cur is B["NV0"] else B["NV0"]
            c_kv = p.t(lambda e, cur=cur: e.matmul(m["pb"][4 + lane][0:NLM, 0:NLM], B["K2T"][0:NLM, 0:NLM],
                                                   cur[0:NLM, 0:NLM], start=True, stop=True),
                       wait=[("v", L["c_prev"])])
            c_kve = p.v(lambda e: e.tensor_copy(B["KV"][0:NLM, 0:NLM], m["pb"][4 + lane][0:NLM, 0:NLM]),
                        wait=[("t", c_kv)])
            c_kvt = ptrans(p, m["pb"][4 + lane][0:NLM, 0:NLM], B["KV"][0:NLM, 0:NLM], m["ident"],
                           wait=[("v", c_kve)])
            c_kvte = p.v(lambda e: e.tensor_copy(B["KVT"][0:NLM, 0:NLM], m["pb"][4 + lane][0:NLM, 0:NLM]),
                         wait=[("t", c_kvt)])
            c_m1 = p.v(lambda e: e.scalar_tensor_tensor(B["NM"][0:NLM, 0:NLM], B["KV"][0:NLM, 0:NLM],
                                                        -1.0, m["I7"][0:NLM, 0:NLM], ALU.mult, ALU.add),
                       wait=[("v", c_kve, True)])
            c_m2 = p.t(lambda e: e.matmul(m["pb"][4 + lane][0:NLM, 0:NLM], B["KVT"][0:NLM, 0:NLM],
                                          B["NM"][0:NLM, 0:NLM], start=True, stop=True),
                       wait=[("v", max(c_m1, c_kvte))])
            c_m2e = p.v(lambda e: e.tensor_copy(B["NM2"][0:NLM, 0:NLM], m["pb"][4 + lane][0:NLM, 0:NLM]),
                        wait=[("t", c_m2)])
            c_m3 = p.v(lambda e: e.scalar_tensor_tensor(B["NM"][0:NLM, 0:NLM], B["NM2"][0:NLM, 0:NLM],
                                                        -1.0, m["I15"][0:NLM, 0:NLM], ALU.mult, ALU.add),
                       wait=[("v", c_m2e, True)])
            c_m4 = p.t(lambda e: e.matmul(m["pb"][4 + lane][0:NLM, 0:NLM], B["KVT"][0:NLM, 0:NLM],
                                          B["NM"][0:NLM, 0:NLM], start=True, stop=True),
                       wait=[("v", c_m3)])
            c_m4e = p.v(lambda e: e.tensor_copy(B["NM2"][0:NLM, 0:NLM], m["pb"][4 + lane][0:NLM, 0:NLM]),
                        wait=[("t", c_m4)])
            c_m5 = p.v(lambda e: e.scalar_tensor_tensor(B["NM"][0:NLM, 0:NLM], B["NM2"][0:NLM, 0:NLM],
                                                        -1.0, m["I13"][0:NLM, 0:NLM], ALU.mult, ALU.add),
                       wait=[("v", c_m4e, True)])
            c_vt = ptrans(p, m["pb"][4 + lane][0:NLM, 0:NLM], cur[0:NLM, 0:NLM], m["ident"],
                          wait=[("v", L["c_prev"]), ("v", c_m5)])
            c_vte = p.v(lambda e: e.tensor_copy(B["NVT"][0:NLM, 0:NLM], m["pb"][4 + lane][0:NLM, 0:NLM]),
                        wait=[("t", c_vt)])
            c_vn = p.t(lambda e: e.matmul(m["pb"][4 + lane][0:NLM, 0:NLM], B["NVT"][0:NLM, 0:NLM],
                                          B["NM"][0:NLM, 0:NLM], start=True, stop=True),
                       wait=[("v", max(c_m5, c_vte))])
            L["c_prev"] = p.v(lambda e, nxt=nxt: e.tensor_scalar(nxt[0:NLM, 0:NLM],
                                                                 m["pb"][4 + lane][0:NLM, 0:NLM],
                                                                 0.25, None, ALU.mult),
                              wait=[("t", c_vn)])
            L["cur"] = nxt
        yield "inv"
        c_it = ptrans(p, m["pb"][4 + lane][0:NLM, 0:NLM], L["cur"][0:NLM, 0:NLM], m["ident"],
                      wait=[("v", L["c_prev"])])
        L["c_inv"] = p.v(lambda e: e.tensor_copy(B["NVT"][0:NLM, 0:NLM], m["pb"][4 + lane][0:NLM, 0:NLM]),
                         wait=[("t", c_it)])
        yield "k3"
        for ch in range(4):
            c_mm = p.t(lambda e, ch=ch: e.matmul(m["pb"][lane * 2 + ch % 2][0:NLM, 0:512],
                                                 B["qlm"][0:HD, 0:NLM],
                                                 B["KT"][0:HD, ch * 512:(ch + 1) * 512],
                                                 start=True, stop=True),
                       wait=[("v", L["k3_free"][ch % 2]), ("v", L["c_lm"])])
            L["k3_free"][ch % 2] = p.v(lambda e, ch=ch: e.tensor_copy(
                B["K3"][0:NLM, ch * 512:(ch + 1) * 512], m["pb"][lane * 2 + ch % 2][0:NLM, 0:512]),
                wait=[("t", c_mm)])
        nsc = B["nsc"]
        c_sm3 = s4_softmax_rows(p, m, B["K3b"][0:NLM, 0:T], B["K3"][0:NLM, 0:T],
                                NLM, T, nsc[0:NLM, 5:6], nsc[0:NLM, 6:7], nsc[0:NLM, 7:8],
                                nsc[0:NLM, 8:9], "k3")
        yield "k3v"
        for tt in range(NTT):
            c_tr = ptrans(p, m["pb"][lane * 2 + tt % 2][:, 0:NLM], B["K3b"][0:NLM, tt * 128:(tt + 1) * 128],
                          m["identb"], wait=[("v", c_sm3), ("v", L["k3t_free"][tt % 2])])
            L["k3t_free"][tt % 2] = p.v(lambda e, tt=tt: e.tensor_copy(
                B["K3T"][:, tt * NLM:(tt + 1) * NLM], m["pb"][lane * 2 + tt % 2][:, 0:NLM]),
                wait=[("t", c_tr)])
        c_k3t = p.n["v"]
        c_k3v = tiled_mm(p, m["pb"][4 + lane][0:NLM, 0:HD],
                         [B["K3T"][:, tt * NLM:(tt + 1) * NLM] for tt in range(NTT)],
                         [B["Vn"][:, tt * HD:(tt + 1) * HD] for tt in range(NTT)],
                         wait=[("v", c_k3t), ("v", L["c_inv"])])
        c_k3ve = p.v(lambda e: e.tensor_copy(B["K3V"][0:NLM, 0:HD], m["pb"][4 + lane][0:NLM, 0:HD]),
                     wait=[("t", c_k3v)])
        c_wh = p.t(lambda e: e.matmul(m["pb"][4 + lane][0:NLM, 0:HD], B["NVT"][0:NLM, 0:NLM],
                                      B["K3V"][0:NLM, 0:HD], start=True, stop=True),
                   wait=[("v", max(L["c_inv"], c_k3ve))])
        L["c_whe"] = p.v(lambda e: e.tensor_copy(B["Wh"][0:NLM, 0:HD], m["pb"][4 + lane][0:NLM, 0:HD]),
                         wait=[("t", c_wh)])
        yield "k1"
        k1s_free = [L["k1s_free"], L["k1s_free"]]
        k1n_free = [0, 0]
        for tt in range(NTT):
            par = tt % 2
            ps = m["pb"][lane * 2 + par]
            K1t = B["K1t"][:, par * NLM:(par + 1) * NLM]
            nc1 = B["k1sc"][:, par * 8:(par + 1) * 8]
            c_mm = p.t(lambda e, tt=tt, ps=ps: e.matmul(
                ps[:, 0:NLM], B["QT"][0:HD, tt * 128:(tt + 1) * 128],
                B["klm"][0:HD, 0:NLM], start=True, stop=True),
                wait=[("v", L["k1_free"][par])])
            c_cp = p.v(lambda e, ps=ps, K1t=K1t: e.tensor_copy(K1t, ps[:, 0:NLM]),
                       wait=[("t", c_mm), ("s", k1s_free[par])])
            c_r = p.v(lambda e, K1t=K1t, nc1=nc1: e.reduce_max(nc1[:, 0:1], K1t, AX.X),
                      wait=[("v", c_cp, True)])
            c_b = p.v(lambda e, nc1=nc1: e.tensor_scalar(nc1[:, 1:2], nc1[:, 0:1], -0.125, None, ALU.mult),
                      wait=[("v", c_r, True)])
            c_e = p.s(lambda e, K1t=K1t, nc1=nc1: e.activation(K1t, K1t, AF.Exp,
                                             scale=0.125, bias=nc1[:, 1:2], accum_out=nc1[:, 2:3]),
                      wait=[("v", c_b)])
            p.s(lambda e, nc1=nc1: e.activation(nc1[:, 3:4], nc1[:, 2:3], AF.Ln))
            c_i = p.s(lambda e, nc1=nc1: e.activation(nc1[:, 4:5], nc1[:, 3:4], AF.Exp, scale=-1.0))
            k1s_free[par] = c_i
            K1n = B["K1n"][:, par * NLM:(par + 1) * NLM]
            c_n = p.v(lambda e, K1t=K1t, K1n=K1n, nc1=nc1: e.tensor_scalar(
                K1n, K1t, nc1[:, 4:5], None, ALU.mult),
                wait=[("s", c_i), ("v", k1n_free[par])])
            c_tr = ptrans(p, m["pb"][4 + lane][0:NLM, 0:128], K1n, m["identb"],
                          wait=[("v", c_n)])
            k1n_free[par] = L["k1_free"][par] = p.v(lambda e, tt=tt: e.tensor_copy(
                B["K1T"][0:NLM, tt * 128:(tt + 1) * 128], m["pb"][4 + lane][0:NLM, 0:128]),
                wait=[("t", c_tr)])
        L["k1s_free"] = max(k1s_free)
        c_k1 = p.n["v"]
        yield "sv"
        sv_pbase = (h % 2) * HD
        sv_tile = h // 2
        for ch in range(4):
            c_mm = p.t(lambda e, ch=ch: e.matmul(
                m["pb"][lane * 2 + ch % 2][0:HD, 0:512], B["Wh"][0:NLM, 0:HD],
                B["K1T"][0:NLM, ch * 512:(ch + 1) * 512], start=True, stop=True),
                wait=[("v", max(c_k1, L["sv_free"][ch % 2], L["c_whe"]))])
            L["sv_free"][ch % 2] = p.v(
                lambda e, ch=ch, sv_pbase=sv_pbase, sv_tile=sv_tile: e.tensor_copy(
                    m["SVT"][sv_pbase:sv_pbase + HD, sv_tile * T + ch * 512:sv_tile * T + (ch + 1) * 512],
                    m["pb"][lane * 2 + ch % 2][0:HD, 0:512]),
                wait=[("t", c_mm)])
        L["wqkv_used"] = p.n["t"]
        yield "done"

    m["xn_used"] = [0, 0]
    for hp in range(NH // 2):
        pair_qkv(lanes, hp)
        gens = [head_steps(lanes[0], 2 * hp, 0), head_steps(lanes[1], 2 * hp + 1, 1)]
        done = [False, False]
        while not all(done):
            for ln in range(2):
                if not done[ln]:
                    try:
                        next(gens[ln])
                    except StopIteration:
                        done[ln] = True
    snap = p.barrier()

    if m.get("defer_wproj"):
        p.wait_snapshot(snap)
        c_wp = 0
        for pi in range(nd):
            c_wp = p.dma("gpsimd", m["w_proj"][:, pi * D:(pi + 1) * D],
                         m["w_proj_d"][pi * 128:(pi + 1) * 128, :])
        p.op("tensor", lambda e: e.nop(), wait=[("qg", c_wp)])
    svt_tiles = [m["SVT"][:, i * T:(i + 1) * T] for i in range(nd)]
    for c in range(T // TC):
        p.wait_snapshot(snap)
        x_t = m["x_ch"]
        c_x = p.dma("sync", x_t[:, 0:nd * TC].rearrange("p (i tc) -> p i tc", i=nd),
                    m["x1_s"][:, c * TC:(c + 1) * TC].rearrange("(i p) tc -> p i tc", p=128))
        p.op("tensor", lambda e: e.nop(), wait=[("qs", c_x)])
        ps_free = [0, 0]
        for po in range(nd):
            ps = m["pb"][po % 2]
            c_mm = tiled_mm(p, ps[:, 0:TC],
                            [m["w_proj"][:, pi * D + po * 128:pi * D + (po + 1) * 128] for pi in range(nd)],
                            [svt_tiles[pi][:, c * TC:(c + 1) * TC] for pi in range(nd)],
                            wait=[("v", ps_free[po % 2])])
            ps_free[po % 2] = p.v(lambda e, po=po, ps=ps: e.tensor_tensor(
                m["xa_ch"][:, po * TC:(po + 1) * TC], ps[:, 0:TC],
                x_t[:, po * TC:(po + 1) * TC], ALU.add),
                wait=[("t", c_mm), ("qa", m["xa_paged"])])
        c_xa = p.n["v"]
        m["xa_paged"] = p.dma("scalar",
                              m["xa_s"][:, c * TC:(c + 1) * TC].rearrange("(i p) tc -> p i tc", p=128),
                              m["xa_ch"][:, 0:nd * TC].rearrange("p (i tc) -> p i tc", i=nd),
                              wait=[("v", c_xa)])
        snap = p.barrier()
    p.op("sync", lambda e: e.nop(), wait=[("qa", p.n["qa"]), ("qs", p.n["qs"]), ("qg", p.n["qg"])])


def s4_preamble(p, m):
    nd = D // 128
    if not m.get("defer_wproj"):
        for pi in range(nd):
            p.dma("gpsimd", m["w_proj"][:, pi * D:(pi + 1) * D], m["w_proj_d"][pi * 128:(pi + 1) * 128, :])
    p.dma("sync", m["ident"][:], m["ident_d"])
    p.dma("gpsimd", m["identb"][:], m["ident_d"])
    p.dma("sync", m["I7"][0:NLM, 0:NLM], m["newc_d"][:, 0:NLM])
    p.dma("sync", m["I15"][0:NLM, 0:NLM], m["newc_d"][:, NLM:2 * NLM])
    p.dma("sync", m["I13"][0:NLM, 0:NLM], m["newc_d"][:, 2 * NLM:3 * NLM])
    p.v(lambda e: e.memset(m["onescol"][:], 1.0))
    p.v(lambda e: e.memset(m["onesrow"][:], 1.0))
    m["qk_free"] = [0, 0]
    m["v_free"] = [0, 0]
    m["k3_free"] = [0, 0]
    m["k3t_free"] = [0, 0]
    m["k1_free"] = [0, 0]
    m["sv_free"] = [0, 0]
    m["k1s_free"] = 0
    m["xa_paged"] = 0
    m["xnl_paged"] = 0


def build_s4():
    nc = _new_nc()
    nd = D // 128
    NTT = T // 128
    m = {}
    m["x1_s"] = nc.dram_tensor("x1_s", [D, T], F32, kind="ExternalInput").ap()
    m["w_proj_d"] = nc.dram_tensor("w_proj", [D, D], F32, kind="ExternalInput").ap()
    m["wqkv_hd"] = [nc.dram_tensor(f"wqkv_{h}", [D, 3 * HD], F32, kind="ExternalInput").ap()
                    for h in range(NH)]
    m["ident_d"] = nc.dram_tensor("ident", [128, 128], F32, kind="ExternalInput").ap()
    m["newc_d"] = nc.dram_tensor("newc", [NLM, 3 * NLM], F32, kind="ExternalInput").ap()
    m["xn1T_s"] = nc.dram_tensor("xn1T_s", [D, T], BF16, kind="ExternalOutput").ap()
    m["xa_s"] = nc.dram_tensor("xa_s", [D, T], F32, kind="ExternalOutput").ap()

    es = contextlib.ExitStack()
    with es:
        sb = lambda name, shape, dt: es.enter_context(nc.sbuf_tensor(name, shape, dt))
        m["w_proj"] = sb("w_proj_sb", [128, nd * D], BF16)
        m["ident"] = sb("ident_sb", [128, 128], F32)
        m["identb"] = sb("identb", [128, 128], BF16)
        m["I7"] = sb("I7", [NLM, NLM], F32)
        m["I15"] = sb("I15", [NLM, NLM], F32)
        m["I13"] = sb("I13", [NLM, NLM], F32)
        m["SVT"] = sb("SVT", [128, nd * T], BF16)
        m["xn_st"] = [sb(f"xn_st{i}", [128, nd * 512], BF16) for i in range(2)]
        m["xnl_ch"] = sb("xnl_ch", [128, nd * 256], BF16)
        m["x_ch"] = sb("x_ch", [128, nd * 256], F32)
        m["xa_ch"] = sb("xa_ch", [128, nd * 256], F32)
        m["sq"] = sb("sq", [128, nd * 256], F32)
        m["row0"] = sb("row0", [1, 256], F32)
        m["row1"] = sb("row1", [1, 256], F32)
        m["row2"] = sb("row2", [1, 256], F32)
        m["mu_rep"] = sb("mu_rep", [128, 2 * 256], F32)
        m["onescol"] = sb("onescol", [128, 1], F32)
        m["onescol_b"] = sb("onescol_b", [128, 1], BF16)
        m["onesrow"] = sb("onesrow", [1, 128], F32)
        m["lane"] = []
        for ln in range(2):
            B = {}
            B["wqkv_h"] = sb(f"wqkv_h{ln}", [128, nd * 3 * HD], BF16)
            B["QT"] = sb(f"QT{ln}", [NLM, T], BF16)
            B["KT"] = sb(f"KT{ln}", [NLM, T], BF16)
            B["Vn"] = sb(f"Vn{ln}", [128, (T // 128) * HD], BF16)
            B["lmf"] = sb(f"lmf{ln}", [NLM, 2 * NLM], F32)
            B["klm"] = sb(f"klm{ln}", [NLM, NLM], BF16)
            B["qlm"] = sb(f"qlm{ln}", [NLM, NLM], BF16)
            B["K2"] = sb(f"K2_{ln}", [NLM, NLM], F32)
            B["K2n"] = sb(f"K2n{ln}", [NLM, NLM], F32)
            B["K2T"] = sb(f"K2T{ln}", [NLM, NLM], F32)
            B["NV0"] = sb(f"NV0{ln}", [NLM, NLM], F32)
            B["NV1"] = sb(f"NV1{ln}", [NLM, NLM], F32)
            B["KV"] = sb(f"KV{ln}", [NLM, NLM], F32)
            B["KVT"] = sb(f"KVT{ln}", [NLM, NLM], F32)
            B["NM"] = sb(f"NM{ln}", [NLM, NLM], F32)
            B["NM2"] = sb(f"NM2{ln}", [NLM, NLM], F32)
            B["NVT"] = sb(f"NVT{ln}", [NLM, NLM], F32)
            B["nsc"] = sb(f"nsc{ln}", [NLM, 16], F32)
            B["nrow"] = sb(f"nrow{ln}", [1, NLM + 4], F32)
            B["K3"] = sb(f"K3_{ln}", [NLM, T], F32)
            kb = sb(f"k3k1_{ln}", [NLM, T], BF16)
            B["K3b"] = kb[0:NLM, 0:T]
            B["K1T"] = kb[0:NLM, 0:T]
            B["K3T"] = sb(f"K3T{ln}", [128, (T // 128) * NLM], BF16)
            B["K3V"] = sb(f"K3V{ln}", [NLM, HD], F32)
            B["Wh"] = sb(f"Wh{ln}", [NLM, HD], BF16)
            B["K1t"] = sb(f"K1t{ln}", [128, 2 * NLM], F32)
            B["K1n"] = sb(f"K1n{ln}", [128, 2 * NLM], BF16)
            B["k1sc"] = sb(f"k1sc{ln}", [128, 16], F32)
            m["lane"].append(B)
        m["pb"] = [es.enter_context(nc.psum_tensor(f"pb{i}", [128, 512], F32)) for i in range(7)]
        m["ps_r0"], m["ps_r1"], m["ps_rep"], m["ps_rep2"] = m["pb"][3], m["pb"][4], m["pb"][5], m["pb"][6]

        sem_names = ["t", "v", "s", "g", "qg", "qs", "qa", "y"]
        sems = {k: es.enter_context(nc.semaphore(k)) for k in sem_names}

        def describe(p):
            s4_preamble(p, m)
            s4_body(p, m)

        with nc.Block() as block:
            for en in ENGINES:
                def mk(en):
                    def section(eng):
                        describe(Prog(nc, sems, active=en, eng=eng))
                    return section
                getattr(block, en)(mk(en))
    return nc


def s4_inmap(inputs, x1):
    w_qkv = np.asarray(inputs["L1_attn_w_qkv"], np.float32)
    eye = np.eye(NLM, dtype=np.float32)
    im = dict(
        x1_s=x1,
        w_proj=np.ascontiguousarray(np.asarray(inputs["L1_attn_w_proj"], np.float32)),
        ident=np.eye(128, dtype=np.float32),
        newc=np.ascontiguousarray(np.concatenate([7 * eye, 15 * eye, 13 * eye], axis=1)),
    )
    for h in range(NH):
        im[f"wqkv_{h}"] = np.ascontiguousarray(
            np.concatenate([w_qkv[:, h * HD:(h + 1) * HD],
                            w_qkv[:, D + h * HD:D + (h + 1) * HD],
                            w_qkv[:, 2 * D + h * HD:2 * D + (h + 1) * HD]], axis=1))
    return im


NE1 = 8
DFF1 = 4096


def s5_body(p, m):
    nd = D // 128
    NTT = T // 128
    TCE = 256
    snap = p.barrier()

    for c in range(4):
        p.wait_snapshot(snap)
        c_xn = p.dma("sync", m["xn_e"][:, 0:nd * 512].rearrange("p (i tc) -> p i tc", i=nd),
                     m["xn1T_s"][:, c * 512:(c + 1) * 512].rearrange("(i p) tc -> p i tc", p=128))
        p.op("tensor", lambda e: e.nop(), wait=[("qs", c_xn)])
        c_lg = tiled_mm(p, m["pb"][2][0:NE1, 0:512],
                        [m["Wg"][:, pi * NE1:(pi + 1) * NE1] for pi in range(nd)],
                        [m["xn_e"][:, pi * 512:(pi + 1) * 512] for pi in range(nd)],
                        wait=[("v", m["lg_free"])])
        m["lg_free"] = p.v(lambda e, c=c: e.tensor_scalar(
            m["logitsT"][0:NE1, c * 512:(c + 1) * 512], m["pb"][2][0:NE1, 0:512],
            m["bg"][0:NE1, 0:1], None, ALU.add), wait=[("t", c_lg)])
        snap = p.barrier()
    NTE = NTT * NE1
    LG = m["gb"][:, 0 * NTE:1 * NTE]
    BI = LG
    M1 = m["gb"][:, 1 * NTE:2 * NTE]
    M2 = m["gb"][:, 2 * NTE:3 * NTE]
    TM = m["gb"][:, 3 * NTE:4 * NTE]
    G1 = m["gb"][:, 4 * NTE:5 * NTE]
    V1 = m["gn"][:, 0:16]
    V2 = m["gn"][:, 16:32]
    P1 = m["gn"][:, 32:48]
    P2 = m["gn"][:, 48:64]
    DF = m["gn"][:, 64:80]
    d3 = lambda ap: ap.rearrange("p (o e) -> p o e", e=NE1)
    bcn = lambda ap: ap.rearrange("p (o e) -> p o e", e=1).broadcast_to((128, NTT, NE1))
    for tt in range(NTT):
        c_tr = ptrans(p, m["pb"][3][:, 0:NE1], m["logitsT"][0:NE1, tt * 128:(tt + 1) * 128],
                      m["ident"], wait=[("v", m["lg_free"]), ("v", m["lg_copy"])])
        m["lg_copy"] = p.v(lambda e, tt=tt: e.tensor_copy(LG[:, tt * NE1:(tt + 1) * NE1],
                                                          m["pb"][3][:, 0:NE1]), wait=[("t", c_tr)])
    c_v1 = p.v(lambda e: e.tensor_reduce(V1, d3(BI), AX.X, ALU.max),
               wait=[("v", m["lg_copy"], True)])
    p.v(lambda e: e.tensor_tensor(d3(M1), d3(BI), bcn(V1), ALU.is_ge), wait=[("v", c_v1, True)])
    p.v(lambda e: e.scalar_tensor_tensor(TM, M1, -1e30, BI, ALU.mult, ALU.add))
    c_v2 = p.v(lambda e: e.tensor_reduce(V2, d3(TM), AX.X, ALU.max))
    p.v(lambda e: e.tensor_tensor(d3(M2), d3(TM), bcn(V2), ALU.is_ge), wait=[("v", c_v2, True)])
    c_df = p.v(lambda e: e.tensor_tensor(DF, V1, V2, ALU.subtract))
    c_p1 = p.s(lambda e: e.activation(P1, DF, AF.Sigmoid), wait=[("v", c_df)])
    c_p2 = p.v(lambda e: e.tensor_scalar(P2, P1, -1.0, 1.0, ALU.mult, ALU.add),
               wait=[("s", c_p1)])
    p.v(lambda e: e.tensor_tensor(d3(G1), d3(M1), bcn(P1), ALU.mult), wait=[("v", c_p2, True)])
    p.v(lambda e: e.tensor_tensor(d3(TM), d3(M2), bcn(P2), ALU.mult))
    c_gw = p.v(lambda e: e.tensor_tensor(m["comb"][:, 0:NTE], G1, TM, ALU.add))
    for tt in range(NTT):
        c_tr2 = ptrans(p, m["pb"][4][0:NE1, 0:128], m["comb"][:, tt * NE1:(tt + 1) * NE1],
                       m["ident"], wait=[("v", c_gw), ("s", m["ct_free"])])
        m["ct_free"] = p.s(lambda e, tt=tt: e.activation(
            m["combT"][0:NE1, tt * 128:(tt + 1) * 128], m["pb"][4][0:NE1, 0:128], AF.Identity),
            wait=[("t", c_tr2)])
    for tt in range(NTT):
        for hh in range(2):
            c_mm = p.t(lambda e, tt=tt, hh=hh: e.matmul(
                m["pb"][tt % 2][:, 0:512], m["combT"][0:NE1, tt * 128:(tt + 1) * 128],
                m["b2all"][0:NE1, hh * 512:(hh + 1) * 512], start=True, stop=True),
                wait=[("v", m["t2_free"][tt % 2]), ("s", m["ct_free"])])
            m["t2_free"][tt % 2] = p.v(lambda e, tt=tt, hh=hh: e.tensor_copy(
                m["Y_acc"][:, tt * D + hh * 512:tt * D + (hh + 1) * 512], m["pb"][tt % 2][:, 0:512]),
                wait=[("t", c_mm)])
    snap = p.barrier()

    first = True
    for e_i in range(NE1):
        for hh in range(2):
            nf = 16
            if first:
                p.wait_snapshot(snap)
                first = False
            c_w1 = p.dma("gpsimd", m["wp1"][:, 0:nd * 2048].rearrange("p (i f) -> p i f", i=nd),
                         m["W1_d"][e_i].rearrange("(i p) f -> p i f", p=128)[:, :, hh * 2048:(hh + 1) * 2048],
                         wait=[("t", m["w1_used"])])
            c_w2 = p.dma("gpsimd", m["wp2"][:, 0:nf * D].rearrange("p (i f) -> p i f", i=nf),
                         m["W2_d"][e_i][hh * 2048:(hh + 1) * 2048, :].rearrange("(i p) f -> p i f", p=128),
                         wait=[("t", m["w2_used"])])
            for c in range(T // TCE):
                c_xn = p.dma("sync", m["xn_e"][:, 0:nd * TCE].rearrange("p (i tc) -> p i tc", i=nd),
                             m["xn1T_s"][:, c * TCE:(c + 1) * TCE].rearrange("(i p) tc -> p i tc", p=128),
                             wait=[("t", m["xn_used"])])
                p.op("tensor", lambda e: e.nop(), wait=[("qs", c_xn), ("qg", c_w2)])
                for po in range(nf):
                    ps = m["pb"][po % 2]
                    c_mm = tiled_mm(p, ps[:, 0:TCE],
                                    [m["wp1"][:, pi * 2048 + po * 128:pi * 2048 + (po + 1) * 128]
                                     for pi in range(nd)],
                                    [m["xn_e"][:, pi * TCE:(pi + 1) * TCE] for pi in range(nd)],
                                    wait=[("s", m["h_free_s"][po % 2]), ("t", m["hg_used"].get(po % 2, 0))])
                    bcol = m["b1c"][:, e_i * 32 + hh * 16 + po:e_i * 32 + hh * 16 + po + 1]
                    m["h_free_s"][po % 2] = p.s(lambda e, po=po, ps=ps, bcol=bcol: e.activation(
                        m["H1"][:, po * TCE:(po + 1) * TCE], ps[:, 0:TCE], AF.Gelu, bias=bcol),
                        wait=[("t", c_mm)])
                c_h = p.n["s"]
                m["xn_used"] = m["w1_used"] = p.n["t"]
                for j in range(TCE // 128):
                    gtile = c * (TCE // 128) + j
                    for half in range(2):
                        ps = m["pb"][2 + (j * 2 + half) % 2]
                        c_mm = tiled_mm(p, ps[:, 0:512],
                                        [m["H1"][:, pi * TCE + j * 128:pi * TCE + (j + 1) * 128]
                                         for pi in range(nf)],
                                        [m["wp2"][:, pi * D + half * 512:pi * D + (half + 1) * 512]
                                         for pi in range(nf)],
                                        wait=[("s", c_h), ("v", m["y_free"][(j * 2 + half) % 2])])
                        dst = m["Y_acc"][:, gtile * D + half * 512:gtile * D + (half + 1) * 512]
                        gcol = m["comb"][:, gtile * NE1 + e_i:gtile * NE1 + e_i + 1]
                        m["y_free"][(j * 2 + half) % 2] = p.v(
                            lambda e, dst=dst, ps=ps, gcol=gcol: e.scalar_tensor_tensor(
                                dst, ps[:, 0:512], gcol, dst, ALU.mult, ALU.add),
                            wait=[("t", c_mm)])
                m["hg_used"][0] = m["hg_used"][1] = m["w2_used"] = p.n["t"]
    snap = p.barrier()

    for tt in range(NTT):
        p.wait_snapshot(snap)
        c_l1 = p.dma("sync", m["xn_e"][:, 0:nd * 128].rearrange("p (i tc) -> p i tc", i=nd),
                     m["xn1T_s"][:, tt * 128:(tt + 1) * 128].rearrange("(i p) tc -> p i tc", p=128),
                     wait=[("v", m["fin_free"])])
        c_l2 = p.dma("sync", m["xa_ch"][:, 0:nd * 128].rearrange("p (i tc) -> p i tc", i=nd),
                     m["xa_s"][:, tt * 128:(tt + 1) * 128].rearrange("(i p) tc -> p i tc", p=128))
        lnin = m["lnin"]
        for dd in range(nd):
            c_tr = ptrans(p, m["pb"][4 + dd % 2][:, 0:128], m["xn_e"][:, dd * 128:(dd + 1) * 128],
                          m["identb"], wait=[("qs", c_l1), ("v", m["trn_free"][dd % 2])])
            m["trn_free"][dd % 2] = p.v(lambda e, dd=dd: e.tensor_tensor(
                lnin[:, dd * 128:(dd + 1) * 128], m["pb"][4 + dd % 2][:, 0:128],
                m["Y_acc"][:, tt * D + dd * 128:(tt) * D + (dd + 1) * 128], ALU.add),
                wait=[("t", c_tr)])
        c_ln = p.n["v"]
        gc = m["gn"][:, 96:128]
        c_s1 = p.v(lambda e: e.reduce_sum(gc[:, 0:1], lnin[:, 0:D], AX.X),
                   wait=[("v", c_ln, True)])
        c_sq = p.s(lambda e: e.activation(m["sqf"][:, 0:D], lnin[:, 0:D], AF.Square),
                   wait=[("v", c_ln)])
        c_s2 = p.v(lambda e: e.reduce_sum(gc[:, 1:2], m["sqf"][:, 0:D], AX.X),
                   wait=[("s", c_sq)])
        p.v(lambda e: e.tensor_scalar(gc[:, 2:3], gc[:, 0:1], -1.0 / D, None, ALU.mult),
            wait=[("v", c_s2, True)])
        p.v(lambda e: e.tensor_scalar(gc[:, 3:4], gc[:, 1:2], 1.0 / D, None, ALU.mult))
        c_mm2 = p.v(lambda e: e.tensor_tensor(gc[:, 4:5], gc[:, 2:3], gc[:, 2:3], ALU.mult))
        p.v(lambda e: e.scalar_tensor_tensor(gc[:, 5:6], gc[:, 4:5], -1.0, gc[:, 3:4],
                                             ALU.mult, ALU.add), wait=[("v", c_mm2, True)])
        c_ve = p.v(lambda e: e.tensor_scalar(gc[:, 6:7], gc[:, 5:6], 1e-5, None, ALU.add),
                   wait=[("v", p.n["v"], True)])
        p.s(lambda e: e.activation(gc[:, 7:8], gc[:, 6:7], AF.Ln), wait=[("v", c_ve)])
        c_rs = p.s(lambda e: e.activation(gc[:, 8:9], gc[:, 7:8], AF.Exp, scale=-0.5))
        c_app = p.v(lambda e: e.tensor_scalar(lnin[:, 0:D], lnin[:, 0:D], gc[:, 2:3], gc[:, 8:9],
                                              ALU.add, ALU.mult), wait=[("s", c_rs)])
        p.v(lambda e: e.tensor_tensor(lnin[:, 0:D], lnin[:, 0:D], m["gamma_rep"][:, 0:D], ALU.mult))
        c_gb = p.v(lambda e: e.tensor_tensor(lnin[:, 0:D], lnin[:, 0:D], m["beta_rep"][:, 0:D], ALU.add))
        for dd in range(nd):
            c_tr = ptrans(p, m["pb"][4 + dd % 2][:, 0:128], m["xa_ch"][:, dd * 128:(dd + 1) * 128],
                          m["ident"], wait=[("qs", c_l2), ("v", m["trn_free"][dd % 2]), ("v", c_gb)])
            m["trn_free"][dd % 2] = p.v(lambda e, dd=dd: e.tensor_tensor(
                m["out_ch"][:, dd * 128:(dd + 1) * 128], m["pb"][4 + dd % 2][:, 0:128],
                lnin[:, dd * 128:(dd + 1) * 128], ALU.add),
                wait=[("t", c_tr), ("qa", m["out_paged"])])
        m["fin_free"] = p.n["v"]
        m["out_paged"] = p.dma("scalar", m["out_s"][tt * 128:(tt + 1) * 128, :],
                               m["out_ch"][:, 0:D], wait=[("v", m["fin_free"])])
        snap = p.barrier()
    p.op("sync", lambda e: e.nop(), wait=[("qa", p.n["qa"]), ("qs", p.n["qs"]), ("qg", p.n["qg"])])


def s5_preamble(p, m):
    nd = D // 128
    p.dma("gpsimd", m["Wg"][:, 0:nd * NE1].rearrange("p (i f) -> p i f", i=nd),
          m["Wg_d"].rearrange("(i p) f -> p i f", p=128))
    p.dma("gpsimd", m["identb"][:], m["ident_d"])
    p.dma("sync", m["ident"][:], m["ident_d"])
    p.dma("sync", m["bg"][0:NE1, 0:1], m["bg_d"])
    p.dma("sync", m["b2all"][0:NE1, :], m["b2all_d"])
    p.dma("sync", m["b1c"][:], m["b1c_d"])
    p.dma("sync", m["gamma_rep"][:], m["gamma_d"])
    p.dma("sync", m["beta_rep"][:], m["beta_d"])
    for k in ["lg_free", "lg_copy", "ct_free", "xn_used", "fin_free", "out_paged", "w1_used", "w2_used"]:
        m[k] = 0
    m["t2_free"] = [0, 0]
    m["h_free_s"] = [0, 0]
    m["y_free"] = [0, 0]
    m["trn_free"] = [0, 0]
    m["hg_used"] = {}


def build_s5():
    nc = _new_nc()
    nd = D // 128
    NTT = T // 128
    m = {}
    m["xn1T_s"] = nc.dram_tensor("xn1T_s", [D, T], BF16, kind="ExternalInput").ap()
    m["xa_s"] = nc.dram_tensor("xa_s", [D, T], F32, kind="ExternalInput").ap()
    m["Wg_d"] = nc.dram_tensor("Wg", [D, NE1], F32, kind="ExternalInput").ap()
    m["bg_d"] = nc.dram_tensor("bg", [NE1, 1], F32, kind="ExternalInput").ap()
    m["W1_d"] = [nc.dram_tensor(f"W1_{e}", [D, DFF1], F32, kind="ExternalInput").ap() for e in range(NE1)]
    m["W2_d"] = [nc.dram_tensor(f"W2_{e}", [DFF1, D], F32, kind="ExternalInput").ap() for e in range(NE1)]
    m["b1c_d"] = nc.dram_tensor("b1c", [128, NE1 * 32], F32, kind="ExternalInput").ap()
    m["b2all_d"] = nc.dram_tensor("b2all", [NE1, D], F32, kind="ExternalInput").ap()
    m["gamma_d"] = nc.dram_tensor("gammar", [128, D], F32, kind="ExternalInput").ap()
    m["beta_d"] = nc.dram_tensor("betar", [128, D], F32, kind="ExternalInput").ap()
    m["ident_d"] = nc.dram_tensor("ident", [128, 128], F32, kind="ExternalInput").ap()
    m["out_s"] = nc.dram_tensor("out_s", [T, D], F32, kind="ExternalOutput").ap()

    es = contextlib.ExitStack()
    with es:
        sb = lambda name, shape, dt: es.enter_context(nc.sbuf_tensor(name, shape, dt))
        m["Wg"] = sb("Wg_sb", [128, nd * NE1], BF16)
        m["identb"] = sb("identb", [128, 128], BF16)
        m["ident"] = sb("ident_sb", [128, 128], F32)
        m["bg"] = sb("bg_sb", [NE1, 1], F32)
        m["b2all"] = sb("b2all_sb", [NE1, D], F32)
        m["b1c"] = sb("b1c_sb", [128, NE1 * 32], F32)
        m["gamma_rep"] = sb("gamma_rep", [128, D], F32)
        m["beta_rep"] = sb("beta_rep", [128, D], F32)
        m["logitsT"] = sb("logitsT", [NE1, T], F32)
        m["gb"] = sb("gb", [128, 5 * NTT * NE1], F32)
        m["gn"] = sb("gn", [128, 128], F32)
        m["comb"] = sb("comb", [128, NTT * NE1], F32)
        m["combT"] = sb("combT", [NE1, T], F32)
        m["Y_acc"] = sb("Y_acc", [128, NTT * D], F32)
        m["xn_e"] = sb("xn_e", [128, nd * 512], BF16)
        m["wp1"] = sb("wp1", [128, nd * 2048], BF16)
        m["wp2"] = sb("wp2", [128, 16 * D], BF16)
        m["H1"] = sb("H1", [128, 16 * 256], BF16)
        m["xa_ch"] = sb("xa_ch", [128, nd * 128], F32)
        m["lnin"] = sb("lnin", [128, D], F32)
        m["sqf"] = sb("sqf", [128, D], F32)
        m["out_ch"] = sb("out_ch", [128, D], F32)
        m["pb"] = [es.enter_context(nc.psum_tensor(f"pb{i}", [128, 512], F32)) for i in range(6)]

        sem_names = ["t", "v", "s", "g", "qg", "qs", "qa", "y"]
        sems = {k: es.enter_context(nc.semaphore(k)) for k in sem_names}

        def describe(p):
            s5_preamble(p, m)
            s5_body(p, m)

        with nc.Block() as block:
            for en in ENGINES:
                def mk(en):
                    def section(eng):
                        describe(Prog(nc, sems, active=en, eng=eng))
                    return section
                getattr(block, en)(mk(en))
    return nc


def s5_inmap(inputs, s4_out):
    b1 = np.asarray(inputs["L1_ffn_b1"], np.float32)
    im = dict(
        xn1T_s=s4_out["xn1T_s"],
        xa_s=s4_out["xa_s"],
        Wg=np.ascontiguousarray(np.asarray(inputs["L1_ffn_Wg"], np.float32)),
        bg=np.asarray(inputs["L1_ffn_bg"], np.float32).reshape(NE1, 1),
        b1c=np.ascontiguousarray(b1.reshape(NE1, 32, 128).transpose(2, 0, 1).reshape(128, NE1 * 32)),
        b2all=np.ascontiguousarray(np.asarray(inputs["L1_ffn_b2"], np.float32)),
        gammar=np.ascontiguousarray(np.broadcast_to(
            np.asarray(inputs["L1_ffn_gamma"], np.float32)[None, :], (128, D))),
        betar=np.ascontiguousarray(np.broadcast_to(
            np.asarray(inputs["L1_ffn_beta"], np.float32)[None, :], (128, D))),
        ident=np.eye(128, dtype=np.float32),
    )
    for e in range(NE1):
        im[f"W1_{e}"] = np.ascontiguousarray(np.asarray(inputs["L1_ffn_W1"], np.float32)[e])
        im[f"W2_{e}"] = np.ascontiguousarray(np.asarray(inputs["L1_ffn_W2"], np.float32)[e])
    return im


def kernel_multi(**inputs):
    n = 8
    x = np.asarray(inputs["L0_attn_x"], np.float32)
    xTs = [np.ascontiguousarray(x[b].T) for b in range(n)]
    cores = list(range(n))

    im1 = [s1_inmap(inputs, b) for b in range(n)]
    r1 = run_bass_kernel_spmd(build_s1(), im1, core_ids=cores).results
    r1 = [{k: np.asarray(v) for k, v in r.items()} for r in r1]

    im2 = [s2_inmap(inputs, r1[b]) for b in range(n)]
    r2 = run_bass_kernel_spmd(build_s2(), im2, core_ids=cores).results
    r2 = [{k: np.asarray(v) for k, v in r.items()} for r in r2]

    im3 = [s3_inmap(inputs, xTs[b], r1[b], r2[b]) for b in range(n)]
    r3 = run_bass_kernel_spmd(build_s3(), im3, core_ids=cores).results
    r3 = [{k: np.asarray(v) for k, v in r.items()} for r in r3]

    im4 = [s4_inmap(inputs, r3[b]["x1_s"]) for b in range(n)]
    r4 = run_bass_kernel_spmd(build_s4(), im4, core_ids=cores).results
    r4 = [{k: np.asarray(v) for k, v in r.items()} for r in r4]

    im5 = [s5_inmap(inputs, r4[b]) for b in range(n)]
    r5 = run_bass_kernel_spmd(build_s5(), im5, core_ids=cores).results

    return np.stack([np.asarray(r5[b]["out_s"]) for b in range(n)], axis=0)


FUSED_DEBUG = False


class PoolAlloc:
    def __init__(self, fpool, bpool, fcap, bcap):
        self.fp, self.bp = fpool, bpool
        self.fcap, self.bcap = fcap, bcap
        self.reset()

    def reset(self):
        self.f = 0
        self.b = 0

    def t(self, name, shape, dt):
        w = shape[-1] if len(shape) == 2 else shape[0]
        p_ = shape[0] if len(shape) == 2 else 1
        w = (w + 15) // 16 * 16
        if dt == F32:
            assert self.f + w <= self.fcap, f"f32 pool overflow at {name}: {self.f + w}"
            ap = self.fp[0:p_, self.f:self.f + shape[-1]]
            self.f += w
        else:
            assert self.b + w <= self.bcap, f"b16 pool overflow at {name}: {self.b + w}"
            ap = self.bp[0:p_, self.b:self.b + shape[-1]]
            self.b += w
        return ap


def build_fused():
    nc = _new_nc()
    nd = D // 128
    ndi = DI // 128
    NE = 16
    NTT = T // 128
    dram = lambda name, shape, dt, kind: nc.dram_tensor(name, shape, dt, kind=kind).ap()
    EI, IN, EO = "ExternalInput", "Internal", "ExternalOutput"

    g = {}
    g["x"] = dram("x", [D, T], F32, EI)
    g["w_in_d"] = dram("w_in", [D, 4096], F32, EI)
    g["lng_d"] = dram("lng", [128, nd], F32, EI)
    g["lnb_d"] = dram("lnb", [128, nd], F32, EI)
    g["convw_d"] = dram("convw", [128, 4 * ndi], F32, EI)
    g["convb_d"] = dram("convb", [128, ndi], F32, EI)
    g["w_dt_d"] = dram("w_dt", [DI, DI], F32, EI)
    g["w_xp_d"] = dram("w_xp", [DI, 32], F32, EI)
    g["bdt_d"] = dram("bdt", [128, ndi], F32, EI)
    g["Alog_d"] = dram("Alog", [128, ndi * DS], F32, EI)
    g["D_d"] = dram("Dv", [128, ndi], F32, EI)
    g["sel32_d"] = dram("sel32", [32, 32 * 128], F32, EI)
    g["w_out_d"] = dram("w_out", [DI, D], F32, EI)
    g["gate_wT_d"] = dram("gate_wT", [D, NE], F32, EI)
    g["ebias_d"] = dram("ebias", [128, NE], F32, EI)
    g["ident_d"] = dram("ident", [128, 128], F32, EI)
    g["b2aug_d"] = dram("b2aug", [17, D], F32, EI)
    g["b1_all_d"] = dram("b1_all", [128, NE * 4], F32, EI)
    g["b3_all_d"] = dram("b3_all", [128, NE * 4], F32, EI)
    g["sb1_d"] = dram("sb1", [128, 8], F32, EI)
    g["sb3_d"] = dram("sb3", [128, 8], F32, EI)
    g["w1T_d"] = [dram(f"w1T_{e}", [D, DFF0], F32, EI) for e in range(NE)]
    g["w3T_d"] = [dram(f"w3T_{e}", [D, DFF0], F32, EI) for e in range(NE)]
    g["w2T_d"] = [dram(f"w2T_{e}", [DFF0, D], F32, EI) for e in range(NE)]
    g["sw1T_d"] = dram("sw1T", [D, FDFF0], F32, EI)
    g["sw3T_d"] = dram("sw3T", [D, FDFF0], F32, EI)
    g["sw2T_d"] = dram("sw2T", [FDFF0, D], F32, EI)
    g["w_proj_d"] = dram("w_proj", [D, D], F32, EI)
    g["wqkv_hd"] = [dram(f"wqkv_{h}", [D, 3 * HD], F32, EI) for h in range(NH)]
    g["newc_d"] = dram("newc", [NLM, 3 * NLM], F32, EI)
    g["Wg_d"] = dram("Wg", [D, NE1], F32, EI)
    g["bg_d"] = dram("bg", [NE1, 1], F32, EI)
    g["W1_d"] = [dram(f"W1_{e}", [D, DFF1], F32, EI) for e in range(NE1)]
    g["W2_d"] = [dram(f"W2_{e}", [DFF1, D], F32, EI) for e in range(NE1)]
    g["b1c_d"] = dram("b1c", [128, NE1 * 32], F32, EI)
    g["b2all_d"] = dram("b2all", [NE1, D], F32, EI)
    g["gamma_d"] = dram("gammar", [128, D], F32, EI)
    g["beta_d"] = dram("betar", [128, D], F32, EI)
    HK = EO if FUSED_DEBUG else IN
    g["xn_s"] = dram("xn_s", [D, T], BF16, HK)
    g["xc_s"] = dram("xc_s", [DI, T], BF16, HK)
    g["z_s"] = dram("z_s", [DI, T], BF16, HK)
    g["yg_s"] = dram("yg_s", [DI, T], BF16, HK)
    g["x1p_s"] = dram("x1p_s", [D, T], F32, HK)
    g["x1_s"] = dram("x1_s", [D, T], F32, HK)
    g["xn1T_s"] = dram("xn1T_s", [D, T], BF16, HK)
    g["xa_s"] = dram("xa_s", [D, T], F32, HK)
    g["out_s"] = dram("out_s", [T, D], F32, EO)

    FCAP = 29400
    BCAP = 46800
    es = contextlib.ExitStack()
    with es:
        sbt = lambda name, shape, dt: es.enter_context(nc.sbuf_tensor(name, shape, dt))
        fpool = sbt("fpool", [128, FCAP], F32)
        bpool = sbt("bpool", [128, BCAP], BF16)
        g["ident"] = sbt("ident_sb", [128, 128], F32)
        g["identb"] = sbt("identb_sb", [128, 128], BF16)
        g["onescol"] = sbt("onescol", [128, 1], F32)
        g["onescol_b"] = sbt("onescol_b", [128, 1], BF16)
        g["onesrow"] = sbt("onesrow", [1, 128], F32)
        pb = [es.enter_context(nc.psum_tensor(f"pb{i}", [128, 512], F32)) for i in range(7)]
        al = PoolAlloc(fpool, bpool, FCAP, BCAP)

        def mk_m1():
            al.reset()
            m = dict(g)
            TC1 = 128
            m["TC1"] = TC1
            m["w_in"] = al.t("w_in", [128, nd * 4096], BF16)
            m["x_ch2"] = [al.t(f"x_ch{i}", [128, nd * TC1], F32) for i in range(2)]
            m["xn_ch2"] = [al.t(f"xn_ch{i}", [128, nd * TC1], BF16) for i in range(2)]
            m["xln_ch"] = al.t("xln_ch", [128, nd * TC1], BF16)
            m["sq"] = al.t("sq", [128, nd * TC1], F32)
            m["braw_ch"] = [al.t(f"braw{i}", [128, ndi * (TC1 + 3)], BF16) for i in range(2)]
            m["z_ch"] = al.t("z_ch", [128, ndi * TC1], BF16)
            m["xc_ch"] = al.t("xc_ch", [128, ndi * TC1], BF16)
            m["cv"] = al.t("cv", [128, 2 * TC1], F32)
            m["row0"] = al.t("row0", [1, TC1], F32)
            m["row1"] = al.t("row1", [1, TC1], F32)
            m["row2"] = al.t("row2", [1, TC1], F32)
            m["mu_rep"] = al.t("mu_rep", [128, 2 * TC1], F32)
            m["lng"] = al.t("lng_sb", [128, nd], F32)
            m["lnb"] = al.t("lnb_sb", [128, nd], F32)
            m["convw"] = al.t("convw_sb", [128, 4 * ndi], F32)
            m["convb"] = al.t("convb_sb", [128, ndi], F32)
            m["ps_r0"], m["ps_r1"], m["ps_rep"], m["ps_rep2"] = pb[2], pb[3], pb[4], pb[5]
            m["ps_mm"] = [pb[0], pb[1]]
            m["s1_ps_free"] = [0, 0]
            return m

        def mk_m2():
            al.reset()
            m = dict(g)
            m["w_dt"] = al.t("w_dt", [128, ndi * DI], BF16)
            m["w_xp"] = al.t("w_xp", [128, ndi * 32], BF16)
            m["bdt"] = al.t("bdt", [128, ndi], F32)
            m["Aneg"] = al.t("Aneg", [128, ndi * DS], F32)
            m["Dcol"] = al.t("Dcol", [128, ndi], F32)
            m["sel32"] = al.t("sel32", [32, 32 * 128], F32)
            m["xc_ch2"] = [al.t(f"xc_ch{i}", [128, ndi * 256], BF16) for i in range(2)]
            m["dt_ch2"] = [al.t(f"dt_ch{i}", [128, ndi * 256], F32) for i in range(2)]
            a1 = al.t("a_ch", [128, DS * 256], F32)
            m["a_ch2"] = [a1, a1]
            m["h_g"] = al.t("h_g", [128, DS * 256], F32)
            m["u_g"] = al.t("u_g", [128, 256], F32)
            m["y_g"] = al.t("y_g", [128, 768], F32)

            m["scr"] = al.t("scr", [128, 768], F32)
            m["B_rep"] = al.t("B_rep", [128, DS * 256], F32)
            m["C_rep"] = al.t("C_rep", [128, DS * 256], BF16)
            m["xdbl"] = al.t("xdbl", [32, 256], F32)
            m["h_state"] = al.t("h_state", [128, ndi * DS], F32)
            m["yg_ch"] = al.t("yg_ch", [128, 2 * 256], BF16)
            m["pb"] = pb[0:5]
            return m

        def mk_m3():
            al.reset()
            m = dict(g)
            pool_a = al.t("pool_a", [128, ndi * D], BF16)
            m["w_out"] = pool_a[:, 0:ndi * D]
            m["wp1"] = pool_a[:, 0:nd * FDFF0]
            m["wp3"] = pool_a[:, nd * FDFF0:2 * nd * FDFF0]
            m["gate_wT"] = al.t("gate_wT", [128, nd * NE], BF16)
            m["ebias_rep"] = al.t("ebias_rep", [128, NE], F32)
            m["b2aug"] = al.t("b2aug", [17, D], F32)
            m["b1_all"] = al.t("b1_all", [128, NE * 4], F32)
            m["b3_all"] = al.t("b3_all", [128, NE * 4], F32)
            m["sb1"] = al.t("sb1", [128, 8], F32)
            m["sb3"] = al.t("sb3", [128, 8], F32)
            m["x_ch"] = al.t("x_ch", [128, nd * 256], F32)
            m["xn_ch"] = al.t("xn_ch", [128, nd * 256], BF16)
            pool_b = al.t("pool_b", [128, ndi * 256], BF16)
            m["yg_ch"] = pool_b[:, 0:ndi * 256]
            m["xn_e"] = pool_b[:, 0:nd * 512]
            m["x1_ch"] = al.t("x1_ch", [128, nd * 256], F32)
            m["sz_g"] = al.t("sz_g3", [128, 512], F32)
            m["scr"] = al.t("scr", [128, 512], F32)
            m["logitsT"] = al.t("logitsT", [16, T], F32)
            m["gatew"] = al.t("gatew", [128, NTT * NE], F32)
            m["gatewT"] = al.t("gatewT", [17, T], F32)
            m["gb"] = al.t("gb", [128, 7 * NTT * NE], F32)
            m["gn"] = al.t("gn", [128, 160], F32)
            m["Y_acc"] = al.t("Y_acc", [128, NTT * D], F32)
            m["wp2"] = al.t("wp2", [128, (FDFF0 // 128) * D], BF16)
            m["H1"] = al.t("H1", [128, 8 * 512], BF16)
            m["H3"] = al.t("H3", [128, 8 * 512], BF16)
            m["Hg"] = al.t("Hg", [128, 8 * 512], BF16)
            m["z_ch"] = m["H1"]
            m["pb"] = pb
            return m

        def mk_m4():
            al.reset()
            m = dict(g)
            m["defer_wproj"] = True
            lane_qk = []
            m["lane"] = []
            for ln in range(2):
                B = {}
                B["wqkv_h"] = al.t(f"wqkv_h{ln}", [128, nd * 3 * HD], BF16)
                B["QT"] = al.t(f"QT{ln}", [NLM, T], BF16)
                B["KT"] = al.t(f"KT{ln}", [NLM, T], BF16)
                B["Vn"] = al.t(f"Vn{ln}", [128, NTT * HD], BF16)
                B["lmf"] = al.t(f"lmf{ln}", [NLM, 2 * NLM], F32)
                B["klm"] = al.t(f"klm{ln}", [NLM, NLM], BF16)
                B["qlm"] = al.t(f"qlm{ln}", [NLM, NLM], BF16)
                for nm in ["K2", "K2n", "K2T", "NV0", "NV1", "KV", "KVT", "NM", "NM2", "NVT"]:
                    B[nm] = al.t(nm + str(ln), [NLM, NLM], F32)
                B["nsc"] = al.t(f"nsc{ln}", [NLM, 16], F32)
                B["nrow"] = al.t(f"nrow{ln}", [1, NLM + 4], F32)
                B["K3"] = al.t(f"K3_{ln}", [NLM, T], F32)
                kb = al.t(f"k3k1_{ln}", [NLM, T], BF16)
                B["K3b"] = kb[0:NLM, 0:T]
                B["K1T"] = kb[0:NLM, 0:T]
                lane_qk.append(kb)
                B["K3T"] = al.t(f"K3T{ln}", [128, NTT * NLM], BF16)
                B["K3V"] = al.t(f"K3V{ln}", [NLM, HD], F32)
                B["Wh"] = al.t(f"Wh{ln}", [NLM, HD], BF16)
                B["K1t"] = al.t(f"K1t{ln}", [128, 2 * NLM], F32)
                B["K1n"] = al.t(f"K1n{ln}", [128, 2 * NLM], BF16)
                B["k1sc"] = al.t(f"k1sc{ln}", [128, 16], F32)
                m["lane"].append(B)
            wp = al.t("wp_region", [128, 0], BF16)
            m["w_proj"] = bpool[:, 0:nd * D]
            m["I7"] = al.t("I7", [NLM, NLM], F32)
            m["I15"] = al.t("I15", [NLM, NLM], F32)
            m["I13"] = al.t("I13", [NLM, NLM], F32)
            m["SVT"] = al.t("SVT", [128, nd * T], BF16)
            m["xn_st"] = [al.t(f"xn_st{i}", [128, nd * 512], BF16) for i in range(2)]
            m["xnl_ch"] = al.t("xnl_ch", [128, nd * 256], BF16)
            m["x_ch"] = al.t("x_ch", [128, nd * 256], F32)
            m["xa_ch"] = al.t("xa_ch", [128, nd * 256], F32)
            m["sq"] = al.t("sq", [128, nd * 256], F32)
            m["row0"] = al.t("row0", [1, 256], F32)
            m["row1"] = al.t("row1", [1, 256], F32)
            m["row2"] = al.t("row2", [1, 256], F32)
            m["mu_rep"] = al.t("mu_rep", [128, 512], F32)
            m["pb"] = pb
            m["ps_r0"], m["ps_r1"], m["ps_rep"], m["ps_rep2"] = pb[3], pb[4], pb[5], pb[6]
            return m

        def mk_m5():
            al.reset()
            m = dict(g)
            m["Wg"] = al.t("Wg", [128, nd * NE1], BF16)
            m["bg"] = al.t("bg", [NE1, 1], F32)
            m["b2all"] = al.t("b2all", [NE1, D], F32)
            m["b1c"] = al.t("b1c", [128, NE1 * 32], F32)
            m["gamma_rep"] = al.t("gamma_rep", [128, D], F32)
            m["beta_rep"] = al.t("beta_rep", [128, D], F32)
            m["logitsT"] = al.t("logitsT", [NE1, T], F32)
            m["gb"] = al.t("gb", [128, 5 * NTT * NE1], F32)
            m["gn"] = al.t("gn", [128, 128], F32)
            m["comb"] = al.t("comb", [128, NTT * NE1], F32)
            m["combT"] = al.t("combT", [NE1, T], F32)
            m["Y_acc"] = al.t("Y_acc", [128, NTT * D], F32)
            m["xn_e"] = al.t("xn_e", [128, nd * 512], BF16)
            m["wp1"] = al.t("wp1", [128, nd * 2048], BF16)
            m["wp2"] = al.t("wp2", [128, 16 * D], BF16)
            m["H1"] = al.t("H1", [128, 16 * 256], BF16)
            m["xa_ch"] = al.t("xa_ch", [128, nd * 128], F32)
            m["lnin"] = al.t("lnin", [128, D], F32)
            m["sqf"] = al.t("sqf", [128, D], F32)
            m["out_ch"] = al.t("out_ch", [128, D], F32)
            m["pb"] = pb[0:6]
            return m

        sem_names = ["t", "v", "s", "g", "qg", "qs", "qa", "y"]
        sems = {k: es.enter_context(nc.semaphore(k)) for k in sem_names}

        def describe(p):
            p.dma("sync", g["ident"][:], g["ident_d"])
            p.dma("gpsimd", g["identb"][:], g["ident_d"])
            p.v(lambda e: e.memset(g["onescol"][:], 1.0))
            p.v(lambda e: e.memset(g["onescol_b"][:], 1.0))
            p.v(lambda e: e.memset(g["onesrow"][:], 1.0))
            def stage_gap():
                sg = p.barrier()
                p.wait_snapshot(sg)
            m1 = mk_m1()
            s1_preamble(p, m1)
            s1_body(p, m1)
            stage_gap()
            m2 = mk_m2()
            s2_preamble(p, m2)
            s2_body(p, m2)
            stage_gap()
            m3 = mk_m3()
            s3_preamble(p, m3)
            s3_body(p, m3)
            stage_gap()
            m4 = mk_m4()
            s4_preamble(p, m4)
            s4_body(p, m4)
            stage_gap()
            m5 = mk_m5()
            s5_preamble(p, m5)
            s5_body(p, m5)

        with nc.Block() as block:
            for en in ENGINES:
                def mk(en):
                    def section(eng):
                        describe(Prog(nc, sems, active=en, eng=eng))
                    return section
                getattr(block, en)(mk(en))
    return nc


def fused_inmap(inputs, b, xT):
    im = {}
    im.update(s1_inmap(inputs, b))
    im.update(s2_inmap(inputs, {"xc_s": None, "z_s": None}))
    im.update(s3_inmap(inputs, xT, {"xn_s": None, "z_s": None}, {"yg_s": None}))
    im.update(s4_inmap(inputs, None))
    im.update(s5_inmap(inputs, {"xn1T_s": None, "xa_s": None}))
    for k in ["xc_s", "z_s", "xn_s", "yg_s", "x1_s", "xn1T_s", "xa_s"]:
        im.pop(k, None)
    im["x"] = xT
    return im


def kernel_fused(**inputs):
    n = 8
    x = np.asarray(inputs["L0_attn_x"], np.float32)
    cores = list(range(n))
    nc = build_fused()
    im = [fused_inmap(inputs, b, np.ascontiguousarray(x[b].T)) for b in range(n)]
    res = run_bass_kernel_spmd(nc, im, core_ids=cores)
    return np.stack([np.asarray(res.results[b]["out_s"]) for b in range(n)], axis=0)


def kernel(**inputs):
    return kernel_fused(**inputs)


# version 20
# speedup vs baseline: 1.0182x; 1.0182x over previous
import numpy as np
import contextlib
import ml_dtypes
import concourse.bass as bass
import concourse.mybir as mybir
from concourse.bass_utils import run_bass_kernel_spmd

F32 = mybir.dt.float32
BF16 = mybir.dt.bfloat16
AF = mybir.ActivationFunctionType
ALU = mybir.AluOpType
AX = mybir.AxisListType

T = 2048
D = 1024
DI = 2048
DS = 16
DFF0 = 512
FDFF0 = 1024
LN_EPS = 1e-5

ENGINES = ("tensor", "vector", "scalar", "gpsimd", "sync")


class Prog:

    def __init__(self, nc, sems, active=None, eng=None):
        self.nc = nc
        self.sems = sems
        self.active = active
        self.eng = eng
        self.n = {k: 0 for k in sems}
        self.hw = {k: 0 for k in sems}

    def _waits(self, wait):
        for w in wait:
            if len(w) == 3:
                s, c, force = w
            else:
                (s, c), force = w, False
            if c > self.hw[s] or (force and c > 0):
                self.eng.wait_ge(self.sems[s], c)
                self.hw[s] = max(self.hw[s], c)

    def op(self, engine, fn, wait=(), inc=None, amt=1):
        if self.active == engine:
            self._waits(wait)
            instr = fn(self.eng)
            if inc is not None:
                instr.then_inc(self.sems[inc], amt)
        if inc is not None:
            self.n[inc] += amt
            if self.active == engine and amt == 1:
                self.hw[inc] = max(self.hw[inc], self.n[inc])
        return self.n[inc] if inc is not None else None

    def t(self, fn, wait=(), inc="t"):
        return self.op("tensor", fn, wait, inc)

    def v(self, fn, wait=(), inc="v"):
        return self.op("vector", fn, wait, inc)

    def s(self, fn, wait=(), inc="s"):
        return self.op("scalar", fn, wait, inc)

    def g(self, fn, wait=(), inc="g"):
        return self.op("gpsimd", fn, wait, inc)

    DMA_SEM = {"sync": "qs", "scalar": "qa", "gpsimd": "qg"}

    def dma(self, queue, out, in_, wait=()):
        return self.op(queue, lambda e: e.dma_start(out=out, in_=in_), wait,
                       inc=self.DMA_SEM[queue], amt=16)

    def barrier(self):
        return dict(self.n)

    def wait_snapshot(self, snap, engines=ENGINES, sems=("t", "v", "s", "g", "y", "qg", "qs", "qa")):
        w = [(sm, snap[sm]) for sm in sems]
        for en in engines:
            if self.active == en:
                self._waits(w)


def ptrans(p, out_psum, in_ap, ident, wait=(), inc="t"):
    npart = in_ap.shape[0]
    return p.t(lambda e: e.matmul(out_psum, in_ap, ident[0:npart, 0:npart],
                                  start=True, stop=True), wait=wait, inc=inc)


def tiled_mm(p, psum, lhsT_tiles, rhs_tiles, wait=(), inc="t"):
    n = len(lhsT_tiles)
    last = None
    for i, (lt, rt) in enumerate(zip(lhsT_tiles, rhs_tiles)):
        last = p.t(lambda e, lt=lt, rt=rt, i=i: e.matmul(
            psum, lt, rt, start=(i == 0), stop=(i == n - 1)),
            wait=(wait if i == 0 else ()), inc=(inc if i == n - 1 else None))
    return last


def ln_stats(p, m, TC, tiles_ap, ntiles, onescol, sq, bf16_ones=False):
    r0, r1, t2 = m["row0"], m["row1"], m["row2"]
    c_sq = 0
    for i in range(ntiles):
        c_sq = p.s(lambda e, i=i: e.activation(sq[:, i * TC:(i + 1) * TC], tiles_ap[i], AF.Square))
    c_mm1 = tiled_mm(p, m["ps_r0"][0:1, 0:TC], [onescol] * ntiles, tiles_ap)
    c_mm2 = tiled_mm(p, m["ps_r1"][0:1, 0:TC], [m["onescol"][:, 0:1]] * ntiles,
                     [sq[:, i * TC:(i + 1) * TC] for i in range(ntiles)],
                     wait=[("s", c_sq)])
    nfeat = ntiles * 128
    p.v(lambda e: e.tensor_scalar(r0[0:1, 0:TC], m["ps_r0"][0:1, 0:TC], -1.0 / nfeat, None, ALU.mult),
        wait=[("t", c_mm2)])
    p.v(lambda e: e.tensor_scalar(r1[0:1, 0:TC], m["ps_r1"][0:1, 0:TC], 1.0 / nfeat, None, ALU.mult),
        wait=[("t", c_mm2)])
    p.v(lambda e: e.tensor_tensor(t2[0:1, 0:TC], r0[0:1, 0:TC], r0[0:1, 0:TC], ALU.mult))
    p.v(lambda e: e.tensor_scalar(t2[0:1, 0:TC], t2[0:1, 0:TC], -1.0, LN_EPS, ALU.mult, ALU.add))
    c_var = p.v(lambda e: e.tensor_tensor(t2[0:1, 0:TC], r1[0:1, 0:TC], t2[0:1, 0:TC], ALU.add))
    p.s(lambda e: e.activation(t2[0:1, 0:TC], t2[0:1, 0:TC], AF.Ln), wait=[("v", c_var)])
    c_rstd = p.s(lambda e: e.activation(t2[0:1, 0:TC], t2[0:1, 0:TC], AF.Exp, scale=-0.5))
    c_rep0 = p.t(lambda e: e.matmul(m["ps_rep"][:, 0:TC], m["onesrow"][0:1, :], r0[0:1, 0:TC],
                                    start=True, stop=True), wait=[("v", c_var)])
    c_rep1 = p.t(lambda e: e.matmul(m["ps_rep2"][:, 0:TC], m["onesrow"][0:1, :], t2[0:1, 0:TC],
                                    start=True, stop=True), wait=[("s", c_rstd)])
    p.v(lambda e: e.tensor_copy(m["mu_rep"][:, 0:TC], m["ps_rep"][:, 0:TC]), wait=[("t", c_rep1)])
    c_reps = p.v(lambda e: e.tensor_copy(m["mu_rep"][:, TC:2 * TC], m["ps_rep2"][:, 0:TC]),
                 wait=[("t", c_rep1)])
    return c_reps



S1_NCH = T // 256
S1_UPTO = "all"


def s1_body(p, m):
    TC = m.get("TC1", 256)
    NCH = T // TC
    nd = D // 128
    ndi = DI // 128
    snap = p.barrier()
    p.wait_snapshot(snap)
    prev_braw = None
    H = dict(x_done={}, xn_done={}, sq_read={}, xln_read={}, zpg={}, xcpg={}, xnpg={})
    for c in range(NCH):
        par = c % 2
        x_t = m["x_ch2"][par]
        xn_t = m["xn_ch2"][par]
        w = H["x_done"].get(c - 2, None)
        c_x = p.dma("sync", x_t[:, 0:nd * TC].rearrange("p (i tc) -> p i tc", i=nd),
                    m["x"][:, c * TC:(c + 1) * TC].rearrange("(i p) tc -> p i tc", p=128),
                    wait=(w if w else ()))
        x_tiles = [x_t[:, i * TC:(i + 1) * TC] for i in range(nd)]
        p.op("scalar", lambda e: e.nop(), wait=[("qs", c_x)] + H["sq_read"].get(c - 1, []))
        p.op("tensor", lambda e: e.nop(), wait=[("qs", c_x)])
        c_reps = ln_stats(p, m, TC, x_tiles, nd, m["onescol"][:, 0:1], m["sq"])
        c_xn_v = 0
        wxn = H["xn_done"].get(c - 2, [])
        for i in range(nd):
            p.v(lambda e, i=i: e.tensor_tensor(m["sq"][:, i * TC:(i + 1) * TC],
                                               x_t[:, i * TC:(i + 1) * TC], m["mu_rep"][:, 0:TC], ALU.add),
                wait=[("v", c_reps)] + (wxn if i == 0 else []))
            c_xn_v = p.v(lambda e, i=i: e.tensor_tensor(xn_t[:, i * TC:(i + 1) * TC],
                                                        m["sq"][:, i * TC:(i + 1) * TC],
                                                        m["mu_rep"][:, TC:2 * TC], ALU.mult))
        c_xnpg = p.dma("scalar", m["xn_s"][:, c * TC:(c + 1) * TC].rearrange("(i p) tc -> p i tc", p=128),
                       xn_t[:, 0:nd * TC].rearrange("p (i tc) -> p i tc", i=nd), wait=[("v", c_xn_v)])

        xn_tiles = [xn_t[:, i * TC:(i + 1) * TC] for i in range(nd)]
        p.op("scalar", lambda e: e.nop(), wait=[("v", c_xn_v)])
        p.op("tensor", lambda e: e.nop(), wait=[("v", c_xn_v)])
        c_sq2_start = p.n["s"]
        c_reps2 = ln_stats(p, m, TC, xn_tiles, nd, m["onescol_b"][:, 0:1], m["sq"])
        c_m1_t = p.n["t"]
        xln_t = m["xln_ch"]
        c_xln = 0
        wxl = H["xln_read"].get(c - 1, [])
        for i in range(nd):
            p.v(lambda e, i=i: e.tensor_tensor(m["sq"][:, i * TC:(i + 1) * TC],
                                               xn_t[:, i * TC:(i + 1) * TC], m["mu_rep"][:, 0:TC], ALU.add),
                wait=[("v", c_reps2)])
            c_vm = p.v(lambda e, i=i: e.tensor_tensor(m["sq"][:, i * TC:(i + 1) * TC],
                                                      m["sq"][:, i * TC:(i + 1) * TC],
                                                      m["mu_rep"][:, TC:2 * TC], ALU.mult))
            c_xln_v = p.v(lambda e, i=i: e.tensor_scalar(xln_t[:, i * TC:(i + 1) * TC],
                                                         m["sq"][:, i * TC:(i + 1) * TC],
                                                         m["lng"][:, i:i + 1], m["lnb"][:, i:i + 1],
                                                         ALU.mult, ALU.add),
                          wait=(wxl if i == 0 else ()))
            c_xln = c_xln_v
        H["sq_read"][c] = [("v", c_xln)]
        H["xn_done"][c] = [("t", c_m1_t), ("s", p.n["s"]), ("qa", c_xnpg)]

        braw = m["braw_ch"][c % 2]
        z_t = m["z_ch"]
        if prev_braw is None:
            c_carry = p.v(lambda e: e.memset(
                braw[:, 0:ndi * (TC + 3)].rearrange("p (i tc) -> p i tc", i=ndi)[:, :, 0:3], 0.0))
        else:
            pb_ = prev_braw
            c_carry = 0
            for i in range(ndi):
                c_carry = p.v(lambda e, i=i: e.tensor_copy(
                    braw[:, i * (TC + 3):i * (TC + 3) + 3],
                    pb_[:, i * (TC + 3) + TC:i * (TC + 3) + TC + 3]))
        ps_free = m["s1_ps_free"]
        c_zwait = H["zpg"].get(c - 1, 0)
        last_mm = 0
        for po in range(32):
            ps = m["ps_mm"][po % 2]
            c_mm = tiled_mm(p, ps[:, 0:TC],
                            [m["w_in"][:, pi * 4096 + po * 128:pi * 4096 + (po + 1) * 128] for pi in range(nd)],
                            [xln_t[:, pi * TC:(pi + 1) * TC] for pi in range(nd)],
                            wait=[("v", c_xln), ("v", ps_free[po % 2])])
            last_mm = c_mm
            if po < 16:
                dst = braw[:, po * (TC + 3) + 3:po * (TC + 3) + 3 + TC]
            else:
                dst = z_t[:, (po - 16) * TC:(po - 15) * TC]
            ps_free[po % 2] = p.v(lambda e, dst=dst, ps=ps: e.tensor_copy(dst, ps[:, 0:TC]),
                                  wait=[("t", c_mm)] + ([("qa", c_zwait)] if po == 16 else []))
        H["xln_read"][c] = [("t", last_mm)]
        c_z = p.n["v"]
        H["zpg"][c] = p.dma("scalar", m["z_s"][:, c * TC:(c + 1) * TC].rearrange("(i p) tc -> p i tc", p=128),
                            z_t[:, 0:ndi * TC].rearrange("p (i tc) -> p i tc", i=ndi), wait=[("v", c_z)])

        xc_t = m["xc_ch"]
        c_silu = {}
        c_xcw = H["xcpg"].get(c - 1, 0)
        for i in range(ndi):
            base = i * (TC + 3)
            cv = m["cv"][:, (i % 2) * TC:(i % 2 + 1) * TC]
            w2 = [("v", c_z), ("v", c_carry)]
            if i >= 2:
                w2.append(("s", c_silu[i - 2]))
            p.v(lambda e, i=i, base=base, cv=cv: e.tensor_scalar(
                cv, braw[:, base + 0:base + 0 + TC], m["convw"][:, 4 * i + 0:4 * i + 1], None, ALU.mult),
                wait=w2)
            for j in range(1, 4):
                p.v(lambda e, i=i, base=base, j=j, cv=cv: e.scalar_tensor_tensor(
                    cv, braw[:, base + j:base + j + TC], m["convw"][:, 4 * i + j:4 * i + j + 1],
                    cv, ALU.mult, ALU.add))
            c_last = p.n["v"]
            c_silu[i] = p.s(lambda e, i=i, cv=cv: e.activation(
                xc_t[:, i * TC:(i + 1) * TC], cv, AF.Silu,
                bias=m["convb"][:, i:i + 1]), wait=[("v", c_last)] + ([("qa", c_xcw)] if i == 0 else []))
        c_conv = c_silu[ndi - 1]
        H["xcpg"][c] = p.dma("scalar", m["xc_s"][:, c * TC:(c + 1) * TC].rearrange("(i p) tc -> p i tc", p=128),
                             xc_t[:, 0:ndi * TC].rearrange("p (i tc) -> p i tc", i=ndi),
                             wait=[("s", c_conv)])
        prev_braw = braw
        H["x_done"][c] = [("v", c_xn_v), ("s", c_sq2_start), ("t", c_m1_t)]
    p.op("sync", lambda e: e.nop(), wait=[("qa", p.n["qa"]), ("qs", p.n["qs"]), ("qg", p.n["qg"])])


def s1_preamble(p, m):
    nd = D // 128
    for pi in range(nd):
        p.dma("gpsimd", m["w_in"][:, pi * 4096:(pi + 1) * 4096],
              m["w_in_d"][pi * 128:(pi + 1) * 128, :])
    p.dma("sync", m["lng"][:], m["lng_d"])
    p.dma("sync", m["lnb"][:], m["lnb_d"])
    p.dma("sync", m["convw"][:], m["convw_d"])
    p.dma("sync", m["convb"][:], m["convb_d"])
    p.v(lambda e: e.memset(m["onescol"][:], 1.0))
    p.v(lambda e: e.memset(m["onescol_b"][:], 1.0))
    p.v(lambda e: e.memset(m["onesrow"][:], 1.0))
    m.setdefault("s1_ps_free", [0, 0])
    m["s1_ps_free"][0] = 0
    m["s1_ps_free"][1] = 0


def _new_nc():
    nc = bass.Bass("TRN2", target_bir_lowering=False)
    nc.detect_race_conditions = False
    return nc


def build_s1():
    nc = _new_nc()
    TC = 256
    nd = D // 128
    ndi = DI // 128
    m = {}
    m["x"] = nc.dram_tensor("x", [D, T], F32, kind="ExternalInput").ap()
    m["w_in_d"] = nc.dram_tensor("w_in", [D, 4096], F32, kind="ExternalInput").ap()
    m["lng_d"] = nc.dram_tensor("lng", [128, nd], F32, kind="ExternalInput").ap()
    m["lnb_d"] = nc.dram_tensor("lnb", [128, nd], F32, kind="ExternalInput").ap()
    m["convw_d"] = nc.dram_tensor("convw", [128, 4 * ndi], F32, kind="ExternalInput").ap()
    m["convb_d"] = nc.dram_tensor("convb", [128, ndi], F32, kind="ExternalInput").ap()
    m["xn_s"] = nc.dram_tensor("xn_s", [D, T], BF16, kind="ExternalOutput").ap()
    m["xc_s"] = nc.dram_tensor("xc_s", [DI, T], BF16, kind="ExternalOutput").ap()
    m["z_s"] = nc.dram_tensor("z_s", [DI, T], BF16, kind="ExternalOutput").ap()

    es = contextlib.ExitStack()
    with es:
        sb = lambda name, shape, dt: es.enter_context(nc.sbuf_tensor(name, shape, dt))
        m["w_in"] = sb("w_in_sb", [128, nd * 4096], BF16)
        m["x_ch2"] = [sb(f"x_ch{i}", [128, nd * TC], F32) for i in range(2)]
        m["xn_ch2"] = [sb(f"xn_ch{i}", [128, nd * TC], BF16) for i in range(2)]
        m["xln_ch"] = sb("xln_ch", [128, nd * TC], BF16)
        m["sq"] = sb("sq", [128, nd * TC], F32)
        m["braw_ch"] = [sb(f"braw{i}", [128, ndi * (TC + 3)], BF16) for i in range(2)]
        m["z_ch"] = sb("z_ch", [128, ndi * TC], BF16)
        m["xc_ch"] = sb("xc_ch", [128, ndi * TC], BF16)
        m["cv"] = sb("cv", [128, 2 * TC], F32)
        m["row0"] = sb("row0", [1, TC], F32)
        m["row1"] = sb("row1", [1, TC], F32)
        m["row2"] = sb("row2", [1, TC], F32)
        m["mu_rep"] = sb("mu_rep", [128, 2 * TC], F32)
        m["onescol"] = sb("onescol", [128, 1], F32)
        m["onescol_b"] = sb("onescol_b", [128, 1], BF16)
        m["onesrow"] = sb("onesrow", [1, 128], F32)
        m["lng"] = sb("lng_sb", [128, nd], F32)
        m["lnb"] = sb("lnb_sb", [128, nd], F32)
        m["convw"] = sb("convw_sb", [128, 4 * ndi], F32)
        m["convb"] = sb("convb_sb", [128, ndi], F32)
        m["ps_r0"] = es.enter_context(nc.psum_tensor("ps_r0", [128, 512], F32))
        m["ps_r1"] = es.enter_context(nc.psum_tensor("ps_r1", [128, 512], F32))
        m["ps_rep"] = es.enter_context(nc.psum_tensor("ps_rep", [128, 512], F32))
        m["ps_rep2"] = es.enter_context(nc.psum_tensor("ps_rep2", [128, 512], F32))
        m["ps_mm"] = [es.enter_context(nc.psum_tensor(f"ps_mm{i}", [128, 512], F32)) for i in range(2)]

        sem_names = ["t", "v", "s", "g", "qg", "qs", "qa", "y"]
        sems = {k: es.enter_context(nc.semaphore(k)) for k in sem_names}

        def describe(p):
            s1_preamble(p, m)
            s1_body(p, m)

        with nc.Block() as block:
            for en in ENGINES:
                def mk(en):
                    def section(eng):
                        describe(Prog(nc, sems, active=en, eng=eng))
                    return section
                getattr(block, en)(mk(en))
    return nc



def _col_tiles(v, p=128):
    return np.ascontiguousarray(np.asarray(v, np.float32).reshape(-1, p).T)


def s1_inmap(inputs, b):
    x = np.asarray(inputs["L0_attn_x"], dtype=np.float32)
    return dict(
        x=np.ascontiguousarray(x[b].T),
        w_in=np.ascontiguousarray(np.asarray(inputs["L0_attn_w_in"], np.float32)),
        lng=_col_tiles(inputs["L0_attn_ln_g"]),
        lnb=_col_tiles(inputs["L0_attn_ln_b"]),
        convw=np.ascontiguousarray(
            np.asarray(inputs["L0_attn_conv_w"], np.float32).reshape(DI, 4)
            .reshape(-1, 128, 4).transpose(1, 0, 2).reshape(128, -1)),
        convb=_col_tiles(inputs["L0_attn_conv_b"]),
    )


def kernel(**inputs):
    n_cores = 8
    nc = build_s1()
    in_maps = [s1_inmap(inputs, b) for b in range(n_cores)]
    res = run_bass_kernel_spmd(nc, in_maps, core_ids=list(range(n_cores)))
    return res


def s2_body(p, m):
    TC = 256
    NCH = T // TC
    ndi = DI // 128
    snap = p.barrier()
    p.wait_snapshot(snap)
    ch_v_end = {-2: 0, -1: 0}
    ch_t_end = {-2: 0, -1: 0}
    scan_end = {-1: 0}
    xc_load = {}
    yg_pg = {}
    for c in range(NCH):
        par = c % 2
        xc_t = m["xc_ch2"][par]
        dt_t = m["dt_ch2"][par]
        xc_load[c] = p.dma("sync", xc_t[:, 0:ndi * TC].rearrange("p (i tc) -> p i tc", i=ndi),
                           m["xc_s"][:, c * TC:(c + 1) * TC].rearrange("(i p) tc -> p i tc", p=128),
                           wait=[("v", ch_v_end[c - 2]), ("t", ch_t_end[c - 2])])
        ps_free = m["s2_ps_free"]
        for g in range(ndi):
            ps = m["pb"][g % 2]
            c_mm = tiled_mm(p, ps[:, 0:TC],
                            [m["w_dt"][:, pi * DI + g * 128:pi * DI + (g + 1) * 128] for pi in range(ndi)],
                            [xc_t[:, pi * TC:(pi + 1) * TC] for pi in range(ndi)],
                            wait=[("v", ps_free[g % 2][0]), ("s", ps_free[g % 2][1]),
                                  ("qs", xc_load[c])])
            bcol = m["bdt"][:, g:g + 1]
            pre = m["scr"][:, 0:TC]
            ab = m["scr"][:, TC:2 * TC]
            rr = m["scr"][:, 2 * TC:3 * TC]
            c_pre = p.v(lambda e, ps=ps, pre=pre, bcol=bcol: e.tensor_scalar(
                pre, ps[:, 0:TC], bcol, None, ALU.add),
                wait=[("t", c_mm), ("s", m["sp_done"].get((c, g - 1), 0))])
            p.s(lambda e, pre=pre, ab=ab: e.activation(ab, pre, AF.Abs), wait=[("v", c_pre)])
            p.s(lambda e, ab=ab: e.activation(ab, ab, AF.Exp, scale=-1.0))
            c_l = p.s(lambda e, ab=ab: e.activation(ab, ab, AF.Ln, bias=1.0))
            c_r = p.s(lambda e, pre=pre, rr=rr: e.activation(rr, pre, AF.Relu))
            m["sp_done"][(c, g)] = c_r
            c_dtv = p.v(lambda e, g=g, ab=ab, rr=rr, dt_t=dt_t: e.tensor_tensor(
                dt_t[:, g * TC:(g + 1) * TC], rr, ab, ALU.add), wait=[("s", c_r)])
            ps_free[g % 2] = (c_dtv, c_r)
        c_dt = p.n["v"]

        c_xd = tiled_mm(p, m["pb"][2][0:32, 0:TC],
                        [m["w_xp"][:, pi * 32:(pi + 1) * 32] for pi in range(ndi)],
                        [xc_t[:, pi * TC:(pi + 1) * TC] for pi in range(ndi)],
                        wait=[("v", m["xd_free"])])
        m["xd_free"] = p.v(lambda e: e.tensor_copy(m["xdbl"][0:32, 0:TC], m["pb"][2][0:32, 0:TC]),
                           wait=[("t", c_xd)])
        c_xdv = m["xd_free"]
        rep_free = m["s2_rep_free"]
        for n in range(32):
            ps = m["pb"][3 + n % 2]
            dst = (m["B_rep"] if n < 16 else m["C_rep"])[:, (n % 16) * TC:(n % 16 + 1) * TC]
            c_mm = p.t(lambda e, ps=ps, n=n: e.matmul(ps[:, 0:TC], m["sel32"][:, n * 128:(n + 1) * 128],
                                                      m["xdbl"][0:32, 0:TC], start=True, stop=True),
                       wait=[("v", c_xdv), ("s", rep_free[n % 2]), ("v", scan_end[c - 1])])
            rep_free[n % 2] = p.s(lambda e, dst=dst, ps=ps: e.activation(dst, ps[:, 0:TC], AF.Identity),
                                  wait=[("t", c_mm)])
        c_rep = p.n["s"]

        yg_t = m["yg_ch"]
        for g in range(ndi):
            dt_g = dt_t[:, g * TC:(g + 1) * TC]
            xc_g = xc_t[:, g * TC:(g + 1) * TC]
            a_g = m["a_ch2"][0]
            c_a_n = []
            for n in range(DS):
                c_a_n.append(p.s(lambda e, n=n, g=g: e.activation(
                    a_g[:, n * TC:(n + 1) * TC], dt_g, AF.Exp,
                    scale=m["Aneg"][:, g * DS + n:g * DS + n + 1]),
                    wait=([("v", c_dt), ("v", m["a_free"][0])] if n == 0 else ())))
            c_u = p.v(lambda e: e.tensor_tensor(m["u_g"][:, 0:TC], dt_g, xc_g, ALU.mult),
                      wait=[("s", c_rep)])
            y_g = m["y_g"][:, 0:TC]
            tmp = m["y_g"][:, TC:2 * TC]
            b_g = m["y_g"][:, 2 * TC:3 * TC]
            hs = m["h_state"]
            for n in range(DS):
                p.v(lambda e, n=n: e.tensor_tensor(b_g, m["u_g"][:, 0:TC],
                                                   m["B_rep"][:, n * TC:(n + 1) * TC], ALU.mult))
                p.v(lambda e, n=n, g=g: e.tensor_tensor_scan(
                    m["h_g"][:, n * TC:(n + 1) * TC], a_g[:, n * TC:(n + 1) * TC], b_g,
                    hs[:, g * DS + n:g * DS + n + 1], ALU.mult, ALU.add),
                    wait=[("s", c_a_n[n])])
                if n == 0:
                    p.v(lambda e, n=n: e.tensor_tensor(y_g, m["h_g"][:, n * TC:(n + 1) * TC],
                                                       m["C_rep"][:, n * TC:(n + 1) * TC], ALU.mult))
                else:
                    p.v(lambda e, n=n: e.tensor_tensor(tmp, m["h_g"][:, n * TC:(n + 1) * TC],
                                                       m["C_rep"][:, n * TC:(n + 1) * TC], ALU.mult))
                    p.v(lambda e: e.tensor_tensor(y_g, y_g, tmp, ALU.add))
            m["a_free"][g % 2] = p.n["v"]
            hv = m["h_g"][:, 0:DS * TC].rearrange("p (n tc) -> p n tc", tc=TC)
            p.v(lambda e, g=g: e.tensor_copy(
                hs[:, g * DS:(g + 1) * DS].rearrange("p (n o) -> p n o", o=1), hv[:, :, TC - 1:TC]))
            slot = yg_t[:, (g % 2) * TC:(g % 2 + 1) * TC]
            c_y2 = p.v(lambda e, g=g, slot=slot: e.scalar_tensor_tensor(slot,
                                                             xc_g, m["Dcol"][:, g:g + 1],
                                                             y_g, ALU.mult, ALU.add),
                       wait=[("qa", yg_pg.get(g % 2, 0))])
            yg_pg[g % 2] = p.dma("scalar",
                             m["yg_s"][g * 128:(g + 1) * 128, c * TC:(c + 1) * TC],
                             slot, wait=[("v", c_y2)])
        scan_end[c] = p.n["v"]
        ch_v_end[c] = p.n["v"]
        ch_t_end[c] = p.n["t"]
    p.op("sync", lambda e: e.nop(), wait=[("qa", p.n["qa"]), ("qs", p.n["qs"]), ("qg", p.n["qg"])])


def s2_preamble(p, m):
    ndi = DI // 128
    for pi in range(ndi):
        p.dma("gpsimd", m["w_dt"][:, pi * DI:(pi + 1) * DI], m["w_dt_d"][pi * 128:(pi + 1) * 128, :])
    for pi in range(ndi):
        p.dma("gpsimd", m["w_xp"][:, pi * 32:(pi + 1) * 32], m["w_xp_d"][pi * 128:(pi + 1) * 128, :])
    p.dma("sync", m["bdt"][:], m["bdt_d"])
    p.dma("sync", m["Aneg"][:], m["Alog_d"])
    p.dma("sync", m["Dcol"][:], m["D_d"])
    c_sel = p.dma("sync", m["sel32"][0:32, :], m["sel32_d"])
    c_e = p.s(lambda e: e.activation(m["Aneg"][:], m["Aneg"][:], AF.Exp), wait=[("qs", p.n["qs"])])
    p.v(lambda e: e.tensor_scalar(m["Aneg"][:], m["Aneg"][:], -1.0, None, ALU.mult), wait=[("s", c_e)])
    p.v(lambda e: e.memset(m["h_state"][:], 0.0))
    m["sp_done"] = {}
    m["xd_free"] = 0
    m["a_free"] = [0, 0]
    m["s2_ps_free"] = [(0, 0), (0, 0)]
    m["s2_rep_free"] = [0, 0]


def build_s2():
    nc = _new_nc()
    TC = 256
    ndi = DI // 128
    m = {}
    m["xc_s"] = nc.dram_tensor("xc_s", [DI, T], BF16, kind="ExternalInput").ap()
    m["w_dt_d"] = nc.dram_tensor("w_dt", [DI, DI], F32, kind="ExternalInput").ap()
    m["w_xp_d"] = nc.dram_tensor("w_xp", [DI, 32], F32, kind="ExternalInput").ap()
    m["bdt_d"] = nc.dram_tensor("bdt", [128, ndi], F32, kind="ExternalInput").ap()
    m["Alog_d"] = nc.dram_tensor("Alog", [128, ndi * DS], F32, kind="ExternalInput").ap()
    m["D_d"] = nc.dram_tensor("Dv", [128, ndi], F32, kind="ExternalInput").ap()
    m["sel32_d"] = nc.dram_tensor("sel32", [32, 32 * 128], F32, kind="ExternalInput").ap()
    m["yg_s"] = nc.dram_tensor("yg_s", [DI, T], BF16, kind="ExternalOutput").ap()

    es = contextlib.ExitStack()
    with es:
        sb = lambda name, shape, dt: es.enter_context(nc.sbuf_tensor(name, shape, dt))
        m["w_dt"] = sb("w_dt_sb", [128, ndi * DI], BF16)
        m["w_xp"] = sb("w_xp_sb", [128, ndi * 32], BF16)
        m["bdt"] = sb("bdt_sb", [128, ndi], F32)
        m["Aneg"] = sb("Aneg", [128, ndi * DS], F32)
        m["Dcol"] = sb("Dcol", [128, ndi], F32)
        m["sel32"] = sb("sel32_sb", [32, 32 * 128], F32)
        m["xc_ch2"] = [sb(f"xc_ch{i}", [128, ndi * TC], BF16) for i in range(2)]
        m["dt_ch2"] = [sb(f"dt_ch{i}", [128, ndi * TC], F32) for i in range(2)]
        a1 = sb("a_ch", [128, DS * TC], F32)
        m["a_ch2"] = [a1, a1]
        m["h_g"] = sb("h_g", [128, DS * TC], F32)
        m["u_g"] = sb("u_g", [128, TC], F32)
        m["y_g"] = sb("y_g", [128, 3 * TC], F32)

        m["scr"] = sb("scr", [128, 3 * TC], F32)
        m["B_rep"] = sb("B_rep", [128, DS * TC], F32)
        m["C_rep"] = sb("C_rep", [128, DS * TC], BF16)
        m["xdbl"] = sb("xdbl", [32, TC], F32)
        m["h_state"] = sb("h_state", [128, ndi * DS], F32)
        m["yg_ch"] = sb("yg_ch", [128, 2 * TC], BF16)
        m["pb"] = [es.enter_context(nc.psum_tensor(f"pb{i}", [128, 512], F32)) for i in range(5)]

        sem_names = ["t", "v", "s", "g", "qg", "qs", "qa", "y"]
        sems = {k: es.enter_context(nc.semaphore(k)) for k in sem_names}

        def describe(p):
            s2_preamble(p, m)
            s2_body(p, m)

        with nc.Block() as block:
            for en in ENGINES:
                def mk(en):
                    def section(eng):
                        describe(Prog(nc, sems, active=en, eng=eng))
                    return section
                getattr(block, en)(mk(en))
    return nc


def s2_inmap(inputs, s1_out):
    ndi = DI // 128
    return dict(
        xc_s=s1_out["xc_s"],
        w_dt=np.ascontiguousarray(np.asarray(inputs["L0_attn_w_dt"], np.float32)),
        w_xp=np.ascontiguousarray(np.asarray(inputs["L0_attn_w_xproj"], np.float32)),
        bdt=_col_tiles(inputs["L0_attn_b_dt"]),
        Alog=np.ascontiguousarray(
            np.asarray(inputs["L0_attn_A_log"], np.float32).reshape(ndi, 128, DS)
            .transpose(1, 0, 2).reshape(128, ndi * DS)),
        Dv=_col_tiles(inputs["L0_attn_D"]),
        sel32=_sel32(),
    )


def _sel32():
    sel = np.zeros((32, 32 * 128), np.float32)
    for n in range(32):
        sel[n, n * 128:(n + 1) * 128] = 1.0
    return sel

S3_DEBUG = False


def s3_body(p, m):
    nd = D // 128
    ndi = DI // 128
    NE = 16
    TCA = 256
    TCE = 512
    NTT = T // 128

    snap = p.barrier()
    for c in range(T // TCA):
        p.wait_snapshot(snap)
        x_t, xn_t, yg_t, x1_t = m["x_ch"], m["xn_ch"], m["yg_ch"], m["x1_ch"]
        c_x = p.dma("sync", x_t[:, 0:nd * TCA].rearrange("p (i tc) -> p i tc", i=nd),
                    m["x"][:, c * TCA:(c + 1) * TCA].rearrange("(i p) tc -> p i tc", p=128))
        c_xn = p.dma("sync", xn_t[:, 0:nd * TCA].rearrange("p (i tc) -> p i tc", i=nd),
                     m["xn_s"][:, c * TCA:(c + 1) * TCA].rearrange("(i p) tc -> p i tc", p=128))
        c_yg = p.dma("sync", yg_t[:, 0:ndi * TCA].rearrange("p (i tc) -> p i tc", i=ndi),
                     m["yg_s"][:, c * TCA:(c + 1) * TCA].rearrange("(i p) tc -> p i tc", p=128))
        c_z = p.dma("sync", m["z_ch"][:, 0:ndi * TCA].rearrange("p (i tc) -> p i tc", i=ndi),
                    m["z_s"][:, c * TCA:(c + 1) * TCA].rearrange("(i p) tc -> p i tc", p=128))
        c_gate = 0
        for i in range(ndi):
            c_sz = p.s(lambda e, i=i: e.activation(m["sz_g"][:, (i % 2) * TCA:(i % 2 + 1) * TCA],
                                                   m["z_ch"][:, i * TCA:(i + 1) * TCA], AF.Silu),
                       wait=[("qs", c_z), ("v", m["sz_free"][i % 2])])
            m["sz_free"][i % 2] = c_gate = p.v(
                lambda e, i=i: e.tensor_tensor(yg_t[:, i * TCA:(i + 1) * TCA],
                                               yg_t[:, i * TCA:(i + 1) * TCA],
                                               m["sz_g"][:, (i % 2) * TCA:(i % 2 + 1) * TCA], ALU.mult),
                wait=[("s", c_sz)])
        p.op("tensor", lambda e: e.nop(), wait=[("v", c_gate)])
        ps_free = [0, 0]
        for po in range(nd):
            ps = m["pb"][po % 2]
            c_mm = tiled_mm(p, ps[:, 0:TCA],
                            [m["w_out"][:, pi * D + po * 128:pi * D + (po + 1) * 128] for pi in range(ndi)],
                            [yg_t[:, pi * TCA:(pi + 1) * TCA] for pi in range(ndi)],
                            wait=[("v", ps_free[po % 2])])
            p.v(lambda e, po=po, ps=ps: e.tensor_tensor(
                m["scr"][:, 0:TCA], ps[:, 0:TCA], x_t[:, po * TCA:(po + 1) * TCA], ALU.add),
                wait=[("t", c_mm), ("qa", m["x1_paged"])])
            ps_free[po % 2] = p.v(lambda e, po=po: e.tensor_tensor(
                x1_t[:, po * TCA:(po + 1) * TCA], m["scr"][:, 0:TCA],
                xn_t[:, po * TCA:(po + 1) * TCA], ALU.add))
        c_x1 = p.n["v"]
        m["x1_paged"] = p.dma("scalar",
                              m["x1p_s"][:, c * TCA:(c + 1) * TCA].rearrange("(i p) tc -> p i tc", p=128),
                              x1_t[:, 0:nd * TCA].rearrange("p (i tc) -> p i tc", i=nd),
                              wait=[("v", c_x1)])
        c_lg = tiled_mm(p, m["pb"][2][0:16, 0:TCA],
                        [m["gate_wT"][:, pi * NE:(pi + 1) * NE] for pi in range(nd)],
                        [xn_t[:, pi * TCA:(pi + 1) * TCA] for pi in range(nd)],
                        wait=[("s", m["lg_free"])])
        m["lg_free"] = p.s(lambda e, c=c: e.activation(
            m["logitsT"][0:16, c * TCA:(c + 1) * TCA], m["pb"][2][0:16, 0:TCA], AF.Identity),
            wait=[("t", c_lg)])
        snap = p.barrier()

    NTE = NTT * NE
    LG = m["gb"][:, 0 * NTE:1 * NTE]
    BI = m["gb"][:, 1 * NTE:2 * NTE]
    SC = m["gb"][:, 2 * NTE:3 * NTE]
    M1 = m["gb"][:, 3 * NTE:4 * NTE]
    M2 = m["gb"][:, 4 * NTE:5 * NTE]
    TM = m["gb"][:, 5 * NTE:6 * NTE]
    G1 = m["gb"][:, 6 * NTE:7 * NTE]
    V1 = m["gn"][:, 0:16]
    V2 = m["gn"][:, 16:32]
    S1 = m["gn"][:, 32:48]
    S2 = m["gn"][:, 48:64]
    RC = m["gn"][:, 64:80]
    W1 = m["gn"][:, 80:96]
    W2 = m["gn"][:, 96:112]
    DEN = m["gn"][:, 112:128]
    DEN2 = m["gn"][:, 128:144]
    RC2 = m["gn"][:, 144:160]
    d3 = lambda ap: ap.rearrange("p (o e) -> p o e", e=NE)
    bc = lambda ap: ap.rearrange("p (o e) -> p o e", o=1).broadcast_to((128, NTT, NE))
    bcn = lambda ap: ap.rearrange("p (o e) -> p o e", e=1).broadcast_to((128, NTT, NE))
    for tt in range(NTT):
        c_tr = ptrans(p, m["pb"][3][:, 0:16], m["logitsT"][0:16, tt * 128:(tt + 1) * 128],
                      m["ident"], wait=[("s", m["lg_free"]), ("v", m["lg_copy"])])
        m["lg_copy"] = p.v(lambda e, tt=tt: e.tensor_copy(LG[:, tt * NE:(tt + 1) * NE],
                                                          m["pb"][3][:, 0:16]), wait=[("t", c_tr)])
    c_lgv = p.v(lambda e: e.tensor_tensor(d3(BI), d3(LG), bc(m["ebias_rep"][:, 0:16]), ALU.add),
                wait=[("v", m["lg_copy"], True)])
    c_sig = p.s(lambda e: e.activation(SC, LG, AF.Sigmoid), wait=[("v", c_lgv)])
    c_v1 = p.v(lambda e: e.tensor_reduce(V1, d3(BI), AX.X, ALU.max))
    p.v(lambda e: e.tensor_tensor(d3(M1), d3(BI), bcn(V1), ALU.is_ge), wait=[("v", c_v1, True)])
    p.v(lambda e: e.scalar_tensor_tensor(TM, M1, -1e30, BI, ALU.mult, ALU.add))
    c_v2 = p.v(lambda e: e.tensor_reduce(V2, d3(TM), AX.X, ALU.max))
    p.v(lambda e: e.tensor_tensor(d3(M2), d3(TM), bcn(V2), ALU.is_ge), wait=[("v", c_v2, True)])
    p.v(lambda e: e.tensor_tensor(TM, SC, M1, ALU.mult), wait=[("s", c_sig)])
    c_s1 = p.v(lambda e: e.tensor_reduce(S1, d3(TM), AX.X, ALU.add))
    p.v(lambda e: e.tensor_tensor(TM, SC, M2, ALU.mult))
    c_s2 = p.v(lambda e: e.tensor_reduce(S2, d3(TM), AX.X, ALU.add))
    p.v(lambda e: e.tensor_tensor(DEN, S1, S2, ALU.add), wait=[("v", c_s2, True)])
    c_den = p.v(lambda e: e.tensor_scalar(DEN2, DEN, 1e-10, None, ALU.add), wait=[("v", p.n["v"], True)])
    p.s(lambda e: e.activation(RC, DEN2, AF.Ln), wait=[("v", c_den)])
    c_rec = p.s(lambda e: e.activation(RC2, RC, AF.Exp, scale=-1.0))
    p.v(lambda e: e.tensor_tensor(W1, S1, RC2, ALU.mult), wait=[("s", c_rec)])
    c_w2 = p.v(lambda e: e.tensor_tensor(W2, S2, RC2, ALU.mult))
    p.v(lambda e: e.tensor_tensor(d3(G1), d3(M1), bcn(W1), ALU.mult), wait=[("v", c_w2, True)])
    p.v(lambda e: e.tensor_tensor(d3(TM), d3(M2), bcn(W2), ALU.mult))
    c_gw = p.v(lambda e: e.tensor_tensor(m["gatew"][:, 0:NTE], G1, TM, ALU.add))
    for tt in range(NTT):
        c_tr2 = ptrans(p, m["pb"][4][0:16, 0:128], m["gatew"][:, tt * NE:(tt + 1) * NE],
                       m["ident"], wait=[("v", c_gw), ("s", m["gt_free"])])
        m["gt_free"] = p.s(lambda e, tt=tt: e.activation(
            m["gatewT"][0:16, tt * 128:(tt + 1) * 128], m["pb"][4][0:16, 0:128], AF.Identity),
            wait=[("t", c_tr2)])
    if S3_DEBUG:
        p.v(lambda e: e.tensor_copy(m["dbgm_sb"][:, 0:NTE], M1), wait=[("v", c_gw)])
        p.v(lambda e: e.tensor_copy(m["dbgm_sb"][:, NTE:2 * NTE], M2))
        p.v(lambda e: e.tensor_copy(m["dbgm_sb"][:, 2 * NTE:3 * NTE], BI))
        p.v(lambda e: e.tensor_copy(m["dbgm_sb"][:, 3 * NTE:4 * NTE], SC))
        p.v(lambda e: e.tensor_copy(m["dbgc_sb"][:, 0:112], m["gn"][:, 0:112]))
    c_gating = p.barrier()

    c_ones = m["c_ones"]
    for tt in range(NTT):
        for h in range(2):
            c_mm = p.t(lambda e, tt=tt, h=h: e.matmul(
                m["pb"][tt % 2][:, 0:512],
                m["gatewT"][0:17, tt * 128:(tt + 1) * 128],
                m["b2aug"][0:17, h * 512:(h + 1) * 512], start=True, stop=True),
                wait=[("v", max(c_ones, m["t2_free"][tt % 2])), ("s", m["gt_free"])])
            m["t2_free"][tt % 2] = p.v(lambda e, tt=tt, h=h: e.tensor_copy(
                m["Y_acc"][:, tt * D + h * 512:tt * D + (h + 1) * 512], m["pb"][tt % 2][:, 0:512]),
                wait=[("t", c_mm)])
    snap = p.barrier()

    for e_i in range(NE + 1):
        shared = (e_i == 0)
        nf = FDFF0 // 128 if shared else DFF0 // 128
        p.wait_snapshot(snap)
        if shared:
            c_w = 0
            c_w = p.dma("gpsimd", m["wp1"][:, 0:nd * FDFF0].rearrange("p (i f) -> p i f", i=nd),
                        m["sw1T_d"].rearrange("(i p) f -> p i f", p=128))
            c_w = p.dma("gpsimd", m["wp3"][:, 0:nd * FDFF0].rearrange("p (i f) -> p i f", i=nd),
                        m["sw3T_d"].rearrange("(i p) f -> p i f", p=128))
            c_w = p.dma("gpsimd", m["wp2"][:, 0:nf * D].rearrange("p (i f) -> p i f", i=nf),
                        m["sw2T_d"].rearrange("(i p) f -> p i f", p=128))
        else:
            ee = e_i - 1
            c_w = p.dma("gpsimd", m["wp1"][:, 0:nd * DFF0].rearrange("p (i f) -> p i f", i=nd),
                        m["w1T_d"][ee].rearrange("(i p) f -> p i f", p=128))
            c_w = p.dma("gpsimd", m["wp3"][:, 0:nd * DFF0].rearrange("p (i f) -> p i f", i=nd),
                        m["w3T_d"][ee].rearrange("(i p) f -> p i f", p=128))
            c_w = p.dma("gpsimd", m["wp2"][:, 0:nf * D].rearrange("p (i f) -> p i f", i=nf),
                        m["w2T_d"][ee].rearrange("(i p) f -> p i f", p=128))
        FD = nf * 128
        for c in range(T // TCE):
            c_xn = p.dma("sync", m["xn_e"][:, 0:nd * TCE].rearrange("p (i tc) -> p i tc", i=nd),
                         m["xn_s"][:, c * TCE:(c + 1) * TCE].rearrange("(i p) tc -> p i tc", p=128),
                         wait=[("t", m["xn_used"])])
            p.op("tensor", lambda e: e.nop(), wait=[("qs", c_xn), ("qg", c_w)])
            for po in range(nf):
                ps = m["pb"][po % 2]
                c_mm = tiled_mm(p, ps[:, 0:TCE],
                                [m["wp1"][:, pi * FD + po * 128:pi * FD + (po + 1) * 128] for pi in range(nd)],
                                [m["xn_e"][:, pi * TCE:(pi + 1) * TCE] for pi in range(nd)],
                                wait=[("v", m["h_free"][po % 2]), ("s", m["h_free_s"][po % 2])])
                if shared:
                    bcol = m["sb1"][:, po:po + 1]
                else:
                    bcol = m["b1_all"][:, (e_i - 1) * 4 + po:(e_i - 1) * 4 + po + 1]
                m["h_free_s"][po % 2] = p.s(lambda e, po=po, ps=ps, bcol=bcol: e.activation(
                    m["H1"][:, po * TCE:(po + 1) * TCE], ps[:, 0:TCE], AF.Silu, bias=bcol),
                    wait=[("t", c_mm)])
                ps2 = m["pb"][2 + po % 2]
                c_mm3 = tiled_mm(p, ps2[:, 0:TCE],
                                 [m["wp3"][:, pi * FD + po * 128:pi * FD + (po + 1) * 128] for pi in range(nd)],
                                 [m["xn_e"][:, pi * TCE:(pi + 1) * TCE] for pi in range(nd)],
                                 wait=[("v", m["h3_free"][po % 2])])
                if shared:
                    bcol3 = m["sb3"][:, po:po + 1]
                else:
                    bcol3 = m["b3_all"][:, (e_i - 1) * 4 + po:(e_i - 1) * 4 + po + 1]
                p.v(lambda e, po=po, ps2=ps2, bcol3=bcol3: e.tensor_scalar(
                    m["H3"][:, po * TCE:(po + 1) * TCE], ps2[:, 0:TCE], bcol3, None, ALU.add),
                    wait=[("t", c_mm3), ("s", m["h_free_s"][po % 2])])
                m["h3_free"][po % 2] = m["h_free"][po % 2] = p.v(
                    lambda e, po=po: e.tensor_tensor(
                        m["Hg"][:, po * TCE:(po + 1) * TCE], m["H1"][:, po * TCE:(po + 1) * TCE],
                        m["H3"][:, po * TCE:(po + 1) * TCE], ALU.mult))
            m["xn_used"] = p.n["t"]
            c_hg = p.n["v"]
            for j in range(TCE // 128):
                gtile = c * (TCE // 128) + j
                for h in range(2):
                    ps = m["pb"][4 + (j * 2 + h) % 2]
                    c_mm = tiled_mm(p, ps[:, 0:512],
                                    [m["Hg"][:, pi * TCE + j * 128:pi * TCE + (j + 1) * 128] for pi in range(nf)],
                                    [m["wp2"][:, pi * D + h * 512:pi * D + (h + 1) * 512] for pi in range(nf)],
                                    wait=[("v", max(c_hg, m["y_free"][(j * 2 + h) % 2]))])
                    dst = m["Y_acc"][:, gtile * D + h * 512:gtile * D + (h + 1) * 512]
                    if shared:
                        m["y_free"][(j * 2 + h) % 2] = p.v(
                            lambda e, dst=dst, ps=ps: e.scalar_tensor_tensor(
                                dst, ps[:, 0:512], 1.0, dst, ALU.mult, ALU.add),
                            wait=[("t", c_mm)])
                    else:
                        gcol = m["gatew"][:, gtile * NE + (e_i - 1):gtile * NE + e_i]
                        m["y_free"][(j * 2 + h) % 2] = p.v(
                            lambda e, dst=dst, ps=ps, gcol=gcol: e.scalar_tensor_tensor(
                                dst, ps[:, 0:512], gcol, dst, ALU.mult, ALU.add),
                            wait=[("t", c_mm)])
        snap = p.barrier()

    for tt in range(NTT):
        p.wait_snapshot(snap)
        c_l = p.dma("sync", m["x1_ch"][:, 0:nd * 128].rearrange("p (i tc) -> p i tc", i=nd),
                    m["x1p_s"][:, tt * 128:(tt + 1) * 128].rearrange("(i p) tc -> p i tc", p=128),
                    wait=[("qa", m["x1o_paged"]), ("v", m["x1o_free"])])
        for dd in range(nd):
            c_tr = ptrans(p, m["pb"][5 + dd % 2][:, 0:128],
                          m["Y_acc"][:, tt * D + dd * 128:tt * D + (dd + 1) * 128],
                          m["ident"], wait=[("v", m["tr_free"][dd % 2])])
            p.op("vector", lambda e: e.nop(), wait=[("qs", c_l)])
            m["tr_free"][dd % 2] = p.v(lambda e, tt=tt, dd=dd: e.tensor_tensor(
                m["x1_ch"][:, dd * 128:(dd + 1) * 128], m["x1_ch"][:, dd * 128:(dd + 1) * 128],
                m["pb"][5 + dd % 2][:, 0:128], ALU.add),
                wait=[("t", c_tr)])
        m["x1o_free"] = p.n["v"]
        m["x1o_paged"] = p.dma("scalar",
                               m["x1_s"][:, tt * 128:(tt + 1) * 128].rearrange("(i p) tc -> p i tc", p=128),
                               m["x1_ch"][:, 0:nd * 128].rearrange("p (i tc) -> p i tc", i=nd),
                               wait=[("v", m["x1o_free"])])
    if S3_DEBUG:
        p.dma("scalar", m["dbg_gatew"], m["gatew"][:])
        p.dma("scalar", m["dbg_logitsT"], m["logitsT"][0:16, :])
        p.dma("scalar", m["dbg_yacc"], m["Y_acc"][:])
        p.dma("scalar", m["dbg_m"], m["dbgm_sb"][:])
        p.dma("scalar", m["dbg_c"], m["dbgc_sb"][:])
    p.op("sync", lambda e: e.nop(), wait=[("qa", p.n["qa"]), ("qs", p.n["qs"]), ("qg", p.n["qg"])])


def s3_preamble(p, m):
    nd = D // 128
    ndi = DI // 128
    for pi in range(ndi):
        p.dma("gpsimd", m["w_out"][:, pi * D:(pi + 1) * D], m["w_out_d"][pi * 128:(pi + 1) * 128, :])
    p.dma("gpsimd", m["gate_wT"][:, 0:nd * 16].rearrange("p (i f) -> p i f", i=nd),
          m["gate_wT_d"].rearrange("(i p) f -> p i f", p=128))
    p.dma("sync", m["ident"][:], m["ident_d"])
    p.dma("sync", m["ebias_rep"][:, 0:16], m["ebias_d"])
    p.dma("sync", m["b2aug"][0:17, :], m["b2aug_d"])
    p.dma("sync", m["b1_all"][:], m["b1_all_d"])
    p.dma("sync", m["b3_all"][:], m["b3_all_d"])
    p.dma("sync", m["sb1"][:], m["sb1_d"])
    p.dma("sync", m["sb3"][:], m["sb3_d"])
    p.v(lambda e: e.memset(m["gatewT"][:], 1.0))
    m["c_ones"] = p.n["v"]
    for k in ["x1_paged", "lg_free", "gt_free", "x1o_paged", "lg_copy", "xn_used"]:
        m[k] = 0
    m["t2_free"] = [0, 0]
    m["sz_free"] = [0, 0]
    m["h_free"] = [0, 0]
    m["h_free_s"] = [0, 0]
    m["h3_free"] = [0, 0]
    m["y_free"] = [0, 0]
    m["tr_free"] = [0, 0]
    m["x1o_free"] = 0


def build_s3():
    nc = _new_nc()
    nd = D // 128
    ndi = DI // 128
    NE = 16
    m = {}
    m["x"] = nc.dram_tensor("x", [D, T], F32, kind="ExternalInput").ap()
    m["xn_s"] = nc.dram_tensor("xn_s", [D, T], BF16, kind="ExternalInput").ap()
    m["yg_s"] = nc.dram_tensor("yg_s", [DI, T], BF16, kind="ExternalInput").ap()
    m["z_s"] = nc.dram_tensor("z_s", [DI, T], BF16, kind="ExternalInput").ap()
    m["w_out_d"] = nc.dram_tensor("w_out", [DI, D], F32, kind="ExternalInput").ap()
    m["gate_wT_d"] = nc.dram_tensor("gate_wT", [D, NE], F32, kind="ExternalInput").ap()
    m["ebias_d"] = nc.dram_tensor("ebias", [128, NE], F32, kind="ExternalInput").ap()
    m["ident_d"] = nc.dram_tensor("ident", [128, 128], F32, kind="ExternalInput").ap()
    m["b2aug_d"] = nc.dram_tensor("b2aug", [17, D], F32, kind="ExternalInput").ap()
    m["b1_all_d"] = nc.dram_tensor("b1_all", [128, NE * 4], F32, kind="ExternalInput").ap()
    m["b3_all_d"] = nc.dram_tensor("b3_all", [128, NE * 4], F32, kind="ExternalInput").ap()
    m["sb1_d"] = nc.dram_tensor("sb1", [128, 8], F32, kind="ExternalInput").ap()
    m["sb3_d"] = nc.dram_tensor("sb3", [128, 8], F32, kind="ExternalInput").ap()
    m["w1T_d"] = [nc.dram_tensor(f"w1T_{e}", [D, DFF0], F32, kind="ExternalInput").ap() for e in range(NE)]
    m["w3T_d"] = [nc.dram_tensor(f"w3T_{e}", [D, DFF0], F32, kind="ExternalInput").ap() for e in range(NE)]
    m["w2T_d"] = [nc.dram_tensor(f"w2T_{e}", [DFF0, D], F32, kind="ExternalInput").ap() for e in range(NE)]
    m["sw1T_d"] = nc.dram_tensor("sw1T", [D, FDFF0], F32, kind="ExternalInput").ap()
    m["sw3T_d"] = nc.dram_tensor("sw3T", [D, FDFF0], F32, kind="ExternalInput").ap()
    m["sw2T_d"] = nc.dram_tensor("sw2T", [FDFF0, D], F32, kind="ExternalInput").ap()
    m["x1p_s"] = nc.dram_tensor("x1p_s", [D, T], F32, kind="ExternalOutput").ap()
    m["x1_s"] = nc.dram_tensor("x1_s", [D, T], F32, kind="ExternalOutput").ap()
    if S3_DEBUG:
        m["dbg_gatew"] = nc.dram_tensor("dbg_gatew", [128, (T // 128) * 16], F32, kind="ExternalOutput").ap()
        m["dbg_logitsT"] = nc.dram_tensor("dbg_logitsT", [16, T], F32, kind="ExternalOutput").ap()
        m["dbg_yacc"] = nc.dram_tensor("dbg_yacc", [128, (T // 128) * D], F32, kind="ExternalOutput").ap()
        m["dbg_m"] = nc.dram_tensor("dbg_m", [128, (T // 128) * 96], F32, kind="ExternalOutput").ap()
        m["dbg_c"] = nc.dram_tensor("dbg_c", [128, 128], F32, kind="ExternalOutput").ap()

    es = contextlib.ExitStack()
    with es:
        sb = lambda name, shape, dt: es.enter_context(nc.sbuf_tensor(name, shape, dt))
        pool_a = sb("pool_a", [128, ndi * D], BF16)
        m["w_out"] = pool_a[:, 0:ndi * D]
        m["gate_wT"] = sb("gate_wT_sb", [128, nd * NE], BF16)
        m["ident"] = sb("ident_sb", [128, 128], F32)
        m["ebias_rep"] = sb("ebias_rep", [128, NE], F32)
        m["b2aug"] = sb("b2aug_sb", [17, D], F32)
        m["b1_all"] = sb("b1_all_sb", [128, NE * 4], F32)
        m["b3_all"] = sb("b3_all_sb", [128, NE * 4], F32)
        m["sb1"] = sb("sb1_sb", [128, 8], F32)
        m["sb3"] = sb("sb3_sb", [128, 8], F32)
        m["x_ch"] = sb("x_ch", [128, nd * 256], F32)
        m["xn_ch"] = sb("xn_ch", [128, nd * 256], BF16)
        pool_b = sb("pool_b", [128, ndi * 256], BF16)
        m["yg_ch"] = pool_b[:, 0:ndi * 256]
        m["xn_e"] = pool_b[:, 0:nd * 512]
        m["z_ch"] = sb("z_ch3", [128, ndi * 256], BF16)
        m["sz_g"] = sb("sz_g3", [128, 512], F32)
        m["x1_ch"] = sb("x1_ch", [128, nd * 256], F32)
        m["scr"] = sb("scr", [128, 512], F32)
        m["logitsT"] = sb("logitsT", [16, T], F32)
        m["gatew"] = sb("gatew", [128, (T // 128) * NE], F32)
        m["gatewT"] = sb("gatewT", [17, T], F32)
        m["gb"] = sb("gb", [128, 7 * (T // 128) * NE], F32)
        m["gn"] = sb("gn", [128, 160], F32)
        if S3_DEBUG:
            m["dbgm_sb"] = sb("dbgm_sb", [128, (T // 128) * 96], F32)
            m["dbgc_sb"] = sb("dbgc_sb", [128, 128], F32)
        m["Y_acc"] = sb("Y_acc", [128, (T // 128) * D], F32)
        m["wp1"] = pool_a[:, 0:nd * FDFF0]
        m["wp3"] = pool_a[:, nd * FDFF0:2 * nd * FDFF0]
        m["wp2"] = sb("wp2", [128, (FDFF0 // 128) * D], BF16)
        m["H1"] = sb("H1", [128, 8 * 512], BF16)
        m["H3"] = sb("H3", [128, 8 * 512], BF16)
        m["Hg"] = sb("Hg", [128, 8 * 512], BF16)
        m["pb"] = [es.enter_context(nc.psum_tensor(f"pb{i}", [128, 512], F32)) for i in range(7)]

        sem_names = ["t", "v", "s", "g", "qg", "qs", "qa", "y"]
        sems = {k: es.enter_context(nc.semaphore(k)) for k in sem_names}

        def describe(p):
            s3_preamble(p, m)
            s3_body(p, m)

        with nc.Block() as block:
            for en in ENGINES:
                def mk(en):
                    def section(eng):
                        describe(Prog(nc, sems, active=en, eng=eng))
                    return section
                getattr(block, en)(mk(en))
    return nc


def s3_inmap(inputs, xT, s1_out, s2_out):
    NE = 16
    w1 = np.asarray(inputs["L0_ffn_w1"], np.float32)
    w3 = np.asarray(inputs["L0_ffn_w3"], np.float32)
    w2 = np.asarray(inputs["L0_ffn_w2"], np.float32)
    b1 = np.asarray(inputs["L0_ffn_b1"], np.float32)
    b3 = np.asarray(inputs["L0_ffn_b3"], np.float32)
    b2 = np.asarray(inputs["L0_ffn_b2"], np.float32)
    sb2 = np.asarray(inputs["L0_ffn_sb2"], np.float32)
    im = dict(
        x=xT,
        xn_s=s1_out["xn_s"],
        yg_s=s2_out["yg_s"],
        z_s=s1_out["z_s"],
        w_out=np.ascontiguousarray(np.asarray(inputs["L0_attn_w_out"], np.float32)),
        gate_wT=np.ascontiguousarray(np.asarray(inputs["L0_ffn_gate_w"], np.float32).T),
        ebias=np.ascontiguousarray(np.broadcast_to(
            np.asarray(inputs["L0_ffn_expert_bias"], np.float32).reshape(1, NE), (128, NE))),
        ident=np.eye(128, dtype=np.float32),
        b2aug=np.ascontiguousarray(np.concatenate([b2, sb2[None, :]], axis=0)),
        b1_all=np.ascontiguousarray(b1.reshape(NE, 4, 128).transpose(2, 0, 1).reshape(128, NE * 4)),
        b3_all=np.ascontiguousarray(b3.reshape(NE, 4, 128).transpose(2, 0, 1).reshape(128, NE * 4)),
        sb1=_col_tiles(inputs["L0_ffn_sb1"]),
        sb3=_col_tiles(inputs["L0_ffn_sb3"]),
        sw1T=np.ascontiguousarray(np.asarray(inputs["L0_ffn_sw1"], np.float32).T),
        sw3T=np.ascontiguousarray(np.asarray(inputs["L0_ffn_sw3"], np.float32).T),
        sw2T=np.ascontiguousarray(np.asarray(inputs["L0_ffn_sw2"], np.float32).T),
    )
    for e in range(NE):
        im[f"w1T_{e}"] = np.ascontiguousarray(w1[e].T)
        im[f"w3T_{e}"] = np.ascontiguousarray(w3[e].T)
        im[f"w2T_{e}"] = np.ascontiguousarray(w2[e].T)
    return im


NH = 16
HD = 64
NLM = 64


def s4_softmax_rows(p, m, dst_bf, src_f32, P_, W, rmx, rsum, rsinv, bias_c, tag):
    c_r = p.v(lambda e: e.reduce_max(rmx, src_f32, AX.X))
    c_b = p.v(lambda e: e.tensor_scalar(bias_c, rmx, -0.125, None, ALU.mult),
              wait=[("v", c_r, True)])
    c_e = p.s(lambda e: e.activation(src_f32, src_f32, AF.Exp, scale=0.125,
                                     bias=bias_c, accum_out=rsum), wait=[("v", c_b)])
    p.s(lambda e: e.activation(rsinv, rsum, AF.Ln))
    c_i = p.s(lambda e: e.activation(rsinv, rsinv, AF.Exp, scale=-1.0))
    return p.v(lambda e: e.tensor_scalar(dst_bf, src_f32, rsinv, None, ALU.mult),
               wait=[("s", c_i)])


def s4_body(p, m):
    nd = D // 128
    TC = 256
    NTT = T // 128
    snap = p.barrier()

    for c in range(T // TC):
        p.wait_snapshot(snap)
        x_t = m["x_ch"]
        c_x = p.dma("sync", x_t[:, 0:nd * TC].rearrange("p (i tc) -> p i tc", i=nd),
                    m["x1_s"][:, c * TC:(c + 1) * TC].rearrange("(i p) tc -> p i tc", p=128))
        x_tiles = [x_t[:, i * TC:(i + 1) * TC] for i in range(nd)]
        p.op("scalar", lambda e: e.nop(), wait=[("qs", c_x)])
        p.op("tensor", lambda e: e.nop(), wait=[("qs", c_x)])
        c_reps = ln_stats(p, m, TC, x_tiles, nd, m["onescol"][:, 0:1], m["sq"])
        c_xn_v = 0
        for i in range(nd):
            p.v(lambda e, i=i: e.tensor_tensor(m["sq"][:, i * TC:(i + 1) * TC],
                                               x_t[:, i * TC:(i + 1) * TC], m["mu_rep"][:, 0:TC], ALU.add),
                wait=[("v", c_reps)])
            c_xn_v = p.v(lambda e, i=i, c=c: e.tensor_tensor(
                m["xnl_ch"][:, i * TC:(i + 1) * TC],
                m["sq"][:, i * TC:(i + 1) * TC], m["mu_rep"][:, TC:2 * TC], ALU.mult),
                wait=[("qa", m.get("xnl_paged", 0))])
        m["xnl_paged"] = p.dma("scalar",
              m["xn1T_s"][:, c * TC:(c + 1) * TC].rearrange("(i p) tc -> p i tc", p=128),
              m["xnl_ch"][:, 0:nd * TC].rearrange("p (i tc) -> p i tc", i=nd),
              wait=[("v", c_xn_v)])
        snap = p.barrier()

    xn_tiles = None
    lanes = [dict(qk_free=[0, 0], v_free=[0, 0], k3_free=[0, 0], k3t_free=[0, 0],
                  k1_free=[0, 0], sv_free=[0, 0], k1s_free=0) for _ in range(2)]

    def pair_qkv(Ls, hpair):
        for lane in range(2):
            Ls[lane]["c_w"] = p.dma("gpsimd",
                m["lane"][lane]["wqkv_h"][:, 0:nd * 3 * HD].rearrange("p (i f) -> p i f", i=nd),
                m["wqkv_hd"][hpair * 2 + lane].rearrange("(i p) f -> p i f", p=128),
                wait=[("t", Ls[lane].get("wqkv_used", 0))])
        for ch in range(4):
            c_xn = p.dma("sync", m["xn_st"][ch % 2][:, 0:nd * 512].rearrange("p (i tc) -> p i tc", i=nd),
                         m["xn1T_s"][:, ch * 512:(ch + 1) * 512].rearrange("(i p) tc -> p i tc", p=128),
                         wait=[("t", m["xn_used"][ch % 2])])
            xs = [m["xn_st"][ch % 2][:, pi * 512:(pi + 1) * 512] for pi in range(nd)]
            for lane in range(2):
                L = Ls[lane]
                B = m["lane"][lane]
                p.op("tensor", lambda e: e.nop(), wait=[("qs", c_xn), ("qg", L["c_w"])])
                for which, dst in ((0, B["QT"]), (1, B["KT"])):
                    ps = m["pb"][lane * 2 + (which % 2)]
                    c_mm = tiled_mm(p, ps[0:HD, 0:512],
                                    [B["wqkv_h"][:, pi * 3 * HD + which * HD:pi * 3 * HD + (which + 1) * HD]
                                     for pi in range(nd)],
                                    xs, wait=[("s", L["qk_free"][which % 2])])
                    L["qk_free"][which % 2] = p.s(lambda e, dst=dst, ch=ch, ps=ps: e.activation(
                        dst[0:HD, ch * 512:(ch + 1) * 512], ps[0:HD, 0:512], AF.Identity),
                        wait=[("t", c_mm)])
                for j in range(4):
                    tt = ch * 4 + j
                    ps = m["pb"][lane * 2 + (j % 2)]
                    c_mm = tiled_mm(p, ps[:, 0:HD],
                                    [m["xn_st"][ch % 2][:, pi * 512 + j * 128:pi * 512 + (j + 1) * 128]
                                     for pi in range(nd)],
                                    [B["wqkv_h"][:, pi * 3 * HD + 2 * HD:pi * 3 * HD + 3 * HD]
                                     for pi in range(nd)],
                                    wait=[("s", L["v_free"][j % 2]), ("s", L["qk_free"][0]),
                                          ("s", L["qk_free"][1])])
                    L["v_free"][j % 2] = p.s(lambda e, tt=tt, ps=ps: e.activation(
                        B["Vn"][:, tt * HD:(tt + 1) * HD], ps[:, 0:HD], AF.Identity),
                        wait=[("t", c_mm)])
            m["xn_used"][ch % 2] = p.n["t"]
        for lane in range(2):
            Ls[lane]["c_qkv_s"] = p.n["s"]

    def head_steps(L, h, lane):
        B = m["lane"][lane]
        yield "landmarks"
        c_klm = p.v(lambda e: e.tensor_reduce(B["lmf"][0:HD, 0:NLM],
                                              B["KT"][0:HD, 0:T].rearrange("p (l g) -> p l g", g=32),
                                              AX.X, ALU.add), wait=[("s", L["c_qkv_s"])])
        p.v(lambda e: e.tensor_scalar(B["klm"][0:HD, 0:NLM], B["lmf"][0:HD, 0:NLM],
                                      1.0 / 32, None, ALU.mult), wait=[("v", c_klm, True)])
        c_qlm = p.v(lambda e: e.tensor_reduce(B["lmf"][0:HD, NLM:2 * NLM],
                                              B["QT"][0:HD, 0:T].rearrange("p (l g) -> p l g", g=32),
                                              AX.X, ALU.add))
        L["c_lm"] = p.v(lambda e: e.tensor_scalar(B["qlm"][0:HD, 0:NLM], B["lmf"][0:HD, NLM:2 * NLM],
                                                  1.0 / 32, None, ALU.mult), wait=[("v", c_qlm, True)])
        yield "k2"
        c_k2 = p.t(lambda e: e.matmul(m["pb"][4 + lane][0:NLM, 0:NLM], B["qlm"][0:HD, 0:NLM],
                                      B["klm"][0:HD, 0:NLM], start=True, stop=True),
                   wait=[("v", L["c_lm"]), ("v", L.get("k2ps_free", 0))])
        c_k2e = p.v(lambda e: e.tensor_copy(B["K2"][0:NLM, 0:NLM], m["pb"][4 + lane][0:NLM, 0:NLM]),
                    wait=[("t", c_k2)])
        nsc = B["nsc"]
        c_sm2 = s4_softmax_rows(p, m, B["K2n"][0:NLM, 0:NLM], B["K2"][0:NLM, 0:NLM],
                                NLM, NLM, nsc[0:NLM, 0:1], nsc[0:NLM, 1:2], nsc[0:NLM, 2:3],
                                nsc[0:NLM, 3:4], "k2")
        c_k2t = ptrans(p, m["pb"][4 + lane][0:NLM, 0:NLM], B["K2n"][0:NLM, 0:NLM], m["ident"],
                       wait=[("v", c_sm2)])
        c_k2tc = p.v(lambda e: e.tensor_copy(B["K2T"][0:NLM, 0:NLM], m["pb"][4 + lane][0:NLM, 0:NLM]),
                     wait=[("t", c_k2t)])
        c_cs = p.t(lambda e: e.matmul(m["pb"][4 + lane][0:1, 0:NLM], m["onescol"][0:NLM, 0:1],
                                      B["K2n"][0:NLM, 0:NLM], start=True, stop=True),
                   wait=[("v", c_sm2), ("v", c_k2tc)])
        c_csv = p.v(lambda e: e.tensor_copy(B["nrow"][0:1, 0:NLM], m["pb"][4 + lane][0:1, 0:NLM]),
                    wait=[("t", c_cs)])
        c_mx = p.v(lambda e: e.reduce_max(B["nrow"][0:1, NLM:NLM + 1], B["nrow"][0:1, 0:NLM], AX.X),
                   wait=[("v", c_csv, True)])
        p.s(lambda e: e.activation(B["nrow"][0:1, NLM + 1:NLM + 2], B["nrow"][0:1, NLM:NLM + 1], AF.Ln),
            wait=[("v", c_mx)])
        c_dd = p.s(lambda e: e.activation(B["nrow"][0:1, NLM + 2:NLM + 3],
                                          B["nrow"][0:1, NLM + 1:NLM + 2], AF.Exp, scale=-1.0))
        c_ddr = p.t(lambda e: e.matmul(m["pb"][4 + lane][0:NLM, 0:1], m["onesrow"][0:1, 0:NLM],
                                       B["nrow"][0:1, NLM + 2:NLM + 3], start=True, stop=True),
                    wait=[("s", c_dd)])
        c_ddc = p.v(lambda e: e.tensor_copy(B["nsc"][0:NLM, 4:5], m["pb"][4 + lane][0:NLM, 0:1]),
                    wait=[("t", c_ddr)])
        cur = B["NV0"]
        L["c_prev"] = p.v(lambda e, cur=cur: e.tensor_scalar(cur[0:NLM, 0:NLM], B["K2T"][0:NLM, 0:NLM],
                                                             B["nsc"][0:NLM, 4:5], None, ALU.mult),
                          wait=[("v", c_ddc, True)])
        L["cur"] = cur
        L["k2ps_free"] = L["c_prev"]
        for it in range(6):
            yield f"newton{it}"
            cur = L["cur"]
            nxt = B["NV1"] if cur is B["NV0"] else B["NV0"]
            c_kv = p.t(lambda e, cur=cur: e.matmul(m["pb"][4 + lane][0:NLM, 0:NLM], B["K2T"][0:NLM, 0:NLM],
                                                   cur[0:NLM, 0:NLM], start=True, stop=True),
                       wait=[("v", L["c_prev"])])
            c_kve = p.v(lambda e: e.tensor_copy(B["KV"][0:NLM, 0:NLM], m["pb"][4 + lane][0:NLM, 0:NLM]),
                        wait=[("t", c_kv)])
            c_kvt = ptrans(p, m["pb"][4 + lane][0:NLM, 0:NLM], B["KV"][0:NLM, 0:NLM], m["ident"],
                           wait=[("v", c_kve)])
            c_kvte = p.v(lambda e: e.tensor_copy(B["KVT"][0:NLM, 0:NLM], m["pb"][4 + lane][0:NLM, 0:NLM]),
                         wait=[("t", c_kvt)])
            c_m1 = p.v(lambda e: e.scalar_tensor_tensor(B["NM"][0:NLM, 0:NLM], B["KV"][0:NLM, 0:NLM],
                                                        -1.0, m["I7"][0:NLM, 0:NLM], ALU.mult, ALU.add),
                       wait=[("v", c_kve, True)])
            c_m2 = p.t(lambda e: e.matmul(m["pb"][4 + lane][0:NLM, 0:NLM], B["KVT"][0:NLM, 0:NLM],
                                          B["NM"][0:NLM, 0:NLM], start=True, stop=True),
                       wait=[("v", max(c_m1, c_kvte))])
            c_m2e = p.v(lambda e: e.tensor_copy(B["NM2"][0:NLM, 0:NLM], m["pb"][4 + lane][0:NLM, 0:NLM]),
                        wait=[("t", c_m2)])
            c_m3 = p.v(lambda e: e.scalar_tensor_tensor(B["NM"][0:NLM, 0:NLM], B["NM2"][0:NLM, 0:NLM],
                                                        -1.0, m["I15"][0:NLM, 0:NLM], ALU.mult, ALU.add),
                       wait=[("v", c_m2e, True)])
            c_m4 = p.t(lambda e: e.matmul(m["pb"][4 + lane][0:NLM, 0:NLM], B["KVT"][0:NLM, 0:NLM],
                                          B["NM"][0:NLM, 0:NLM], start=True, stop=True),
                       wait=[("v", c_m3)])
            c_m4e = p.v(lambda e: e.tensor_copy(B["NM2"][0:NLM, 0:NLM], m["pb"][4 + lane][0:NLM, 0:NLM]),
                        wait=[("t", c_m4)])
            c_m5 = p.v(lambda e: e.scalar_tensor_tensor(B["NM"][0:NLM, 0:NLM], B["NM2"][0:NLM, 0:NLM],
                                                        -1.0, m["I13"][0:NLM, 0:NLM], ALU.mult, ALU.add),
                       wait=[("v", c_m4e, True)])
            c_vt = ptrans(p, m["pb"][4 + lane][0:NLM, 0:NLM], cur[0:NLM, 0:NLM], m["ident"],
                          wait=[("v", L["c_prev"]), ("v", c_m5)])
            c_vte = p.v(lambda e: e.tensor_copy(B["NVT"][0:NLM, 0:NLM], m["pb"][4 + lane][0:NLM, 0:NLM]),
                        wait=[("t", c_vt)])
            c_vn = p.t(lambda e: e.matmul(m["pb"][4 + lane][0:NLM, 0:NLM], B["NVT"][0:NLM, 0:NLM],
                                          B["NM"][0:NLM, 0:NLM], start=True, stop=True),
                       wait=[("v", max(c_m5, c_vte))])
            L["c_prev"] = p.v(lambda e, nxt=nxt: e.tensor_scalar(nxt[0:NLM, 0:NLM],
                                                                 m["pb"][4 + lane][0:NLM, 0:NLM],
                                                                 0.25, None, ALU.mult),
                              wait=[("t", c_vn)])
            L["cur"] = nxt
        yield "inv"
        c_it = ptrans(p, m["pb"][4 + lane][0:NLM, 0:NLM], L["cur"][0:NLM, 0:NLM], m["ident"],
                      wait=[("v", L["c_prev"])])
        L["c_inv"] = p.v(lambda e: e.tensor_copy(B["NVT"][0:NLM, 0:NLM], m["pb"][4 + lane][0:NLM, 0:NLM]),
                         wait=[("t", c_it)])
        yield "k3"
        for ch in range(4):
            c_mm = p.t(lambda e, ch=ch: e.matmul(m["pb"][lane * 2 + ch % 2][0:NLM, 0:512],
                                                 B["qlm"][0:HD, 0:NLM],
                                                 B["KT"][0:HD, ch * 512:(ch + 1) * 512],
                                                 start=True, stop=True),
                       wait=[("v", L["k3_free"][ch % 2]), ("v", L["c_lm"])])
            L["k3_free"][ch % 2] = p.v(lambda e, ch=ch: e.tensor_copy(
                B["K3"][0:NLM, ch * 512:(ch + 1) * 512], m["pb"][lane * 2 + ch % 2][0:NLM, 0:512]),
                wait=[("t", c_mm)])
        nsc = B["nsc"]
        c_sm3 = s4_softmax_rows(p, m, B["K3b"][0:NLM, 0:T], B["K3"][0:NLM, 0:T],
                                NLM, T, nsc[0:NLM, 5:6], nsc[0:NLM, 6:7], nsc[0:NLM, 7:8],
                                nsc[0:NLM, 8:9], "k3")
        yield "k3v"
        for tt in range(NTT):
            c_tr = ptrans(p, m["pb"][lane * 2 + tt % 2][:, 0:NLM], B["K3b"][0:NLM, tt * 128:(tt + 1) * 128],
                          m["identb"], wait=[("v", c_sm3), ("v", L["k3t_free"][tt % 2])])
            L["k3t_free"][tt % 2] = p.v(lambda e, tt=tt: e.tensor_copy(
                B["K3T"][:, tt * NLM:(tt + 1) * NLM], m["pb"][lane * 2 + tt % 2][:, 0:NLM]),
                wait=[("t", c_tr)])
        c_k3t = p.n["v"]
        c_k3v = tiled_mm(p, m["pb"][4 + lane][0:NLM, 0:HD],
                         [B["K3T"][:, tt * NLM:(tt + 1) * NLM] for tt in range(NTT)],
                         [B["Vn"][:, tt * HD:(tt + 1) * HD] for tt in range(NTT)],
                         wait=[("v", c_k3t), ("v", L["c_inv"])])
        c_k3ve = p.v(lambda e: e.tensor_copy(B["K3V"][0:NLM, 0:HD], m["pb"][4 + lane][0:NLM, 0:HD]),
                     wait=[("t", c_k3v)])
        c_wh = p.t(lambda e: e.matmul(m["pb"][4 + lane][0:NLM, 0:HD], B["NVT"][0:NLM, 0:NLM],
                                      B["K3V"][0:NLM, 0:HD], start=True, stop=True),
                   wait=[("v", max(L["c_inv"], c_k3ve))])
        L["c_whe"] = p.v(lambda e: e.tensor_copy(B["Wh"][0:NLM, 0:HD], m["pb"][4 + lane][0:NLM, 0:HD]),
                         wait=[("t", c_wh)])
        yield "k1"
        k1s_free = [L["k1s_free"], L["k1s_free"]]
        k1n_free = [0, 0]
        for tt in range(NTT):
            par = tt % 2
            ps = m["pb"][lane * 2 + par]
            K1t = B["K1t"][:, par * NLM:(par + 1) * NLM]
            nc1 = B["k1sc"][:, par * 8:(par + 1) * 8]
            c_mm = p.t(lambda e, tt=tt, ps=ps: e.matmul(
                ps[:, 0:NLM], B["QT"][0:HD, tt * 128:(tt + 1) * 128],
                B["klm"][0:HD, 0:NLM], start=True, stop=True),
                wait=[("v", L["k1_free"][par])])
            c_cp = p.v(lambda e, ps=ps, K1t=K1t: e.tensor_copy(K1t, ps[:, 0:NLM]),
                       wait=[("t", c_mm), ("s", k1s_free[par])])
            c_r = p.v(lambda e, K1t=K1t, nc1=nc1: e.reduce_max(nc1[:, 0:1], K1t, AX.X),
                      wait=[("v", c_cp, True)])
            c_b = p.v(lambda e, nc1=nc1: e.tensor_scalar(nc1[:, 1:2], nc1[:, 0:1], -0.125, None, ALU.mult),
                      wait=[("v", c_r, True)])
            c_e = p.s(lambda e, K1t=K1t, nc1=nc1: e.activation(K1t, K1t, AF.Exp,
                                             scale=0.125, bias=nc1[:, 1:2], accum_out=nc1[:, 2:3]),
                      wait=[("v", c_b)])
            p.s(lambda e, nc1=nc1: e.activation(nc1[:, 3:4], nc1[:, 2:3], AF.Ln))
            c_i = p.s(lambda e, nc1=nc1: e.activation(nc1[:, 4:5], nc1[:, 3:4], AF.Exp, scale=-1.0))
            k1s_free[par] = c_i
            K1n = B["K1n"][:, par * NLM:(par + 1) * NLM]
            c_n = p.v(lambda e, K1t=K1t, K1n=K1n, nc1=nc1: e.tensor_scalar(
                K1n, K1t, nc1[:, 4:5], None, ALU.mult),
                wait=[("s", c_i), ("v", k1n_free[par])])
            c_tr = ptrans(p, m["pb"][4 + lane][0:NLM, 0:128], K1n, m["identb"],
                          wait=[("v", c_n)])
            k1n_free[par] = L["k1_free"][par] = p.v(lambda e, tt=tt: e.tensor_copy(
                B["K1T"][0:NLM, tt * 128:(tt + 1) * 128], m["pb"][4 + lane][0:NLM, 0:128]),
                wait=[("t", c_tr)])
        L["k1s_free"] = max(k1s_free)
        c_k1 = p.n["v"]
        yield "sv"
        sv_pbase = (h % 2) * HD
        sv_tile = h // 2
        for ch in range(4):
            c_mm = p.t(lambda e, ch=ch: e.matmul(
                m["pb"][lane * 2 + ch % 2][0:HD, 0:512], B["Wh"][0:NLM, 0:HD],
                B["K1T"][0:NLM, ch * 512:(ch + 1) * 512], start=True, stop=True),
                wait=[("v", max(c_k1, L["sv_free"][ch % 2], L["c_whe"]))])
            L["sv_free"][ch % 2] = p.v(
                lambda e, ch=ch, sv_pbase=sv_pbase, sv_tile=sv_tile: e.tensor_copy(
                    m["SVT"][sv_pbase:sv_pbase + HD, sv_tile * T + ch * 512:sv_tile * T + (ch + 1) * 512],
                    m["pb"][lane * 2 + ch % 2][0:HD, 0:512]),
                wait=[("t", c_mm)])
        L["wqkv_used"] = p.n["t"]
        yield "done"

    m["xn_used"] = [0, 0]
    for hp in range(NH // 2):
        pair_qkv(lanes, hp)
        gens = [head_steps(lanes[0], 2 * hp, 0), head_steps(lanes[1], 2 * hp + 1, 1)]
        done = [False, False]
        while not all(done):
            for ln in range(2):
                if not done[ln]:
                    try:
                        next(gens[ln])
                    except StopIteration:
                        done[ln] = True
    snap = p.barrier()

    if m.get("defer_wproj"):
        p.wait_snapshot(snap)
        c_wp = 0
        for pi in range(nd):
            c_wp = p.dma("gpsimd", m["w_proj"][:, pi * D:(pi + 1) * D],
                         m["w_proj_d"][pi * 128:(pi + 1) * 128, :])
        p.op("tensor", lambda e: e.nop(), wait=[("qg", c_wp)])
    svt_tiles = [m["SVT"][:, i * T:(i + 1) * T] for i in range(nd)]
    for c in range(T // TC):
        p.wait_snapshot(snap)
        x_t = m["x_ch"]
        c_x = p.dma("sync", x_t[:, 0:nd * TC].rearrange("p (i tc) -> p i tc", i=nd),
                    m["x1_s"][:, c * TC:(c + 1) * TC].rearrange("(i p) tc -> p i tc", p=128))
        p.op("tensor", lambda e: e.nop(), wait=[("qs", c_x)])
        ps_free = [0, 0]
        for po in range(nd):
            ps = m["pb"][po % 2]
            c_mm = tiled_mm(p, ps[:, 0:TC],
                            [m["w_proj"][:, pi * D + po * 128:pi * D + (po + 1) * 128] for pi in range(nd)],
                            [svt_tiles[pi][:, c * TC:(c + 1) * TC] for pi in range(nd)],
                            wait=[("v", ps_free[po % 2])])
            ps_free[po % 2] = p.v(lambda e, po=po, ps=ps: e.tensor_tensor(
                m["xa_ch"][:, po * TC:(po + 1) * TC], ps[:, 0:TC],
                x_t[:, po * TC:(po + 1) * TC], ALU.add),
                wait=[("t", c_mm), ("qa", m["xa_paged"])])
        c_xa = p.n["v"]
        m["xa_paged"] = p.dma("scalar",
                              m["xa_s"][:, c * TC:(c + 1) * TC].rearrange("(i p) tc -> p i tc", p=128),
                              m["xa_ch"][:, 0:nd * TC].rearrange("p (i tc) -> p i tc", i=nd),
                              wait=[("v", c_xa)])
        snap = p.barrier()
    p.op("sync", lambda e: e.nop(), wait=[("qa", p.n["qa"]), ("qs", p.n["qs"]), ("qg", p.n["qg"])])


def s4_preamble(p, m):
    nd = D // 128
    if not m.get("defer_wproj"):
        for pi in range(nd):
            p.dma("gpsimd", m["w_proj"][:, pi * D:(pi + 1) * D], m["w_proj_d"][pi * 128:(pi + 1) * 128, :])
    p.dma("sync", m["ident"][:], m["ident_d"])
    p.dma("gpsimd", m["identb"][:], m["ident_d"])
    p.dma("sync", m["I7"][0:NLM, 0:NLM], m["newc_d"][:, 0:NLM])
    p.dma("sync", m["I15"][0:NLM, 0:NLM], m["newc_d"][:, NLM:2 * NLM])
    p.dma("sync", m["I13"][0:NLM, 0:NLM], m["newc_d"][:, 2 * NLM:3 * NLM])
    p.v(lambda e: e.memset(m["onescol"][:], 1.0))
    p.v(lambda e: e.memset(m["onesrow"][:], 1.0))
    m["qk_free"] = [0, 0]
    m["v_free"] = [0, 0]
    m["k3_free"] = [0, 0]
    m["k3t_free"] = [0, 0]
    m["k1_free"] = [0, 0]
    m["sv_free"] = [0, 0]
    m["k1s_free"] = 0
    m["xa_paged"] = 0
    m["xnl_paged"] = 0


def build_s4():
    nc = _new_nc()
    nd = D // 128
    NTT = T // 128
    m = {}
    m["x1_s"] = nc.dram_tensor("x1_s", [D, T], F32, kind="ExternalInput").ap()
    m["w_proj_d"] = nc.dram_tensor("w_proj", [D, D], F32, kind="ExternalInput").ap()
    m["wqkv_hd"] = [nc.dram_tensor(f"wqkv_{h}", [D, 3 * HD], F32, kind="ExternalInput").ap()
                    for h in range(NH)]
    m["ident_d"] = nc.dram_tensor("ident", [128, 128], F32, kind="ExternalInput").ap()
    m["newc_d"] = nc.dram_tensor("newc", [NLM, 3 * NLM], F32, kind="ExternalInput").ap()
    m["xn1T_s"] = nc.dram_tensor("xn1T_s", [D, T], BF16, kind="ExternalOutput").ap()
    m["xa_s"] = nc.dram_tensor("xa_s", [D, T], F32, kind="ExternalOutput").ap()

    es = contextlib.ExitStack()
    with es:
        sb = lambda name, shape, dt: es.enter_context(nc.sbuf_tensor(name, shape, dt))
        m["w_proj"] = sb("w_proj_sb", [128, nd * D], BF16)
        m["ident"] = sb("ident_sb", [128, 128], F32)
        m["identb"] = sb("identb", [128, 128], BF16)
        m["I7"] = sb("I7", [NLM, NLM], F32)
        m["I15"] = sb("I15", [NLM, NLM], F32)
        m["I13"] = sb("I13", [NLM, NLM], F32)
        m["SVT"] = sb("SVT", [128, nd * T], BF16)
        m["xn_st"] = [sb(f"xn_st{i}", [128, nd * 512], BF16) for i in range(2)]
        m["xnl_ch"] = sb("xnl_ch", [128, nd * 256], BF16)
        m["x_ch"] = sb("x_ch", [128, nd * 256], F32)
        m["xa_ch"] = sb("xa_ch", [128, nd * 256], F32)
        m["sq"] = sb("sq", [128, nd * 256], F32)
        m["row0"] = sb("row0", [1, 256], F32)
        m["row1"] = sb("row1", [1, 256], F32)
        m["row2"] = sb("row2", [1, 256], F32)
        m["mu_rep"] = sb("mu_rep", [128, 2 * 256], F32)
        m["onescol"] = sb("onescol", [128, 1], F32)
        m["onescol_b"] = sb("onescol_b", [128, 1], BF16)
        m["onesrow"] = sb("onesrow", [1, 128], F32)
        m["lane"] = []
        for ln in range(2):
            B = {}
            B["wqkv_h"] = sb(f"wqkv_h{ln}", [128, nd * 3 * HD], BF16)
            B["QT"] = sb(f"QT{ln}", [NLM, T], BF16)
            B["KT"] = sb(f"KT{ln}", [NLM, T], BF16)
            B["Vn"] = sb(f"Vn{ln}", [128, (T // 128) * HD], BF16)
            B["lmf"] = sb(f"lmf{ln}", [NLM, 2 * NLM], F32)
            B["klm"] = sb(f"klm{ln}", [NLM, NLM], BF16)
            B["qlm"] = sb(f"qlm{ln}", [NLM, NLM], BF16)
            B["K2"] = sb(f"K2_{ln}", [NLM, NLM], F32)
            B["K2n"] = sb(f"K2n{ln}", [NLM, NLM], F32)
            B["K2T"] = sb(f"K2T{ln}", [NLM, NLM], F32)
            B["NV0"] = sb(f"NV0{ln}", [NLM, NLM], F32)
            B["NV1"] = sb(f"NV1{ln}", [NLM, NLM], F32)
            B["KV"] = sb(f"KV{ln}", [NLM, NLM], F32)
            B["KVT"] = sb(f"KVT{ln}", [NLM, NLM], F32)
            B["NM"] = sb(f"NM{ln}", [NLM, NLM], F32)
            B["NM2"] = sb(f"NM2{ln}", [NLM, NLM], F32)
            B["NVT"] = sb(f"NVT{ln}", [NLM, NLM], F32)
            B["nsc"] = sb(f"nsc{ln}", [NLM, 16], F32)
            B["nrow"] = sb(f"nrow{ln}", [1, NLM + 4], F32)
            B["K3"] = sb(f"K3_{ln}", [NLM, T], F32)
            kb = sb(f"k3k1_{ln}", [NLM, T], BF16)
            B["K3b"] = kb[0:NLM, 0:T]
            B["K1T"] = kb[0:NLM, 0:T]
            B["K3T"] = sb(f"K3T{ln}", [128, (T // 128) * NLM], BF16)
            B["K3V"] = sb(f"K3V{ln}", [NLM, HD], F32)
            B["Wh"] = sb(f"Wh{ln}", [NLM, HD], BF16)
            B["K1t"] = sb(f"K1t{ln}", [128, 2 * NLM], F32)
            B["K1n"] = sb(f"K1n{ln}", [128, 2 * NLM], BF16)
            B["k1sc"] = sb(f"k1sc{ln}", [128, 16], F32)
            m["lane"].append(B)
        m["pb"] = [es.enter_context(nc.psum_tensor(f"pb{i}", [128, 512], F32)) for i in range(7)]
        m["ps_r0"], m["ps_r1"], m["ps_rep"], m["ps_rep2"] = m["pb"][3], m["pb"][4], m["pb"][5], m["pb"][6]

        sem_names = ["t", "v", "s", "g", "qg", "qs", "qa", "y"]
        sems = {k: es.enter_context(nc.semaphore(k)) for k in sem_names}

        def describe(p):
            s4_preamble(p, m)
            s4_body(p, m)

        with nc.Block() as block:
            for en in ENGINES:
                def mk(en):
                    def section(eng):
                        describe(Prog(nc, sems, active=en, eng=eng))
                    return section
                getattr(block, en)(mk(en))
    return nc


def s4_inmap(inputs, x1):
    w_qkv = np.asarray(inputs["L1_attn_w_qkv"], np.float32)
    eye = np.eye(NLM, dtype=np.float32)
    im = dict(
        x1_s=x1,
        w_proj=np.ascontiguousarray(np.asarray(inputs["L1_attn_w_proj"], np.float32)),
        ident=np.eye(128, dtype=np.float32),
        newc=np.ascontiguousarray(np.concatenate([7 * eye, 15 * eye, 13 * eye], axis=1)),
    )
    for h in range(NH):
        im[f"wqkv_{h}"] = np.ascontiguousarray(
            np.concatenate([w_qkv[:, h * HD:(h + 1) * HD],
                            w_qkv[:, D + h * HD:D + (h + 1) * HD],
                            w_qkv[:, 2 * D + h * HD:2 * D + (h + 1) * HD]], axis=1))
    return im


NE1 = 8
DFF1 = 4096


def s5_body(p, m):
    nd = D // 128
    NTT = T // 128
    TCE = 256
    snap = p.barrier()

    for c in range(4):
        p.wait_snapshot(snap)
        c_xn = p.dma("sync", m["xn_e"][:, 0:nd * 512].rearrange("p (i tc) -> p i tc", i=nd),
                     m["xn1T_s"][:, c * 512:(c + 1) * 512].rearrange("(i p) tc -> p i tc", p=128))
        p.op("tensor", lambda e: e.nop(), wait=[("qs", c_xn)])
        c_lg = tiled_mm(p, m["pb"][2][0:NE1, 0:512],
                        [m["Wg"][:, pi * NE1:(pi + 1) * NE1] for pi in range(nd)],
                        [m["xn_e"][:, pi * 512:(pi + 1) * 512] for pi in range(nd)],
                        wait=[("v", m["lg_free"])])
        m["lg_free"] = p.v(lambda e, c=c: e.tensor_scalar(
            m["logitsT"][0:NE1, c * 512:(c + 1) * 512], m["pb"][2][0:NE1, 0:512],
            m["bg"][0:NE1, 0:1], None, ALU.add), wait=[("t", c_lg)])
        snap = p.barrier()
    NTE = NTT * NE1
    LG = m["gb"][:, 0 * NTE:1 * NTE]
    BI = LG
    M1 = m["gb"][:, 1 * NTE:2 * NTE]
    M2 = m["gb"][:, 2 * NTE:3 * NTE]
    TM = m["gb"][:, 3 * NTE:4 * NTE]
    G1 = m["gb"][:, 4 * NTE:5 * NTE]
    V1 = m["gn"][:, 0:16]
    V2 = m["gn"][:, 16:32]
    P1 = m["gn"][:, 32:48]
    P2 = m["gn"][:, 48:64]
    DF = m["gn"][:, 64:80]
    d3 = lambda ap: ap.rearrange("p (o e) -> p o e", e=NE1)
    bcn = lambda ap: ap.rearrange("p (o e) -> p o e", e=1).broadcast_to((128, NTT, NE1))
    for tt in range(NTT):
        c_tr = ptrans(p, m["pb"][3][:, 0:NE1], m["logitsT"][0:NE1, tt * 128:(tt + 1) * 128],
                      m["ident"], wait=[("v", m["lg_free"]), ("v", m["lg_copy"])])
        m["lg_copy"] = p.v(lambda e, tt=tt: e.tensor_copy(LG[:, tt * NE1:(tt + 1) * NE1],
                                                          m["pb"][3][:, 0:NE1]), wait=[("t", c_tr)])
    c_v1 = p.v(lambda e: e.tensor_reduce(V1, d3(BI), AX.X, ALU.max),
               wait=[("v", m["lg_copy"], True)])
    p.v(lambda e: e.tensor_tensor(d3(M1), d3(BI), bcn(V1), ALU.is_ge), wait=[("v", c_v1, True)])
    p.v(lambda e: e.scalar_tensor_tensor(TM, M1, -1e30, BI, ALU.mult, ALU.add))
    c_v2 = p.v(lambda e: e.tensor_reduce(V2, d3(TM), AX.X, ALU.max))
    p.v(lambda e: e.tensor_tensor(d3(M2), d3(TM), bcn(V2), ALU.is_ge), wait=[("v", c_v2, True)])
    c_df = p.v(lambda e: e.tensor_tensor(DF, V1, V2, ALU.subtract))
    c_p1 = p.s(lambda e: e.activation(P1, DF, AF.Sigmoid), wait=[("v", c_df)])
    c_p2 = p.v(lambda e: e.tensor_scalar(P2, P1, -1.0, 1.0, ALU.mult, ALU.add),
               wait=[("s", c_p1)])
    p.v(lambda e: e.tensor_tensor(d3(G1), d3(M1), bcn(P1), ALU.mult), wait=[("v", c_p2, True)])
    p.v(lambda e: e.tensor_tensor(d3(TM), d3(M2), bcn(P2), ALU.mult))
    c_gw = p.v(lambda e: e.tensor_tensor(m["comb"][:, 0:NTE], G1, TM, ALU.add))
    for tt in range(NTT):
        c_tr2 = ptrans(p, m["pb"][4][0:NE1, 0:128], m["comb"][:, tt * NE1:(tt + 1) * NE1],
                       m["ident"], wait=[("v", c_gw), ("s", m["ct_free"])])
        m["ct_free"] = p.s(lambda e, tt=tt: e.activation(
            m["combT"][0:NE1, tt * 128:(tt + 1) * 128], m["pb"][4][0:NE1, 0:128], AF.Identity),
            wait=[("t", c_tr2)])
    for tt in range(NTT):
        for hh in range(2):
            c_mm = p.t(lambda e, tt=tt, hh=hh: e.matmul(
                m["pb"][tt % 2][:, 0:512], m["combT"][0:NE1, tt * 128:(tt + 1) * 128],
                m["b2all"][0:NE1, hh * 512:(hh + 1) * 512], start=True, stop=True),
                wait=[("v", m["t2_free"][tt % 2]), ("s", m["ct_free"])])
            m["t2_free"][tt % 2] = p.v(lambda e, tt=tt, hh=hh: e.tensor_copy(
                m["Y_acc"][:, tt * D + hh * 512:tt * D + (hh + 1) * 512], m["pb"][tt % 2][:, 0:512]),
                wait=[("t", c_mm)])
    snap = p.barrier()

    first = True
    for e_i in range(NE1):
        for hh in range(2):
            nf = 16
            if first:
                p.wait_snapshot(snap)
                first = False
            c_w1 = p.dma("gpsimd", m["wp1"][:, 0:nd * 2048].rearrange("p (i f) -> p i f", i=nd),
                         m["W1_d"][e_i].rearrange("(i p) f -> p i f", p=128)[:, :, hh * 2048:(hh + 1) * 2048],
                         wait=[("t", m["w1_used"])])
            c_w2 = p.dma("gpsimd", m["wp2"][:, 0:nf * D].rearrange("p (i f) -> p i f", i=nf),
                         m["W2_d"][e_i][hh * 2048:(hh + 1) * 2048, :].rearrange("(i p) f -> p i f", p=128),
                         wait=[("t", m["w2_used"])])
            for c in range(T // TCE):
                c_xn = p.dma("sync", m["xn_e"][:, 0:nd * TCE].rearrange("p (i tc) -> p i tc", i=nd),
                             m["xn1T_s"][:, c * TCE:(c + 1) * TCE].rearrange("(i p) tc -> p i tc", p=128),
                             wait=[("t", m["xn_used"])])
                p.op("tensor", lambda e: e.nop(), wait=[("qs", c_xn), ("qg", c_w2)])
                for po in range(nf):
                    ps = m["pb"][po % 2]
                    c_mm = tiled_mm(p, ps[:, 0:TCE],
                                    [m["wp1"][:, pi * 2048 + po * 128:pi * 2048 + (po + 1) * 128]
                                     for pi in range(nd)],
                                    [m["xn_e"][:, pi * TCE:(pi + 1) * TCE] for pi in range(nd)],
                                    wait=[("s", m["h_free_s"][po % 2]), ("t", m["hg_used"].get(po % 2, 0))])
                    bcol = m["b1c"][:, e_i * 32 + hh * 16 + po:e_i * 32 + hh * 16 + po + 1]
                    m["h_free_s"][po % 2] = p.s(lambda e, po=po, ps=ps, bcol=bcol: e.activation(
                        m["H1"][:, po * TCE:(po + 1) * TCE], ps[:, 0:TCE], AF.Gelu, bias=bcol),
                        wait=[("t", c_mm)])
                c_h = p.n["s"]
                m["xn_used"] = m["w1_used"] = p.n["t"]
                for j in range(TCE // 128):
                    gtile = c * (TCE // 128) + j
                    for half in range(2):
                        ps = m["pb"][2 + (j * 2 + half) % 2]
                        c_mm = tiled_mm(p, ps[:, 0:512],
                                        [m["H1"][:, pi * TCE + j * 128:pi * TCE + (j + 1) * 128]
                                         for pi in range(nf)],
                                        [m["wp2"][:, pi * D + half * 512:pi * D + (half + 1) * 512]
                                         for pi in range(nf)],
                                        wait=[("s", c_h), ("v", m["y_free"][(j * 2 + half) % 2])])
                        dst = m["Y_acc"][:, gtile * D + half * 512:gtile * D + (half + 1) * 512]
                        gcol = m["comb"][:, gtile * NE1 + e_i:gtile * NE1 + e_i + 1]
                        m["y_free"][(j * 2 + half) % 2] = p.v(
                            lambda e, dst=dst, ps=ps, gcol=gcol: e.scalar_tensor_tensor(
                                dst, ps[:, 0:512], gcol, dst, ALU.mult, ALU.add),
                            wait=[("t", c_mm)])
                m["hg_used"][0] = m["hg_used"][1] = m["w2_used"] = p.n["t"]
    snap = p.barrier()

    for tt in range(NTT):
        p.wait_snapshot(snap)
        c_l1 = p.dma("sync", m["xn_e"][:, 0:nd * 128].rearrange("p (i tc) -> p i tc", i=nd),
                     m["xn1T_s"][:, tt * 128:(tt + 1) * 128].rearrange("(i p) tc -> p i tc", p=128),
                     wait=[("v", m["fin_free"])])
        c_l2 = p.dma("sync", m["xa_ch"][:, 0:nd * 128].rearrange("p (i tc) -> p i tc", i=nd),
                     m["xa_s"][:, tt * 128:(tt + 1) * 128].rearrange("(i p) tc -> p i tc", p=128))
        lnin = m["lnin"]
        for dd in range(nd):
            c_tr = ptrans(p, m["pb"][4 + dd % 2][:, 0:128], m["xn_e"][:, dd * 128:(dd + 1) * 128],
                          m["identb"], wait=[("qs", c_l1), ("v", m["trn_free"][dd % 2])])
            m["trn_free"][dd % 2] = p.v(lambda e, dd=dd: e.tensor_tensor(
                lnin[:, dd * 128:(dd + 1) * 128], m["pb"][4 + dd % 2][:, 0:128],
                m["Y_acc"][:, tt * D + dd * 128:(tt) * D + (dd + 1) * 128], ALU.add),
                wait=[("t", c_tr)])
        c_ln = p.n["v"]
        gc = m["gn"][:, 96:128]
        c_s1 = p.v(lambda e: e.reduce_sum(gc[:, 0:1], lnin[:, 0:D], AX.X),
                   wait=[("v", c_ln, True)])
        c_sq = p.s(lambda e: e.activation(m["sqf"][:, 0:D], lnin[:, 0:D], AF.Square),
                   wait=[("v", c_ln)])
        c_s2 = p.v(lambda e: e.reduce_sum(gc[:, 1:2], m["sqf"][:, 0:D], AX.X),
                   wait=[("s", c_sq)])
        p.v(lambda e: e.tensor_scalar(gc[:, 2:3], gc[:, 0:1], -1.0 / D, None, ALU.mult),
            wait=[("v", c_s2, True)])
        p.v(lambda e: e.tensor_scalar(gc[:, 3:4], gc[:, 1:2], 1.0 / D, None, ALU.mult))
        c_mm2 = p.v(lambda e: e.tensor_tensor(gc[:, 4:5], gc[:, 2:3], gc[:, 2:3], ALU.mult))
        p.v(lambda e: e.scalar_tensor_tensor(gc[:, 5:6], gc[:, 4:5], -1.0, gc[:, 3:4],
                                             ALU.mult, ALU.add), wait=[("v", c_mm2, True)])
        c_ve = p.v(lambda e: e.tensor_scalar(gc[:, 6:7], gc[:, 5:6], 1e-5, None, ALU.add),
                   wait=[("v", p.n["v"], True)])
        p.s(lambda e: e.activation(gc[:, 7:8], gc[:, 6:7], AF.Ln), wait=[("v", c_ve)])
        c_rs = p.s(lambda e: e.activation(gc[:, 8:9], gc[:, 7:8], AF.Exp, scale=-0.5))
        c_app = p.v(lambda e: e.tensor_scalar(lnin[:, 0:D], lnin[:, 0:D], gc[:, 2:3], gc[:, 8:9],
                                              ALU.add, ALU.mult), wait=[("s", c_rs)])
        p.v(lambda e: e.tensor_tensor(lnin[:, 0:D], lnin[:, 0:D], m["gamma_rep"][:, 0:D], ALU.mult))
        c_gb = p.v(lambda e: e.tensor_tensor(lnin[:, 0:D], lnin[:, 0:D], m["beta_rep"][:, 0:D], ALU.add))
        for dd in range(nd):
            c_tr = ptrans(p, m["pb"][4 + dd % 2][:, 0:128], m["xa_ch"][:, dd * 128:(dd + 1) * 128],
                          m["ident"], wait=[("qs", c_l2), ("v", m["trn_free"][dd % 2]), ("v", c_gb)])
            m["trn_free"][dd % 2] = p.v(lambda e, dd=dd: e.tensor_tensor(
                m["out_ch"][:, dd * 128:(dd + 1) * 128], m["pb"][4 + dd % 2][:, 0:128],
                lnin[:, dd * 128:(dd + 1) * 128], ALU.add),
                wait=[("t", c_tr), ("qa", m["out_paged"])])
        m["fin_free"] = p.n["v"]
        m["out_paged"] = p.dma("scalar", m["out_s"][tt * 128:(tt + 1) * 128, :],
                               m["out_ch"][:, 0:D], wait=[("v", m["fin_free"])])
        snap = p.barrier()
    p.op("sync", lambda e: e.nop(), wait=[("qa", p.n["qa"]), ("qs", p.n["qs"]), ("qg", p.n["qg"])])


def s5_preamble(p, m):
    nd = D // 128
    p.dma("gpsimd", m["Wg"][:, 0:nd * NE1].rearrange("p (i f) -> p i f", i=nd),
          m["Wg_d"].rearrange("(i p) f -> p i f", p=128))
    p.dma("gpsimd", m["identb"][:], m["ident_d"])
    p.dma("sync", m["ident"][:], m["ident_d"])
    p.dma("sync", m["bg"][0:NE1, 0:1], m["bg_d"])
    p.dma("sync", m["b2all"][0:NE1, :], m["b2all_d"])
    p.dma("sync", m["b1c"][:], m["b1c_d"])
    p.dma("sync", m["gamma_rep"][:], m["gamma_d"])
    p.dma("sync", m["beta_rep"][:], m["beta_d"])
    for k in ["lg_free", "lg_copy", "ct_free", "xn_used", "fin_free", "out_paged", "w1_used", "w2_used"]:
        m[k] = 0
    m["t2_free"] = [0, 0]
    m["h_free_s"] = [0, 0]
    m["y_free"] = [0, 0]
    m["trn_free"] = [0, 0]
    m["hg_used"] = {}


def build_s5():
    nc = _new_nc()
    nd = D // 128
    NTT = T // 128
    m = {}
    m["xn1T_s"] = nc.dram_tensor("xn1T_s", [D, T], BF16, kind="ExternalInput").ap()
    m["xa_s"] = nc.dram_tensor("xa_s", [D, T], F32, kind="ExternalInput").ap()
    m["Wg_d"] = nc.dram_tensor("Wg", [D, NE1], F32, kind="ExternalInput").ap()
    m["bg_d"] = nc.dram_tensor("bg", [NE1, 1], F32, kind="ExternalInput").ap()
    m["W1_d"] = [nc.dram_tensor(f"W1_{e}", [D, DFF1], F32, kind="ExternalInput").ap() for e in range(NE1)]
    m["W2_d"] = [nc.dram_tensor(f"W2_{e}", [DFF1, D], F32, kind="ExternalInput").ap() for e in range(NE1)]
    m["b1c_d"] = nc.dram_tensor("b1c", [128, NE1 * 32], F32, kind="ExternalInput").ap()
    m["b2all_d"] = nc.dram_tensor("b2all", [NE1, D], F32, kind="ExternalInput").ap()
    m["gamma_d"] = nc.dram_tensor("gammar", [128, D], F32, kind="ExternalInput").ap()
    m["beta_d"] = nc.dram_tensor("betar", [128, D], F32, kind="ExternalInput").ap()
    m["ident_d"] = nc.dram_tensor("ident", [128, 128], F32, kind="ExternalInput").ap()
    m["out_s"] = nc.dram_tensor("out_s", [T, D], F32, kind="ExternalOutput").ap()

    es = contextlib.ExitStack()
    with es:
        sb = lambda name, shape, dt: es.enter_context(nc.sbuf_tensor(name, shape, dt))
        m["Wg"] = sb("Wg_sb", [128, nd * NE1], BF16)
        m["identb"] = sb("identb", [128, 128], BF16)
        m["ident"] = sb("ident_sb", [128, 128], F32)
        m["bg"] = sb("bg_sb", [NE1, 1], F32)
        m["b2all"] = sb("b2all_sb", [NE1, D], F32)
        m["b1c"] = sb("b1c_sb", [128, NE1 * 32], F32)
        m["gamma_rep"] = sb("gamma_rep", [128, D], F32)
        m["beta_rep"] = sb("beta_rep", [128, D], F32)
        m["logitsT"] = sb("logitsT", [NE1, T], F32)
        m["gb"] = sb("gb", [128, 5 * NTT * NE1], F32)
        m["gn"] = sb("gn", [128, 128], F32)
        m["comb"] = sb("comb", [128, NTT * NE1], F32)
        m["combT"] = sb("combT", [NE1, T], F32)
        m["Y_acc"] = sb("Y_acc", [128, NTT * D], F32)
        m["xn_e"] = sb("xn_e", [128, nd * 512], BF16)
        m["wp1"] = sb("wp1", [128, nd * 2048], BF16)
        m["wp2"] = sb("wp2", [128, 16 * D], BF16)
        m["H1"] = sb("H1", [128, 16 * 256], BF16)
        m["xa_ch"] = sb("xa_ch", [128, nd * 128], F32)
        m["lnin"] = sb("lnin", [128, D], F32)
        m["sqf"] = sb("sqf", [128, D], F32)
        m["out_ch"] = sb("out_ch", [128, D], F32)
        m["pb"] = [es.enter_context(nc.psum_tensor(f"pb{i}", [128, 512], F32)) for i in range(6)]

        sem_names = ["t", "v", "s", "g", "qg", "qs", "qa", "y"]
        sems = {k: es.enter_context(nc.semaphore(k)) for k in sem_names}

        def describe(p):
            s5_preamble(p, m)
            s5_body(p, m)

        with nc.Block() as block:
            for en in ENGINES:
                def mk(en):
                    def section(eng):
                        describe(Prog(nc, sems, active=en, eng=eng))
                    return section
                getattr(block, en)(mk(en))
    return nc


def s5_inmap(inputs, s4_out):
    b1 = np.asarray(inputs["L1_ffn_b1"], np.float32)
    im = dict(
        xn1T_s=s4_out["xn1T_s"],
        xa_s=s4_out["xa_s"],
        Wg=np.ascontiguousarray(np.asarray(inputs["L1_ffn_Wg"], np.float32)),
        bg=np.asarray(inputs["L1_ffn_bg"], np.float32).reshape(NE1, 1),
        b1c=np.ascontiguousarray(b1.reshape(NE1, 32, 128).transpose(2, 0, 1).reshape(128, NE1 * 32)),
        b2all=np.ascontiguousarray(np.asarray(inputs["L1_ffn_b2"], np.float32)),
        gammar=np.ascontiguousarray(np.broadcast_to(
            np.asarray(inputs["L1_ffn_gamma"], np.float32)[None, :], (128, D))),
        betar=np.ascontiguousarray(np.broadcast_to(
            np.asarray(inputs["L1_ffn_beta"], np.float32)[None, :], (128, D))),
        ident=np.eye(128, dtype=np.float32),
    )
    for e in range(NE1):
        im[f"W1_{e}"] = np.ascontiguousarray(np.asarray(inputs["L1_ffn_W1"], np.float32)[e])
        im[f"W2_{e}"] = np.ascontiguousarray(np.asarray(inputs["L1_ffn_W2"], np.float32)[e])
    return im


def kernel_multi(**inputs):
    n = 8
    x = np.asarray(inputs["L0_attn_x"], np.float32)
    xTs = [np.ascontiguousarray(x[b].T) for b in range(n)]
    cores = list(range(n))

    im1 = [s1_inmap(inputs, b) for b in range(n)]
    r1 = run_bass_kernel_spmd(build_s1(), im1, core_ids=cores).results
    r1 = [{k: np.asarray(v) for k, v in r.items()} for r in r1]

    im2 = [s2_inmap(inputs, r1[b]) for b in range(n)]
    r2 = run_bass_kernel_spmd(build_s2(), im2, core_ids=cores).results
    r2 = [{k: np.asarray(v) for k, v in r.items()} for r in r2]

    im3 = [s3_inmap(inputs, xTs[b], r1[b], r2[b]) for b in range(n)]
    r3 = run_bass_kernel_spmd(build_s3(), im3, core_ids=cores).results
    r3 = [{k: np.asarray(v) for k, v in r.items()} for r in r3]

    im4 = [s4_inmap(inputs, r3[b]["x1_s"]) for b in range(n)]
    r4 = run_bass_kernel_spmd(build_s4(), im4, core_ids=cores).results
    r4 = [{k: np.asarray(v) for k, v in r.items()} for r in r4]

    im5 = [s5_inmap(inputs, r4[b]) for b in range(n)]
    r5 = run_bass_kernel_spmd(build_s5(), im5, core_ids=cores).results

    return np.stack([np.asarray(r5[b]["out_s"]) for b in range(n)], axis=0)


FUSED_DEBUG = False


class PoolAlloc:
    def __init__(self, fpool, bpool, fcap, bcap):
        self.fp, self.bp = fpool, bpool
        self.fcap, self.bcap = fcap, bcap
        self.reset()

    def reset(self):
        self.f = 0
        self.b = 0

    def t(self, name, shape, dt):
        w = shape[-1] if len(shape) == 2 else shape[0]
        p_ = shape[0] if len(shape) == 2 else 1
        w = (w + 15) // 16 * 16
        if dt == F32:
            assert self.f + w <= self.fcap, f"f32 pool overflow at {name}: {self.f + w}"
            ap = self.fp[0:p_, self.f:self.f + shape[-1]]
            self.f += w
        else:
            assert self.b + w <= self.bcap, f"b16 pool overflow at {name}: {self.b + w}"
            ap = self.bp[0:p_, self.b:self.b + shape[-1]]
            self.b += w
        return ap


def build_fused():
    nc = _new_nc()
    nd = D // 128
    ndi = DI // 128
    NE = 16
    NTT = T // 128
    dram = lambda name, shape, dt, kind: nc.dram_tensor(name, shape, dt, kind=kind).ap()
    EI, IN, EO = "ExternalInput", "Internal", "ExternalOutput"

    g = {}
    g["x"] = dram("x", [D, T], F32, EI)
    g["w_in_d"] = dram("w_in", [D, 4096], F32, EI)
    g["lng_d"] = dram("lng", [128, nd], F32, EI)
    g["lnb_d"] = dram("lnb", [128, nd], F32, EI)
    g["convw_d"] = dram("convw", [128, 4 * ndi], F32, EI)
    g["convb_d"] = dram("convb", [128, ndi], F32, EI)
    g["w_dt_d"] = dram("w_dt", [DI, DI], F32, EI)
    g["w_xp_d"] = dram("w_xp", [DI, 32], F32, EI)
    g["bdt_d"] = dram("bdt", [128, ndi], F32, EI)
    g["Alog_d"] = dram("Alog", [128, ndi * DS], F32, EI)
    g["D_d"] = dram("Dv", [128, ndi], F32, EI)
    g["sel32_d"] = dram("sel32", [32, 32 * 128], F32, EI)
    g["w_out_d"] = dram("w_out", [DI, D], F32, EI)
    g["gate_wT_d"] = dram("gate_wT", [D, NE], F32, EI)
    g["ebias_d"] = dram("ebias", [128, NE], F32, EI)
    g["ident_d"] = dram("ident", [128, 128], F32, EI)
    g["b2aug_d"] = dram("b2aug", [17, D], F32, EI)
    g["b1_all_d"] = dram("b1_all", [128, NE * 4], F32, EI)
    g["b3_all_d"] = dram("b3_all", [128, NE * 4], F32, EI)
    g["sb1_d"] = dram("sb1", [128, 8], F32, EI)
    g["sb3_d"] = dram("sb3", [128, 8], F32, EI)
    g["w1T_d"] = [dram(f"w1T_{e}", [D, DFF0], F32, EI) for e in range(NE)]
    g["w3T_d"] = [dram(f"w3T_{e}", [D, DFF0], F32, EI) for e in range(NE)]
    g["w2T_d"] = [dram(f"w2T_{e}", [DFF0, D], F32, EI) for e in range(NE)]
    g["sw1T_d"] = dram("sw1T", [D, FDFF0], F32, EI)
    g["sw3T_d"] = dram("sw3T", [D, FDFF0], F32, EI)
    g["sw2T_d"] = dram("sw2T", [FDFF0, D], F32, EI)
    g["w_proj_d"] = dram("w_proj", [D, D], F32, EI)
    g["wqkv_hd"] = [dram(f"wqkv_{h}", [D, 3 * HD], F32, EI) for h in range(NH)]
    g["newc_d"] = dram("newc", [NLM, 3 * NLM], F32, EI)
    g["Wg_d"] = dram("Wg", [D, NE1], F32, EI)
    g["bg_d"] = dram("bg", [NE1, 1], F32, EI)
    g["W1_d"] = [dram(f"W1_{e}", [D, DFF1], F32, EI) for e in range(NE1)]
    g["W2_d"] = [dram(f"W2_{e}", [DFF1, D], F32, EI) for e in range(NE1)]
    g["b1c_d"] = dram("b1c", [128, NE1 * 32], F32, EI)
    g["b2all_d"] = dram("b2all", [NE1, D], F32, EI)
    g["gamma_d"] = dram("gammar", [128, D], F32, EI)
    g["beta_d"] = dram("betar", [128, D], F32, EI)
    HK = EO if FUSED_DEBUG else IN
    g["xn_s"] = dram("xn_s", [D, T], BF16, HK)
    g["xc_s"] = dram("xc_s", [DI, T], BF16, HK)
    g["z_s"] = dram("z_s", [DI, T], BF16, HK)
    g["yg_s"] = dram("yg_s", [DI, T], BF16, HK)
    g["x1p_s"] = dram("x1p_s", [D, T], F32, HK)
    g["x1_s"] = dram("x1_s", [D, T], F32, HK)
    g["xn1T_s"] = dram("xn1T_s", [D, T], BF16, HK)
    g["xa_s"] = dram("xa_s", [D, T], F32, HK)
    g["out_s"] = dram("out_s", [T, D], F32, EO)

    FCAP = 29400
    BCAP = 46800
    es = contextlib.ExitStack()
    with es:
        sbt = lambda name, shape, dt: es.enter_context(nc.sbuf_tensor(name, shape, dt))
        fpool = sbt("fpool", [128, FCAP], F32)
        bpool = sbt("bpool", [128, BCAP], BF16)
        g["ident"] = sbt("ident_sb", [128, 128], F32)
        g["identb"] = sbt("identb_sb", [128, 128], BF16)
        g["onescol"] = sbt("onescol", [128, 1], F32)
        g["onescol_b"] = sbt("onescol_b", [128, 1], BF16)
        g["onesrow"] = sbt("onesrow", [1, 128], F32)
        pb = [es.enter_context(nc.psum_tensor(f"pb{i}", [128, 512], F32)) for i in range(7)]
        al = PoolAlloc(fpool, bpool, FCAP, BCAP)

        def mk_m1():
            al.reset()
            m = dict(g)
            TC1 = 128
            m["TC1"] = TC1
            m["w_in"] = al.t("w_in", [128, nd * 4096], BF16)
            m["x_ch2"] = [al.t(f"x_ch{i}", [128, nd * TC1], F32) for i in range(2)]
            m["xn_ch2"] = [al.t(f"xn_ch{i}", [128, nd * TC1], BF16) for i in range(2)]
            m["xln_ch"] = al.t("xln_ch", [128, nd * TC1], BF16)
            m["sq"] = al.t("sq", [128, nd * TC1], F32)
            m["braw_ch"] = [al.t(f"braw{i}", [128, ndi * (TC1 + 3)], BF16) for i in range(2)]
            m["z_ch"] = al.t("z_ch", [128, ndi * TC1], BF16)
            m["xc_ch"] = al.t("xc_ch", [128, ndi * TC1], BF16)
            m["cv"] = al.t("cv", [128, 2 * TC1], F32)
            m["row0"] = al.t("row0", [1, TC1], F32)
            m["row1"] = al.t("row1", [1, TC1], F32)
            m["row2"] = al.t("row2", [1, TC1], F32)
            m["mu_rep"] = al.t("mu_rep", [128, 2 * TC1], F32)
            m["lng"] = al.t("lng_sb", [128, nd], F32)
            m["lnb"] = al.t("lnb_sb", [128, nd], F32)
            m["convw"] = al.t("convw_sb", [128, 4 * ndi], F32)
            m["convb"] = al.t("convb_sb", [128, ndi], F32)
            m["ps_r0"], m["ps_r1"], m["ps_rep"], m["ps_rep2"] = pb[2], pb[3], pb[4], pb[5]
            m["ps_mm"] = [pb[0], pb[1]]
            m["s1_ps_free"] = [0, 0]
            return m

        def mk_m2():
            al.reset()
            m = dict(g)
            m["w_dt"] = al.t("w_dt", [128, ndi * DI], BF16)
            m["w_xp"] = al.t("w_xp", [128, ndi * 32], BF16)
            m["bdt"] = al.t("bdt", [128, ndi], F32)
            m["Aneg"] = al.t("Aneg", [128, ndi * DS], F32)
            m["Dcol"] = al.t("Dcol", [128, ndi], F32)
            m["sel32"] = al.t("sel32", [32, 32 * 128], F32)
            m["xc_ch2"] = [al.t(f"xc_ch{i}", [128, ndi * 256], BF16) for i in range(2)]
            m["dt_ch2"] = [al.t(f"dt_ch{i}", [128, ndi * 256], F32) for i in range(2)]
            a1 = al.t("a_ch", [128, DS * 256], F32)
            m["a_ch2"] = [a1, a1]
            m["h_g"] = al.t("h_g", [128, DS * 256], F32)
            m["u_g"] = al.t("u_g", [128, 256], F32)
            m["y_g"] = al.t("y_g", [128, 768], F32)

            m["scr"] = al.t("scr", [128, 768], F32)
            m["B_rep"] = al.t("B_rep", [128, DS * 256], F32)
            m["C_rep"] = al.t("C_rep", [128, DS * 256], BF16)
            m["xdbl"] = al.t("xdbl", [32, 256], F32)
            m["h_state"] = al.t("h_state", [128, ndi * DS], F32)
            m["yg_ch"] = al.t("yg_ch", [128, 2 * 256], BF16)
            m["pb"] = pb[0:5]
            return m

        def mk_m3():
            al.reset()
            m = dict(g)
            pool_a = al.t("pool_a", [128, ndi * D], BF16)
            m["w_out"] = pool_a[:, 0:ndi * D]
            m["wp1"] = pool_a[:, 0:nd * FDFF0]
            m["wp3"] = pool_a[:, nd * FDFF0:2 * nd * FDFF0]
            m["gate_wT"] = al.t("gate_wT", [128, nd * NE], BF16)
            m["ebias_rep"] = al.t("ebias_rep", [128, NE], F32)
            m["b2aug"] = al.t("b2aug", [17, D], F32)
            m["b1_all"] = al.t("b1_all", [128, NE * 4], F32)
            m["b3_all"] = al.t("b3_all", [128, NE * 4], F32)
            m["sb1"] = al.t("sb1", [128, 8], F32)
            m["sb3"] = al.t("sb3", [128, 8], F32)
            m["x_ch"] = al.t("x_ch", [128, nd * 256], F32)
            m["xn_ch"] = al.t("xn_ch", [128, nd * 256], BF16)
            pool_b = al.t("pool_b", [128, ndi * 256], BF16)
            m["yg_ch"] = pool_b[:, 0:ndi * 256]
            m["xn_e"] = pool_b[:, 0:nd * 512]
            m["x1_ch"] = al.t("x1_ch", [128, nd * 256], F32)
            m["sz_g"] = al.t("sz_g3", [128, 512], F32)
            m["scr"] = al.t("scr", [128, 512], F32)
            m["logitsT"] = al.t("logitsT", [16, T], F32)
            m["gatew"] = al.t("gatew", [128, NTT * NE], F32)
            m["gatewT"] = al.t("gatewT", [17, T], F32)
            m["gb"] = al.t("gb", [128, 7 * NTT * NE], F32)
            m["gn"] = al.t("gn", [128, 160], F32)
            m["Y_acc"] = al.t("Y_acc", [128, NTT * D], F32)
            m["wp2"] = al.t("wp2", [128, (FDFF0 // 128) * D], BF16)
            m["H1"] = al.t("H1", [128, 8 * 512], BF16)
            m["H3"] = al.t("H3", [128, 8 * 512], BF16)
            m["Hg"] = al.t("Hg", [128, 8 * 512], BF16)
            m["z_ch"] = m["H1"]
            m["pb"] = pb
            return m

        def mk_m4():
            al.reset()
            m = dict(g)
            m["defer_wproj"] = True
            lane_qk = []
            m["lane"] = []
            for ln in range(2):
                B = {}
                B["wqkv_h"] = al.t(f"wqkv_h{ln}", [128, nd * 3 * HD], BF16)
                B["QT"] = al.t(f"QT{ln}", [NLM, T], BF16)
                B["KT"] = al.t(f"KT{ln}", [NLM, T], BF16)
                B["Vn"] = al.t(f"Vn{ln}", [128, NTT * HD], BF16)
                B["lmf"] = al.t(f"lmf{ln}", [NLM, 2 * NLM], F32)
                B["klm"] = al.t(f"klm{ln}", [NLM, NLM], BF16)
                B["qlm"] = al.t(f"qlm{ln}", [NLM, NLM], BF16)
                for nm in ["K2", "K2n", "K2T", "NV0", "NV1", "KV", "KVT", "NM", "NM2", "NVT"]:
                    B[nm] = al.t(nm + str(ln), [NLM, NLM], F32)
                B["nsc"] = al.t(f"nsc{ln}", [NLM, 16], F32)
                B["nrow"] = al.t(f"nrow{ln}", [1, NLM + 4], F32)
                B["K3"] = al.t(f"K3_{ln}", [NLM, T], F32)
                kb = al.t(f"k3k1_{ln}", [NLM, T], BF16)
                B["K3b"] = kb[0:NLM, 0:T]
                B["K1T"] = kb[0:NLM, 0:T]
                lane_qk.append(kb)
                B["K3T"] = al.t(f"K3T{ln}", [128, NTT * NLM], BF16)
                B["K3V"] = al.t(f"K3V{ln}", [NLM, HD], F32)
                B["Wh"] = al.t(f"Wh{ln}", [NLM, HD], BF16)
                B["K1t"] = al.t(f"K1t{ln}", [128, 2 * NLM], F32)
                B["K1n"] = al.t(f"K1n{ln}", [128, 2 * NLM], BF16)
                B["k1sc"] = al.t(f"k1sc{ln}", [128, 16], F32)
                m["lane"].append(B)
            wp = al.t("wp_region", [128, 0], BF16)
            m["w_proj"] = bpool[:, 0:nd * D]
            m["I7"] = al.t("I7", [NLM, NLM], F32)
            m["I15"] = al.t("I15", [NLM, NLM], F32)
            m["I13"] = al.t("I13", [NLM, NLM], F32)
            m["SVT"] = al.t("SVT", [128, nd * T], BF16)
            m["xn_st"] = [al.t(f"xn_st{i}", [128, nd * 512], BF16) for i in range(2)]
            m["xnl_ch"] = al.t("xnl_ch", [128, nd * 256], BF16)
            m["x_ch"] = al.t("x_ch", [128, nd * 256], F32)
            m["xa_ch"] = al.t("xa_ch", [128, nd * 256], F32)
            m["sq"] = al.t("sq", [128, nd * 256], F32)
            m["row0"] = al.t("row0", [1, 256], F32)
            m["row1"] = al.t("row1", [1, 256], F32)
            m["row2"] = al.t("row2", [1, 256], F32)
            m["mu_rep"] = al.t("mu_rep", [128, 512], F32)
            m["pb"] = pb
            m["ps_r0"], m["ps_r1"], m["ps_rep"], m["ps_rep2"] = pb[3], pb[4], pb[5], pb[6]
            return m

        def mk_m5():
            al.reset()
            m = dict(g)
            m["Wg"] = al.t("Wg", [128, nd * NE1], BF16)
            m["bg"] = al.t("bg", [NE1, 1], F32)
            m["b2all"] = al.t("b2all", [NE1, D], F32)
            m["b1c"] = al.t("b1c", [128, NE1 * 32], F32)
            m["gamma_rep"] = al.t("gamma_rep", [128, D], F32)
            m["beta_rep"] = al.t("beta_rep", [128, D], F32)
            m["logitsT"] = al.t("logitsT", [NE1, T], F32)
            m["gb"] = al.t("gb", [128, 5 * NTT * NE1], F32)
            m["gn"] = al.t("gn", [128, 128], F32)
            m["comb"] = al.t("comb", [128, NTT * NE1], F32)
            m["combT"] = al.t("combT", [NE1, T], F32)
            m["Y_acc"] = al.t("Y_acc", [128, NTT * D], F32)
            m["xn_e"] = al.t("xn_e", [128, nd * 512], BF16)
            m["wp1"] = al.t("wp1", [128, nd * 2048], BF16)
            m["wp2"] = al.t("wp2", [128, 16 * D], BF16)
            m["H1"] = al.t("H1", [128, 16 * 256], BF16)
            m["xa_ch"] = al.t("xa_ch", [128, nd * 128], F32)
            m["lnin"] = al.t("lnin", [128, D], F32)
            m["sqf"] = al.t("sqf", [128, D], F32)
            m["out_ch"] = al.t("out_ch", [128, D], F32)
            m["pb"] = pb[0:6]
            return m

        sem_names = ["t", "v", "s", "g", "qg", "qs", "qa", "y"]
        sems = {k: es.enter_context(nc.semaphore(k)) for k in sem_names}

        def describe(p):
            p.dma("sync", g["ident"][:], g["ident_d"])
            p.dma("gpsimd", g["identb"][:], g["ident_d"])
            p.v(lambda e: e.memset(g["onescol"][:], 1.0))
            p.v(lambda e: e.memset(g["onescol_b"][:], 1.0))
            p.v(lambda e: e.memset(g["onesrow"][:], 1.0))
            def stage_gap():
                sg = p.barrier()
                p.wait_snapshot(sg)
            m1 = mk_m1()
            s1_preamble(p, m1)
            s1_body(p, m1)
            stage_gap()
            m2 = mk_m2()
            s2_preamble(p, m2)
            s2_body(p, m2)
            stage_gap()
            m3 = mk_m3()
            s3_preamble(p, m3)
            s3_body(p, m3)
            stage_gap()
            m4 = mk_m4()
            s4_preamble(p, m4)
            s4_body(p, m4)
            stage_gap()
            m5 = mk_m5()
            s5_preamble(p, m5)
            s5_body(p, m5)

        with nc.Block() as block:
            for en in ENGINES:
                def mk(en):
                    def section(eng):
                        describe(Prog(nc, sems, active=en, eng=eng))
                    return section
                getattr(block, en)(mk(en))
    return nc


def fused_inmap(inputs, b, xT):
    im = {}
    im.update(s1_inmap(inputs, b))
    im.update(s2_inmap(inputs, {"xc_s": None, "z_s": None}))
    im.update(s3_inmap(inputs, xT, {"xn_s": None, "z_s": None}, {"yg_s": None}))
    im.update(s4_inmap(inputs, None))
    im.update(s5_inmap(inputs, {"xn1T_s": None, "xa_s": None}))
    for k in ["xc_s", "z_s", "xn_s", "yg_s", "x1_s", "xn1T_s", "xa_s"]:
        im.pop(k, None)
    im["x"] = xT
    return im


def kernel_fused(**inputs):
    n = 8
    x = np.asarray(inputs["L0_attn_x"], np.float32)
    cores = list(range(n))
    nc = build_fused()
    im = [fused_inmap(inputs, b, np.ascontiguousarray(x[b].T)) for b in range(n)]
    res = run_bass_kernel_spmd(nc, im, core_ids=cores)
    return np.stack([np.asarray(res.results[b]["out_s"]) for b in range(n)], axis=0)


def kernel(**inputs):
    return kernel_fused(**inputs)
